# version 23
# baseline (speedup 1.0000x reference)
import numpy as np
from contextlib import ExitStack
import concourse.bass as bass
import concourse.mybir as mybir
from concourse.bass_utils import run_bass_kernel_spmd

F32 = mybir.dt.float32
BF16 = mybir.dt.bfloat16
I32 = mybir.dt.int32
AF = mybir.ActivationFunctionType
ALU = mybir.AluOpType
AX = mybir.AxisListType

NSEQ, SL, D, NH, DK = 2, 2048, 1024, 16, 64
T = NSEQ * SL
NDC = D // 128
NTT = SL // 128
NQB = SL // 512
NKC = SL // 128
E = 8
HID = 4096
NHC = HID // 128
NG = 8
TCH = 512
NTC = T // TCH
NSUB = TCH // 128


class Sem:
    def __init__(self, stack, nc, name):
        self.h = stack.enter_context(nc.semaphore(name))
        self.n = 0
    def inc(self, instr, k=1):
        instr.then_inc(self.h, k)
        self.n += k
        return self.n


def build():
    nc = bass.Bass("TRN2", target_bir_lowering=False, debug=False)
    x_in = nc.dram_tensor("x_in", [T, D], F32, kind="ExternalInput").ap()
    wqt = nc.dram_tensor("wqt", [D, D], F32, kind="ExternalInput").ap()
    wkt = nc.dram_tensor("wkt", [D, D], F32, kind="ExternalInput").ap()
    wvt = nc.dram_tensor("wvt", [D, D], F32, kind="ExternalInput").ap()
    wot = nc.dram_tensor("wot", [D, D], F32, kind="ExternalInput").ap()
    wgr = nc.dram_tensor("wgr", [D, 16], F32, kind="ExternalInput").ap()
    wg = nc.dram_tensor("wg", [8, 16], F32, kind="ExternalInput").ap()
    w1 = nc.dram_tensor("w1", [E, D, HID], F32, kind="ExternalInput").ap()
    b1r = nc.dram_tensor("b1r", [E, 128, 32], F32, kind="ExternalInput").ap()
    w2 = nc.dram_tensor("w2", [E, HID, D], F32, kind="ExternalInput").ap()
    b2f = nc.dram_tensor("b2f", [1, E * D], F32, kind="ExternalInput").ap()
    out = nc.dram_tensor("out", [T, D], F32, kind="ExternalOutput").ap()
    x2s = nc.dram_tensor("x2s", [T, D], F32, kind="Internal").ap()
    xl2s = nc.dram_tensor("xl2s", [NDC, 128, T], BF16, kind="Internal").ap()
    wab = nc.dram_tensor("wab", [4, 128, NDC * 1024], BF16, kind="Internal").ap()
    w1b = nc.dram_tensor("w1b", [E, NG, 128, 4096], BF16, kind="Internal").ap()
    w2b = nc.dram_tensor("w2b", [E, NHC, 128, 1024], BF16, kind="Internal").ap()

    with ExitStack() as st:
        S = st.enter_context
        ident = S(nc.sbuf_tensor("ident", [128, 128], BF16))
        onesb = S(nc.sbuf_tensor("onesb", [1, 128], BF16))
        eps_col = S(nc.sbuf_tensor("eps_col", [128, 1], F32))
        neg12_col = S(nc.sbuf_tensor("neg12_col", [128, 1], F32))
        wgr_sb = S(nc.sbuf_tensor("wgr_sb", [128, NDC * 16], BF16))
        wg_sb = S(nc.sbuf_tensor("wg_sb", [8, 16], F32))
        wgu_bf = S(nc.sbuf_tensor("wgu_bf", [8, 16], BF16))
        wguT = S(nc.sbuf_tensor("wguT", [16, 8], BF16))
        xrT = S(nc.sbuf_tensor("xrT", [16, T], BF16))
        oh_sb = S(nc.sbuf_tensor("oh_sb", [128, 32 * E], F32))
        sc_sb = S(nc.sbuf_tensor("sc_sb", [128, 32], F32))
        sq = S(nc.sbuf_tensor("sq", [8, 16], F32))
        nrm = S(nc.sbuf_tensor("nrm", [8, 1], F32))
        nrm2 = S(nc.sbuf_tensor("nrm2", [8, 1], F32))
        rn = S(nc.sbuf_tensor("rn", [8, 1], F32))

        s_ld = Sem(st, nc, "s_ld")
        s_x = [Sem(st, nc, f"s_x{i}") for i in range(4)]
        s_w = [Sem(st, nc, f"s_w{i}") for i in range(2)]
        s_w1 = [Sem(st, nc, f"s_w1{i}") for i in range(3)]
        s_w2 = [Sem(st, nc, f"s_w2{i}") for i in range(3)]
        s_xb = [Sem(st, nc, f"s_xb{i}") for i in range(2)]
        s_x2b = Sem(st, nc, "s_x2b")
        s_ppe = [Sem(st, nc, f"s_ppe{i}") for i in range(E)]
        s_ppwa = Sem(st, nc, "s_ppwa")
        s_xl2 = [Sem(st, nc, f"s_xl2{i}") for i in range(4)]
        s_x2w = [Sem(st, nc, f"s_x2w{i}") for i in range(4)]
        s_pp = Sem(st, nc, "s_pp")
        s_pps = Sem(st, nc, "s_pps")
        s_ldw = Sem(st, nc, "s_ldw")
        s_pe = Sem(st, nc, "s_pe")
        s_act = Sem(st, nc, "s_act")
        s_dve = Sem(st, nc, "s_dve")
        s_st = Sem(st, nc, "s_st")
        s_gp = Sem(st, nc, "s_gp")

        wait_hwm = {}
        def ewait(eng, sem, val):
            if val <= 0:
                return
            hw = wait_hwm.setdefault(id(eng), {})
            if hw.get(id(sem.h), 0) >= val:
                return
            eng.wait_ge(sem.h, val)
            hw[id(sem.h)] = val
        bank_release = {}
        def nbanks(h):
            return max(1, (int(h.ap().free_nbytes()) + 2047) // 2048)
        def acquire_banks(h):
            m = nc.lookup_mloc(h)
            for b in range(m.bank, m.bank + nbanks(h)):
                r = bank_release.get(b)
                if r is not None:
                    ewait(nc.tensor, r[0], r[1])
        def release_banks(h, sem, val):
            m = nc.lookup_mloc(h)
            for b in range(m.bank, m.bank + nbanks(h)):
                bank_release[b] = (sem, val)
        def acquire_bank_cols(h, c0, c1, elt):
            m = nc.lookup_mloc(h)
            b0 = m.bank + (c0 * elt) // 2048
            b1 = m.bank + ((c1 * elt) - 1) // 2048
            for b in range(b0, b1 + 1):
                r = bank_release.get(b)
                if r is not None:
                    ewait(nc.tensor, r[0], r[1])
        def release_bank_cols(h, c0, c1, elt, sem, val):
            m = nc.lookup_mloc(h)
            b0 = m.bank + (c0 * elt) // 2048
            b1 = m.bank + ((c1 * elt) - 1) // 2048
            for b in range(b0, b1 + 1):
                bank_release[b] = (sem, val)

        with ExitStack() as ast_:
            A = ast_.enter_context
            bmask = A(nc.sbuf_tensor("bmask", [128, 896], BF16))
            xlnT = A(nc.sbuf_tensor("xlnT", [128, NDC * SL], BF16))
            qT = A(nc.sbuf_tensor("qT", [128, NDC * SL], BF16))
            kT = A(nc.sbuf_tensor("kT", [128, NDC * SL], BF16))
            vst = A(nc.sbuf_tensor("vst", [128, NKC * NH * 65], BF16))
            wsb = [A(nc.sbuf_tensor(f"wsb{i}", [128, NDC * 1024], BF16)) for i in range(2)]
            st6 = [A(nc.sbuf_tensor(f"st6{i}", [128, 12], F32)) for i in range(4)]
            mv = [A(nc.sbuf_tensor(f"mv{i}", [128, 2], F32)) for i in range(4)]
            stdv = [A(nc.sbuf_tensor(f"stdv{i}", [128, 1], F32)) for i in range(4)]
            negm = [A(nc.sbuf_tensor(f"negm{i}", [128, 1], F32)) for i in range(4)]
            rstd = [A(nc.sbuf_tensor(f"rstd{i}", [128, 1], F32)) for i in range(4)]
            x_tile_h = [A(nc.sbuf_tensor(f"x_tile{i}", [128, D], F32)) for i in range(2)]
            xln_t_h = [A(nc.sbuf_tensor(f"xln_t{i}", [128, D], BF16)) for i in range(2)]
            x_tile = [x_tile_h[0][:, :], x_tile_h[1][:, :],
                      vst[:, 2048:4096].bitcast(F32), vst[:, 4096:6144].bitcast(F32)]
            xln_t = [xln_t_h[0][:, :], xln_t_h[1][:, :],
                     vst[:, 0:1024], vst[:, 1024:2048]]
            x2_sb = [kT[:, i * 2048:(i + 1) * 2048].bitcast(F32) for i in range(4)]
            trbuf = [kT[:, 8192 + i * 1024: 8192 + (i + 1) * 1024] for i in range(4)]

            with nc.sbuf_tensor("iota32", [128, 448], I32) as iota32:
                g1 = s_gp.inc(nc.gpsimd.iota(iota32[:, :], [[1, 448]], base=-384, channel_multiplier=-1))
                ewait(nc.vector, s_gp, g1)
                nc.vector.tensor_scalar(bmask[:, 0:448], iota32[:, :], 0, None, op0=ALU.is_ge)
                d0 = s_dve.inc(nc.vector.tensor_scalar(ident[:, 0:64], iota32[:, 384:448], 0, None, op0=ALU.is_equal))
                ewait(nc.gpsimd, s_dve, d0)
                g2 = s_gp.inc(nc.gpsimd.iota(iota32[:, :], [[1, 448]], base=64, channel_multiplier=-1))
                ewait(nc.vector, s_gp, g2)
                nc.vector.tensor_scalar(bmask[:, 448:896], iota32[:, :], 0, None, op0=ALU.is_ge)
                d1 = s_dve.inc(nc.vector.tensor_scalar(ident[:, 64:128], iota32[:, 0:64], 0, None, op0=ALU.is_equal))
                nc.vector.memset(onesb[:, :], 1.0)
                nc.vector.memset(eps_col[:, :], 1e-5)
                nc.vector.memset(neg12_col[:, :], -12.0)
                vst4 = vst.ap().rearrange("p (kc h c) -> p kc h c", kc=NKC, h=NH)
                setup_dve = s_dve.inc(nc.vector.engine_nop())

                s_ld.inc(nc.sync.dma_start(wg_sb[:, :], wg), 16)
                s_pp.inc(nc.gpsimd.dma_start(wgr_sb.ap().rearrange("p (dc c) -> p dc c", dc=NDC),
                                             wgr.rearrange("(dc p) c -> p dc c", p=128)), 16)
                ewait(nc.scalar, s_ld, s_ld.n)
                a = s_act.inc(nc.scalar.activation(sq[:, :], wg_sb[:, :], AF.Square))
                ewait(nc.vector, s_act, a)
                v1 = s_dve.inc(nc.vector.reduce_sum(nrm[:, :], sq[:, :], axis=AX.X))
                ewait(nc.scalar, s_dve, v1)
                a = s_act.inc(nc.scalar.activation(nrm2[:, :], nrm[:, :], AF.Sqrt))
                ewait(nc.vector, s_act, a)
                v2 = s_dve.inc(nc.vector.tensor_scalar(nrm2[:, :], nrm2[:, :], 1e-4, None, op0=ALU.max))
                ewait(nc.vector, s_dve, v2)
                v3 = s_dve.inc(nc.vector.reciprocal(rn[:, :], nrm2[:, :]))
                ewait(nc.vector, s_dve, v3)
                v4 = s_dve.inc(nc.vector.tensor_scalar(wgu_bf[:, :], wg_sb[:, :], rn[:, :], None, op0=ALU.mult))
                with nc.psum_tensor("ps_wgu", [16, 8], BF16) as ps_wgu:
                    ewait(nc.tensor, s_dve, v4)
                    ewait(nc.tensor, s_dve, d1)
                    t1 = s_pe.inc(nc.tensor.transpose(ps_wgu[:, :], wgu_bf[:, :], ident[0:8, 0:8]))
                    ewait(nc.scalar, s_pe, t1)
                    wgu_ready = s_act.inc(nc.scalar.copy(wguT[:, :], ps_wgu[:, :]))

            pp_ready = {}
            def run_prepass():
                for widx, wsrc in enumerate((wqt, wkt, wvt, wot)):
                    src = wsrc.rearrange("(dc p) o -> p dc o", p=128)
                    dst = wab[widx].rearrange("p (dc o) -> p dc o", dc=NDC)
                    s_ppwa.inc(nc.gpsimd.dma_start(dst, src), 16)
                pp_ready['wa'] = s_ppwa.n
                for e_ in range(E):
                    for g in range(NG):
                        src = w1[e_][:, g * 512:(g + 1) * 512].rearrange("(dc p) h -> p dc h", p=128)
                        dst = w1b[e_, g].rearrange("p (dc h) -> p dc h", dc=NDC)
                        s_ppe[e_].inc(nc.gpsimd.dma_start(dst, src), 16)
                    for j in range(NG):
                        src = w2[e_][j * 512:(j + 1) * 512, :].rearrange("(hc p) d -> p hc d", p=128)
                        dst = w2b[e_, 4 * j:4 * j + 4].rearrange("hc p d -> p hc d")
                        s_ppe[e_].inc(nc.gpsimd.dma_start(dst, src), 16)
                    pp_ready[('e', e_)] = s_ppe[e_].n
            run_prepass()

            w_done = {}
            def wait_w(eng, idx):
                ewait(eng, s_w[idx % 2], w_done[idx])
            w_last_use = {}
            def load_w(idx, widx):
                slot = idx % 2
                if w_last_use.get(slot):
                    ewait(nc.sync, s_pe, w_last_use[slot])
                ewait(nc.sync, s_ppwa, pp_ready['wa'])
                w_done[idx] = s_w[slot].inc(nc.sync.dma_start(wsb[slot][:, :], wab[widx]), 16)

            x_done = {}
            x_free = [0] * 4
            def load_x(seq, tt, slot):
                row0 = seq * SL + tt * 128
                if x_free[slot]:
                    ewait(nc.sync, s_dve, x_free[slot])
                x_done[(seq, tt)] = (slot, s_x[slot].inc(nc.sync.dma_start(x_tile[slot], x_in[row0:row0 + 128, :]), 16))

            def ln_tile(slot, src_sb, dst_bf):
                nc.vector.bn_stats(st6[slot][:, 0:6], src_sb[:, 0:512])
                b1_ = s_dve.inc(nc.vector.bn_stats(st6[slot][:, 6:12], src_sb[:, 512:1024]))
                ewait(nc.vector, s_dve, b1_)
                b2_ = s_dve.inc(nc.vector.bn_aggr(mv[slot][:, :], st6[slot][:, :]))
                ewait(nc.scalar, s_dve, b2_)
                s_act.inc(nc.scalar.activation(stdv[slot][:, :], mv[slot][:, 1:2], AF.Sqrt, bias=eps_col[:, :]))
                a2 = s_act.inc(nc.scalar.mul(negm[slot][:, :], mv[slot][:, 0:1], -1.0))
                ewait(nc.vector, s_act, a2)
                r1 = s_dve.inc(nc.vector.reciprocal(rstd[slot][:, :], stdv[slot][:, :]))
                ewait(nc.vector, s_dve, r1)
                r2 = s_dve.inc(nc.vector.tensor_scalar(
                    dst_bf, src_sb, negm[slot][:, :], rstd[slot][:, :],
                    op0=ALU.add, op1=ALU.mult))
                return r2

            xln_rel = [0] * 4
            x2_rel = [0] * 4
            x2_reader = [0] * 4
            tr_free = [0] * 4
            tr_read = [0] * 4

            for seq in range(NSEQ):
                with nc.psum_tensor(f"ps_tr0_{seq}", [128, 1024], BF16) as tr0, \
                     nc.psum_tensor(f"ps_tr1_{seq}", [128, 1024], BF16) as tr1:
                    ps_tr = [tr0, tr1]
                    for i in range(3):
                        load_x(seq, i, i % 4)
                    for tt in range(NTT):
                        slot = tt % 4
                        xs, xv = x_done[(seq, tt)]
                        ewait(nc.vector, s_x[xs], xv)
                        if xln_rel[slot]:
                            ewait(nc.vector, s_pe, xln_rel[slot])
                        r2 = ln_tile(slot, x_tile[slot], xln_t[slot])
                        x_free[slot] = r2
                        if tt + 3 < NTT:
                            load_x(seq, tt + 3, (tt + 3) % 4)
                        ewait(nc.tensor, s_dve, r2)
                        if seq == 0 and tt == 0:
                            ewait(nc.tensor, s_dve, setup_dve)
                        pb = ps_tr[tt % 2]
                        acquire_banks(pb)
                        for i in range(8):
                            mm = nc.tensor.transpose(pb[:, i * 128:(i + 1) * 128], xln_t[slot][:, i * 128:(i + 1) * 128], ident[:, :])
                        xln_rel[slot] = s_pe.inc(mm)
                        ewait(nc.scalar, s_pe, xln_rel[slot])
                        dst = xlnT.ap().rearrange("p (dc t) -> p dc t", dc=NDC)[:, :, tt * 128:(tt + 1) * 128]
                        src = pb.ap().rearrange("p (dc t) -> p dc t", dc=NDC)
                        ev = s_act.inc(nc.scalar.copy(dst, src))
                        release_banks(pb, s_act, ev)
                    ewait(nc.vector, s_pe, xln_rel[2])
                    ewait(nc.vector, s_pe, xln_rel[3])
                    ones_fix = s_dve.inc(nc.vector.memset(vst4[:, :, :, 64:65], 1.0))
                xlnT_ready = s_act.n

                for i_ in range(4):
                    if tr_free[i_]:
                        ewait(nc.scalar, s_xl2[i_], tr_free[i_])
                    if x2_rel[i_]:
                        ewait(nc.scalar, s_x2w[i_], x2_rel[i_])
                load_w(3 * seq + 0, 0)
                load_w(3 * seq + 1, 1)
                with nc.psum_tensor(f"ps_qk0_{seq}", [128, 512], F32) as qk0, \
                     nc.psum_tensor(f"ps_qk1_{seq}", [128, 512], F32) as qk1, \
                     nc.psum_tensor(f"ps_v0_{seq}", [128, 1024], F32) as pv0, \
                     nc.psum_tensor(f"ps_v1_{seq}", [128, 1024], F32) as pv1:
                    ps_qk = [qk0, qk1]
                    ps_v = [pv0, pv1]
                    for wi, dstT in ((0, qT), (1, kT)):
                        widx = 3 * seq + wi
                        wslot = widx % 2
                        wait_w(nc.tensor, widx)
                        ewait(nc.tensor, s_act, xlnT_ready)
                        n = 0
                        for oc in range(8):
                            for tb in range(4):
                                pb = ps_qk[n % 2]
                                n += 1
                                acquire_banks(pb)
                                for dc in range(NDC):
                                    mm = nc.tensor.matmul(
                                        pb[:, :],
                                        wsb[wslot][:, dc * 1024 + oc * 128: dc * 1024 + oc * 128 + 128],
                                        xlnT[:, dc * SL + tb * 512: dc * SL + tb * 512 + 512],
                                        start=(dc == 0), stop=(dc == NDC - 1))
                                pe_c = s_pe.inc(mm)
                                if oc == 7 and tb == 3:
                                    w_last_use[wslot] = pe_c
                                    if wi == 0:
                                        load_w(3 * seq + 2, 2)
                                ewait(nc.scalar, s_pe, pe_c)
                                ev = s_act.inc(nc.scalar.copy(dstT[:, oc * SL + tb * 512: oc * SL + tb * 512 + 512], pb[:, :]))
                                release_banks(pb, s_act, ev)
                    widx = 3 * seq + 2
                    wslot = widx % 2
                    wait_w(nc.tensor, widx)
                    for tt in range(NTT):
                        pb = ps_v[tt % 2]
                        acquire_banks(pb)
                        for dc in range(NDC):
                            for dh in range(2):
                                mm = nc.tensor.matmul(
                                    pb[:, dh * 512:(dh + 1) * 512],
                                    xlnT[:, dc * SL + tt * 128: dc * SL + tt * 128 + 128],
                                    wsb[wslot][:, dc * 1024 + dh * 512: dc * 1024 + dh * 512 + 512],
                                    start=(dc == 0), stop=(dc == NDC - 1))
                        pe_c = s_pe.inc(mm)
                        if tt == NTT - 1:
                            w_last_use[wslot] = pe_c
                        ewait(nc.scalar, s_pe, pe_c)
                        dst = vst4[:, tt, :, 0:64]
                        src = pb.ap().rearrange("p (h c) -> p h c", h=NH)
                        ev = s_act.inc(nc.scalar.copy(dst, src))
                        release_banks(pb, s_act, ev)

                load_w(6 + seq, 3)
                with nc.psum_tensor(f"ps_sc_{seq}", [128, 2048], F32) as ps_sc, \
                     nc.psum_tensor(f"ps_cx0_{seq}", [128, 260], F32) as pcx0, \
                     nc.psum_tensor(f"ps_cx1_{seq}", [128, 260], F32) as pcx1, \
                     nc.psum_tensor(f"ps_ctr_{seq}", [128, 256], BF16) as ptr:
                    ps_cx = [pcx0, pcx1]
                    pT = [xlnT[:, i * 1024:(i + 1) * 1024] for i in range(3)]
                    nrmc = [xlnT[:, 3072 + k * 128: 3072 + (k + 1) * 128] for k in range(4)]
                    rcol = [xlnT[:, 3584 + 2 * k: 3584 + 2 * k + 2].bitcast(F32) for k in range(8)]
                    QBS = 256
                    NQB2 = SL // QBS
                    blocks = [(hp, qb) for hp in range(NH // 2) for qb in range(NQB2)]
                    strips = []
                    for bi, (hp, qb) in enumerate(blocks):
                        for kc in range(2 * qb + 2):
                            strips.append((bi, hp, qb, kc))
                    NSTR = len(strips)
                    assert NSTR % 2 == 0
                    exp_done = {}
                    msk_done = {}
                    pT_rel = [0, 0, 0]
                    nslot_ctr = [0]
                    nrm_rel = [0] * 4

                    def issue_sT2(qi):
                        i0_, i1_ = 2 * qi, 2 * qi + 1
                        slot = qi % 3
                        sc0 = (qi % 2) * 1024
                        acquire_bank_cols(ps_sc, sc0, sc0 + 1024, 4)
                        for (i_, half) in ((i0_, 0), (i0_, 1), (i1_, 0), (i1_, 1)):
                            bi, hp, qb, kc = strips[i_]
                            oc = hp
                            q0 = oc * SL + qb * QBS
                            k0 = oc * SL + kc * 128
                            hb = half * 64
                            dst = ps_sc[:, sc0 + half * 512 + (i_ % 2) * 256: sc0 + half * 512 + (i_ % 2) * 256 + 256]
                            mm = nc.tensor.matmul(
                                dst,
                                kT[hb:hb + 64, k0:k0 + 128], qT[hb:hb + 64, q0:q0 + QBS],
                                start=True, stop=True)
                        done = s_pe.inc(mm)
                        ewait(nc.scalar, s_pe, done)
                        if pT_rel[slot]:
                            ewait(nc.scalar, s_pe, pT_rel[slot])
                        a = s_act.inc(nc.scalar.activation(pT[slot], ps_sc[:, sc0:sc0 + 1024], AF.Exp, bias=neg12_col[:, :], scale=0.125))
                        release_bank_cols(ps_sc, sc0, sc0 + 1024, 4, s_act, a)
                        exp_done[qi] = a
                        for i_ in (i0_, i1_):
                            bi, hp, qb, kc = strips[i_]
                            if kc >= 2 * qb:
                                o = qb * QBS - kc * 128 + 384
                                c0 = (i_ % 2) * 256
                                ewait(nc.vector, s_act, a)
                                nc.vector.tensor_tensor(
                                    pT[slot][:, c0:c0 + 256], pT[slot][:, c0:c0 + 256],
                                    bmask[:, o:o + 256], op=ALU.mult)
                                msk_done[i_] = s_dve.inc(nc.vector.tensor_tensor(
                                    pT[slot][:, 512 + c0:512 + c0 + 256], pT[slot][:, 512 + c0:512 + c0 + 256],
                                    bmask[:, o:o + 256], op=ALU.mult))

                    def issue_ctx(i):
                        bi, hp, qb, kc = strips[i]
                        slot = (i // 2) % 3
                        c0 = (i % 2) * 256
                        pc = ps_cx[bi % 2]
                        nkc = 2 * qb + 2
                        if kc == 0:
                            acquire_banks(pc)
                        if i in msk_done:
                            ewait(nc.tensor, s_dve, msk_done[i])
                        else:
                            ewait(nc.tensor, s_act, exp_done[i // 2])
                        v0 = kc * NH * 65
                        for half in (0, 1):
                            for hh in (0, 1):
                                mm = nc.tensor.matmul(
                                    pc[:, half * 130 + hh * 65: half * 130 + hh * 65 + 65],
                                    pT[slot][:, hh * 512 + c0 + half * 128: hh * 512 + c0 + half * 128 + 128],
                                    vst[:, v0 + (2 * hp + hh) * 65: v0 + (2 * hp + hh) * 65 + 65],
                                    start=(kc == 0 and half == 0 and hh == 0),
                                    stop=(kc == nkc - 1 and half == 1 and hh == 1))
                        r = s_pe.inc(mm)
                        if i % 2 == 1:
                            pT_rel[slot] = r
                        return r

                    def tail_dve(bi, last_pe):
                        pc = ps_cx[bi % 2]
                        ewait(nc.vector, s_pe, last_pe)
                        ks = []
                        nv = 0
                        for half in (0, 1):
                            k = nslot_ctr[0]
                            nslot_ctr[0] = (k + 1) % 4
                            ks.append(k)
                            if nrm_rel[k]:
                                ewait(nc.vector, s_pe, nrm_rel[k])
                            r0, r1 = rcol[2 * k], rcol[2 * k + 1]
                            nc.vector.reciprocal(r0, pc[:, half * 130 + 64: half * 130 + 65])
                            nc.vector.reciprocal(r1, pc[:, half * 130 + 129: half * 130 + 130])
                            nc.vector.tensor_scalar(
                                nrmc[k][:, 0:64], pc[:, half * 130: half * 130 + 64], r0, None, op0=ALU.mult)
                            nv = s_dve.inc(nc.vector.tensor_scalar(
                                nrmc[k][:, 64:128], pc[:, half * 130 + 65: half * 130 + 129], r1, None, op0=ALU.mult))
                        release_banks(pc, s_dve, nv)
                        return (bi, nv, ks)

                    def tail_pe(bi, nv, ks):
                        hp, qb = blocks[bi]
                        q0 = hp * SL + qb * QBS
                        acquire_banks(ptr)
                        ewait(nc.tensor, s_dve, nv)
                        for half in (0, 1):
                            mm = nc.tensor.transpose(ptr[:, half * 128:(half + 1) * 128], nrmc[ks[half]][:, :], ident[:, :])
                        tr = s_pe.inc(mm)
                        nrm_rel[ks[0]] = tr
                        nrm_rel[ks[1]] = tr
                        ewait(nc.scalar, s_pe, tr)
                        cc = s_act.inc(nc.scalar.copy(qT[:, q0:q0 + QBS], ptr[:, :]))
                        release_banks(ptr, s_act, cc)
                        return cc

                    NQUAD = NSTR // 2
                    ewait(nc.tensor, s_dve, ones_fix)
                    issue_sT2(0)
                    pend = []
                    ctxT_ready = 0
                    for qi in range(NQUAD + 1):
                        if qi + 1 < NQUAD:
                            issue_sT2(qi + 1)
                        if qi < 1:
                            continue
                        qc = qi - 1
                        for i in (2 * qc, 2 * qc + 1):
                            bi, hp, qb, kc = strips[i]
                            last_pe = issue_ctx(i)
                            if kc == 2 * qb + 1:
                                pend.append(tail_dve(bi, last_pe))
                                if len(pend) > 1:
                                    ctxT_ready = tail_pe(*pend.pop(0))
                    while pend:
                        ctxT_ready = tail_pe(*pend.pop(0))
                ctxT = qT

                widx = 6 + seq
                wslot = widx % 2
                with nc.psum_tensor(f"ps_o0_{seq}", [128, 1024], F32) as o0, \
                     nc.psum_tensor(f"ps_o1_{seq}", [128, 1024], F32) as o1, \
                     nc.psum_tensor(f"ps_tr2_{seq}", [128, 1024], BF16) as tr2, \
                     nc.psum_tensor(f"ps_tr3_{seq}", [128, 1024], BF16) as tr3, \
                     nc.psum_tensor(f"ps_xr0_{seq}", [16, 128], F32) as xr0, \
                     nc.psum_tensor(f"ps_xr1_{seq}", [16, 128], F32) as xr1:
                    ps_o = [o0, o1]
                    ps_tr = [tr2, tr3]
                    ps_xr = [xr0, xr1]
                    wait_w(nc.tensor, widx)
                    ewait(nc.tensor, s_act, ctxT_ready)
                    ewait(nc.sync, s_act, ctxT_ready)
                    LAGC = 3
                    r2_of = {}
                    for step in range(NTT + LAGC):
                        if step < NTT:
                            tt = step
                            slot = tt % 4
                            if tt == 0:
                                for i in range(3):
                                    load_x(seq, i, i)
                            if tt + 3 < NTT:
                                load_x(seq, tt + 3, (tt + 3) % 4)
                            pb = ps_o[tt % 2]
                            acquire_banks(pb)
                            for dc in range(NDC):
                                for dh in range(2):
                                    mm = nc.tensor.matmul(
                                        pb[:, dh * 512:(dh + 1) * 512],
                                        ctxT[:, dc * SL + tt * 128: dc * SL + tt * 128 + 128],
                                        wsb[wslot][:, dc * 1024 + dh * 512: dc * 1024 + dh * 512 + 512],
                                        start=(dc == 0), stop=(dc == NDC - 1))
                            pe_c = s_pe.inc(mm)
                            if tt == NTT - 1:
                                w_last_use[wslot] = pe_c
                            ewait(nc.vector, s_pe, pe_c)
                            xs, xv = x_done[(seq, tt)]
                            ewait(nc.vector, s_x[xs], xv)
                            if x2_rel[slot]:
                                ewait(nc.vector, s_x2w[slot], x2_rel[slot])
                            if x2_reader[slot]:
                                ewait(nc.vector, s_dve, x2_reader[slot])
                            rs = s_dve.inc(nc.vector.tensor_tensor(x2_sb[slot], pb[:, :], x_tile[slot], op=ALU.add))
                            x_free[slot] = rs
                            release_banks(pb, s_dve, rs)
                            row0 = seq * SL + tt * 128
                            ewait(nc.sync, s_dve, rs)
                            x2_rel[slot] = s_x2w[slot].inc(nc.sync.dma_start(x2s[row0:row0 + 128, :], x2_sb[slot]), 16)
                            if xln_rel[slot]:
                                ewait(nc.vector, s_pe, xln_rel[slot])
                            r2_of[tt] = ln_tile(slot, x2_sb[slot], xln_t[slot])
                            x2_reader[slot] = r2_of[tt]
                        if step >= LAGC:
                            tt = step - LAGC
                            slot = tt % 4
                            pt = ps_tr[tt % 2]
                            acquire_banks(pt)
                            ewait(nc.tensor, s_dve, r2_of[tt])
                            for i in range(8):
                                mm = nc.tensor.transpose(pt[:, i * 128:(i + 1) * 128], xln_t[slot][:, i * 128:(i + 1) * 128], ident[:, :])
                            xln_rel[slot] = s_pe.inc(mm)
                            ewait(nc.scalar, s_pe, xln_rel[slot])
                            if tr_free[slot]:
                                ewait(nc.scalar, s_xl2[slot], tr_free[slot])
                            if tr_read[slot]:
                                ewait(nc.scalar, s_pe, tr_read[slot])
                            ev = s_act.inc(nc.scalar.copy(trbuf[slot].rearrange("p (dc t) -> p dc t", dc=NDC),
                                                          pt.ap().rearrange("p (dc t) -> p dc t", dc=NDC)))
                            release_banks(pt, s_act, ev)
                            ewait(nc.sync, s_act, ev)
                            t0c = seq * SL + tt * 128
                            tr_free[slot] = s_xl2[slot].inc(nc.sync.dma_start(
                                xl2s[:, :, t0c:t0c + 128].rearrange("dc p t -> p dc t"),
                                trbuf[slot].rearrange("p (dc t) -> p dc t", dc=NDC)), 16)
                            pxr = ps_xr[tt % 2]
                            acquire_banks(pxr)
                            ewait(nc.tensor, s_act, ev)
                            ewait(nc.tensor, s_pp, 16)
                            for dc in range(NDC):
                                mm = nc.tensor.matmul(
                                    pxr[:, :],
                                    wgr_sb[:, dc * 16:(dc + 1) * 16],
                                    trbuf[slot][:, dc * 128:(dc + 1) * 128],
                                    start=(dc == 0), stop=(dc == NDC - 1))
                            xr_pe = s_pe.inc(mm)
                            tr_read[slot] = xr_pe
                            ewait(nc.scalar, s_pe, xr_pe)
                            ev2 = s_act.inc(nc.scalar.copy(xrT[:, t0c:t0c + 128], pxr[:, :]))
                            release_banks(pxr, s_act, ev2)

            NGS = 4
            gmax = [A(nc.sbuf_tensor(f"gmax{i}", [128, 1], F32)) for i in range(NGS)]
            gnegm = [A(nc.sbuf_tensor(f"gnegm{i}", [128, 1], F32)) for i in range(NGS)]
            gexp = [A(nc.sbuf_tensor(f"gexp{i}", [128, 8], F32)) for i in range(NGS)]
            gsum = [A(nc.sbuf_tensor(f"gsum{i}", [128, 1], F32)) for i in range(NGS)]
            xr_ready = s_act.n
            with nc.psum_tensor("ps_lg0", [128, 8], F32) as lg0, \
                 nc.psum_tensor("ps_lg1", [128, 8], F32) as lg1, \
                 nc.psum_tensor("ps_lg2", [128, 8], F32) as lg2, \
                 nc.psum_tensor("ps_lg3", [128, 8], F32) as lg3:
                ps_lg = [lg0, lg1, lg2, lg3]
                ewait(nc.tensor, s_act, xr_ready)
                ewait(nc.tensor, s_act, wgu_ready)
                a1_slot = [0] * NGS
                sm_slot = [0] * NGS
                for tt in range(32):
                    slot = tt % NGS
                    pb = ps_lg[slot]
                    acquire_banks(pb)
                    mm = nc.tensor.matmul(pb[:, :], xrT[:, tt * 128:(tt + 1) * 128], wguT[:, :], start=True, stop=True)
                    pe_c = s_pe.inc(mm)
                    ewait(nc.vector, s_pe, pe_c)
                    if a1_slot[slot]:
                        ewait(nc.vector, s_act, a1_slot[slot])
                    m1 = s_dve.inc(nc.vector.reduce_max(gmax[slot][:, :], pb[:, :], axis=AX.X))
                    ewait(nc.scalar, s_dve, m1)
                    a1 = s_act.inc(nc.scalar.mul(gnegm[slot][:, :], gmax[slot][:, :], -1.0))
                    a1_slot[slot] = a1
                    ewait(nc.scalar, s_act, a1)
                    if sm_slot[slot]:
                        ewait(nc.scalar, s_dve, sm_slot[slot])
                    a2 = s_act.inc(nc.scalar.activation(gexp[slot][:, :], pb[:, :], AF.Exp, bias=gnegm[slot][:, :]))
                    ewait(nc.vector, s_act, a2)
                    oh1 = s_dve.inc(nc.vector.tensor_scalar(oh_sb[:, tt * 8:(tt + 1) * 8], pb[:, :], gmax[slot][:, :], None, op0=ALU.is_equal))
                    release_banks(pb, s_dve, oh1)
                    sm = s_dve.inc(nc.vector.reduce_sum(gsum[slot][:, :], gexp[slot][:, :], axis=AX.X))
                    sm_slot[slot] = sm
                    ewait(nc.vector, s_dve, sm)
                    s_dve.inc(nc.vector.reciprocal(sc_sb[:, tt:tt + 1], gsum[slot][:, :]))
            gating_done = s_dve.n
            act_fin = s_act.n
            ewait(nc.tensor, s_dve, gating_done)
            ewait(nc.scalar, s_dve, gating_done)
            ewait(nc.sync, s_st, s_st.n)
            for i_ in range(4):
                ewait(nc.sync, s_xl2[i_], s_xl2[i_].n)
                ewait(nc.sync, s_x2w[i_], s_x2w[i_].n)
            ewait(nc.sync, s_dve, gating_done)
            ewait(nc.sync, s_act, act_fin)
            ewait(nc.gpsimd, s_dve, gating_done)
            ewait(nc.gpsimd, s_act, act_fin)

        with ExitStack() as fst:
            F = fst.enter_context
            xbuf = [F(nc.sbuf_tensor(f"xbuf{i}", [128, NDC * TCH], BF16)) for i in range(2)]
            hT = F(nc.sbuf_tensor("hT", [128, NHC * TCH], BF16))
            w1s = [F(nc.sbuf_tensor(f"w1s{i}", [128, 4096], BF16)) for i in range(3)]
            w2s = [F(nc.sbuf_tensor(f"w2s{i}", [128, 4096], BF16)) for i in range(3)]
            acc = F(nc.sbuf_tensor("acc", [128, NSUB * 1024], F32))
            b1sb = F(nc.sbuf_tensor("b1sb", [128, E * 32], F32))
            b2sb = F(nc.sbuf_tensor("b2sb", [1, E * 1024], BF16))
            x2b = [F(nc.sbuf_tensor(f"x2b{i}", [128, 1024], F32)) for i in range(NSUB)]
            osb = [F(nc.sbuf_tensor(f"osb{i}", [128, 1024], F32)) for i in range(NSUB)]

            s_ld.inc(nc.sync.dma_start(b1sb.ap().rearrange("p (e c) -> p e c", e=E), b1r.rearrange("e p c -> p e c")), 16)
            s_pp.inc(nc.gpsimd.dma_start(b2sb[:, :], b2f), 16)
            const_ld = s_ld.n
            const_pp = s_pp.n

            xln_done = [0] * NTC
            def load_xln(tc):
                src = xl2s[:, :, tc * TCH:(tc + 1) * TCH].rearrange("dc p t -> p dc t")
                dst = xbuf[tc % 2].ap().rearrange("p (dc t) -> p dc t", dc=NDC)
                xln_done[tc] = s_xb[tc % 2].inc(nc.sync.dma_start(dst, src), 16)
            load_xln(0)

            xln_last_use = [0] * NTC
            w1_done = {}
            w1_last_use = {}
            w2_done = {}
            w2_last_use = {}
            gelu_done = [0] * NHC
            last_stt = 0
            acc_writer = [0] * NSUB
            x2b_free = [0] * NSUB
            osb_store_done = [0] * NSUB
            pending_stores = []
            x2_done = {}
            w1_issued = set()
            w2_issued = set()
            w1_slot_of = {}
            w2_slot_of = {}
            w1_ctr = [0]
            w2_ctr = [0]

            def issue_w1(tc_, e_, g_):
                if (tc_, e_, g_) in w1_issued or tc_ >= NTC:
                    return
                w1_issued.add((tc_, e_, g_))
                slot = w1_ctr[0] % 3
                w1_ctr[0] += 1
                w1_slot_of[(tc_, e_, g_)] = slot
                prev_use = w1_last_use.get(slot, 0)
                if prev_use:
                    ewait(nc.sync, s_pe, prev_use)
                ewait(nc.sync, s_ppe[e_], pp_ready[('e', e_)])
                w1_done[(tc_, e_, g_)] = (slot, s_w1[slot].inc(nc.sync.dma_start(w1s[slot][:, :], w1b[e_, g_]), 16))

            def issue_w2(tc_, e_, j_):
                if (tc_, e_, j_) in w2_issued or tc_ >= NTC:
                    return
                w2_issued.add((tc_, e_, j_))
                slot = w2_ctr[0] % 3
                w2_ctr[0] += 1
                w2_slot_of[(tc_, e_, j_)] = slot
                prev_use = w2_last_use.get(slot, 0)
                if prev_use:
                    ewait(nc.gpsimd, s_pe, prev_use)
                ewait(nc.gpsimd, s_ppe[e_], pp_ready[('e', e_)])
                src = w2b[e_, 4 * j_:4 * j_ + 4].rearrange("h p d -> p h d")
                dst = w2s[slot].ap().rearrange("p (h d) -> p h d", h=4)
                w2_done[(tc_, e_, j_)] = (slot, s_w2[slot].inc(nc.gpsimd.dma_start(dst, src), 16))

            def flush_stores():
                while pending_stores:
                    tc_, tsub_, w = pending_stores.pop(0)
                    row0 = tc_ * TCH + tsub_ * 128
                    ewait(nc.sync, s_dve, w)
                    osb_store_done[tsub_] = s_st.inc(nc.sync.dma_start(out[row0:row0 + 128, :], osb[tsub_][:, :]), 16)

            first_e_mm = True
            ewait(nc.tensor, s_ld, const_ld)
            ewait(nc.tensor, s_pp, const_pp)
            ewait(nc.vector, s_ld, const_ld)
            ewait(nc.scalar, s_ld, const_ld)

            for tc in range(NTC):
                if tc + 1 < NTC:
                    if tc >= 1:
                        ewait(nc.sync, s_pe, xln_last_use[tc - 1])
                    load_xln(tc + 1)
                xb = xbuf[tc % 2]
                for e in range(E):
                    if e == 1:
                        flush_stores()
                    if e == E - 1:
                        for tsub in range(NSUB):
                            row0 = tc * TCH + tsub * 128
                            if x2b_free[tsub]:
                                ewait(nc.sync, s_dve, x2b_free[tsub])
                            x2_done[(tc, tsub)] = s_x2b.inc(nc.sync.dma_start(x2b[tsub][:, :], x2s[row0:row0 + 128, :]), 16)
                    with nc.psum_tensor(f"ps_f1a_{tc}_{e}", [128, 512], F32) as f1a, \
                         nc.psum_tensor(f"ps_f1b_{tc}_{e}", [128, 512], F32) as f1b:
                        ps_f1 = [f1a, f1b]
                        issue_w1(tc, e, 0)
                        issue_w1(tc, e, 1)
                        for g in range(NG):
                            slot = w1_slot_of[(tc, e, g)]
                            if g + 2 < NG:
                                issue_w1(tc, e, g + 2)
                            else:
                                ntc_ = tc + 1 if e == E - 1 else tc
                                issue_w1(ntc_, (e + 1) % E, g + 2 - NG)
                            w1s_, w1v_ = w1_done[(tc, e, g)]
                            ewait(nc.tensor, s_w1[w1s_], w1v_)
                            if g == 0:
                                ewait(nc.tensor, s_xb[tc % 2], xln_done[tc])
                            for hc in range(4 * g, 4 * g + 4):
                                pb = ps_f1[hc % 2]
                                acquire_banks(pb)
                                for dc in range(NDC):
                                    mm = nc.tensor.matmul(
                                        pb[:, :],
                                        w1s[slot][:, dc * 512 + (hc % 4) * 128: dc * 512 + (hc % 4) * 128 + 128],
                                        xb[:, dc * 512:(dc + 1) * 512],
                                        start=(dc == 0), stop=(dc == NDC - 1))
                                w1_last_use[slot] = s_pe.inc(mm)
                                if hc == NHC - 1:
                                    xln_last_use[tc] = w1_last_use[slot]
                                ewait(nc.scalar, s_pe, w1_last_use[slot])
                                g_i = nc.scalar.activation(
                                    hT[:, hc * TCH:(hc + 1) * TCH], pb[:, :], AF.Gelu,
                                    bias=b1sb[:, e * 32 + hc: e * 32 + hc + 1])
                                gelu_done[hc] = s_act.inc(g_i)
                                release_banks(pb, s_act, gelu_done[hc])
                        first_e_mm = False
                    with nc.psum_tensor(f"ps_f2a_{tc}_{e}", [128, 1024], F32) as f2a, \
                         nc.psum_tensor(f"ps_f2b_{tc}_{e}", [128, 1024], F32) as f2b, \
                         nc.psum_tensor(f"ps_f2c_{tc}_{e}", [128, 1024], F32) as f2c, \
                         nc.psum_tensor(f"ps_f2d_{tc}_{e}", [128, 1024], F32) as f2d:
                        ps_f2 = [f2a, f2b, f2c, f2d]
                        f2_stop = [0] * NSUB
                        issue_w2(tc, e, 0)
                        issue_w2(tc, e, 1)
                        for tsub in range(NSUB):
                            acquire_banks(ps_f2[tsub])
                            for dh in range(2):
                                nc.tensor.matmul(
                                    ps_f2[tsub][:, dh * 512:(dh + 1) * 512],
                                    onesb[:, :],
                                    b2sb[0:1, e * 1024 + dh * 512: e * 1024 + dh * 512 + 512],
                                    start=True, stop=False)
                        for j in range(NHC // 4):
                            slot = w2_slot_of[(tc, e, j)]
                            if j + 2 < NHC // 4:
                                issue_w2(tc, e, j + 2)
                            else:
                                ntc_ = tc + 1 if e == E - 1 else tc
                                issue_w2(ntc_, (e + 1) % E, j + 2 - NHC // 4)
                            w2s_, w2v_ = w2_done[(tc, e, j)]
                            ewait(nc.tensor, s_w2[w2s_], w2v_)
                            for hc in range(4 * j, 4 * j + 4):
                                for tsub in range(NSUB):
                                    for dh in range(2):
                                        mm = nc.tensor.matmul(
                                            ps_f2[tsub][:, dh * 512:(dh + 1) * 512],
                                            hT[:, hc * TCH + tsub * 128: hc * TCH + tsub * 128 + 128],
                                            w2s[slot][:, (hc % 4) * 1024 + dh * 512: (hc % 4) * 1024 + (dh + 1) * 512],
                                            start=False, stop=(hc == NHC - 1 and dh == 1))
                                    if hc == NHC - 1:
                                        f2_stop[tsub] = s_pe.inc(mm)
                            if f2_stop[NSUB - 1]:
                                w2_last_use[slot] = f2_stop[NSUB - 1] if 4 * j + 3 == NHC - 1 else s_pe.inc(mm)
                            else:
                                w2_last_use[slot] = s_pe.inc(mm)
                        for tsub in range(NSUB):
                            ewait(nc.vector, s_pe, f2_stop[tsub])
                            mcol = oh_sb[:, (tc * NSUB + tsub) * E + e: (tc * NSUB + tsub) * E + e + 1]
                            halves = ((0, 512), (512, 1024)) if tsub == 0 else ((0, 1024),)
                            for (c0, c1) in halves:
                                a_sl = acc[:, tsub * 1024 + c0: tsub * 1024 + c1]
                                if e == 0:
                                    i = nc.vector.tensor_scalar(a_sl, ps_f2[tsub][:, c0:c1], mcol, None, op0=ALU.mult)
                                else:
                                    i = nc.vector.scalar_tensor_tensor(a_sl, ps_f2[tsub][:, c0:c1], mcol, a_sl, op0=ALU.mult, op1=ALU.add)
                                w = s_dve.inc(i)
                                release_bank_cols(ps_f2[tsub], c0, c1, 4, s_dve, w)
                            acc_writer[tsub] = w
                        last_stt = acc_writer[NSUB - 1]
                for tsub in range(NSUB):
                    ewait(nc.vector, s_x2b, x2_done[(tc, NSUB - 1)])
                    ewait(nc.vector, s_dve, acc_writer[tsub])
                    if osb_store_done[tsub]:
                        ewait(nc.vector, s_st, osb_store_done[tsub])
                    sc = sc_sb[:, tc * NSUB + tsub: tc * NSUB + tsub + 1]
                    i = nc.vector.scalar_tensor_tensor(osb[tsub][:, :], acc[:, tsub * 1024:(tsub + 1) * 1024], sc, x2b[tsub][:, :], op0=ALU.mult, op1=ALU.add)
                    w = s_dve.inc(i)
                    pending_stores.append((tc, tsub, w))
                    acc_writer[tsub] = w
                    x2b_free[tsub] = w
            flush_stores()
            nc.sync.wait_ge(s_st.h, s_st.n)
    return nc


_NC_CACHE = {}

def _get_nc():
    if "nc" not in _NC_CACHE:
        _NC_CACHE["nc"] = build()
    return _NC_CACHE["nc"]


def kernel(L0_attn_x, L0_attn_W_Q, L0_attn_W_K, L0_attn_W_V, L0_attn_W_O,
           L0_ffn_wg_red, L0_ffn_wg, L0_ffn_w1, L0_ffn_b1, L0_ffn_w2, L0_ffn_b2):
    x = np.asarray(L0_attn_x, dtype=np.float32)
    WQ = np.asarray(L0_attn_W_Q, dtype=np.float32)
    WK = np.asarray(L0_attn_W_K, dtype=np.float32)
    WV = np.asarray(L0_attn_W_V, dtype=np.float32)
    WO = np.asarray(L0_attn_W_O, dtype=np.float32)
    wg_red = np.asarray(L0_ffn_wg_red, dtype=np.float32)
    wg = np.asarray(L0_ffn_wg, dtype=np.float32)
    w1 = np.ascontiguousarray(np.asarray(L0_ffn_w1, dtype=np.float32))
    b1 = np.asarray(L0_ffn_b1, dtype=np.float32)
    w2 = np.ascontiguousarray(np.asarray(L0_ffn_w2, dtype=np.float32))
    b2 = np.asarray(L0_ffn_b2, dtype=np.float32)

    wqt = np.ascontiguousarray(WQ.T)
    wkt = np.ascontiguousarray(WK.T)
    wvt = np.ascontiguousarray(WV.T)
    wot = np.ascontiguousarray(WO.T)
    wgr = np.ascontiguousarray(wg_red.T)
    b1r = np.ascontiguousarray(b1.reshape(E, 32, 128).transpose(0, 2, 1))
    b2f = np.ascontiguousarray(b2.reshape(1, E * D))

    nc = _get_nc()
    shared = dict(wqt=wqt, wkt=wkt, wvt=wvt, wot=wot, wgr=wgr, wg=wg,
                  w1=w1, b1r=b1r, w2=w2, b2f=b2f)
    in_maps = []
    for c in range(8):
        shard = np.ascontiguousarray(x[2 * c:2 * c + 2].reshape(T, D))
        m = dict(shared)
        m["x_in"] = shard
        in_maps.append(m)
    res = run_bass_kernel_spmd(nc, in_maps, core_ids=list(range(8)))
    outs = [np.asarray(res.results[c]["out"], dtype=np.float32).reshape(2, SL, D) for c in range(8)]
    return np.concatenate(outs, axis=0)


if __name__ == "__main__":
    import importlib.util
    spec = importlib.util.spec_from_file_location("_ref", "/root/problem/reference.py")
    ref = importlib.util.module_from_spec(spec)
    spec.loader.exec_module(ref)
    inputs = {k: np.asarray(v) for k, v in ref.setup_inputs().items()}
    got = kernel(**inputs)
    want = np.asarray(ref.reference(**inputs))
    err = got - want
    ratio = float((err ** 2).mean() / (want ** 2).mean())
    print("residual variance ratio:", ratio, "(bar 2.27e-04)")
    print("PASS" if ratio < 2.27e-4 else "FAIL")


# version 24
# speedup vs baseline: 1.0260x; 1.0260x over previous
import numpy as np
from contextlib import ExitStack
import concourse.bass as bass
import concourse.mybir as mybir
from concourse.bass_utils import run_bass_kernel_spmd

F32 = mybir.dt.float32
BF16 = mybir.dt.bfloat16
I32 = mybir.dt.int32
AF = mybir.ActivationFunctionType
ALU = mybir.AluOpType
AX = mybir.AxisListType

NSEQ, SL, D, NH, DK = 2, 2048, 1024, 16, 64
T = NSEQ * SL
NDC = D // 128
NTT = SL // 128
NQB = SL // 512
NKC = SL // 128
E = 8
HID = 4096
NHC = HID // 128
NG = 8
TCH = 512
NTC = T // TCH
NSUB = TCH // 128


class Sem:
    def __init__(self, stack, nc, name):
        self.h = stack.enter_context(nc.semaphore(name))
        self.n = 0
    def inc(self, instr, k=1):
        instr.then_inc(self.h, k)
        self.n += k
        return self.n


def build():
    nc = bass.Bass("TRN2", target_bir_lowering=False, debug=False)
    x_in = nc.dram_tensor("x_in", [T, D], F32, kind="ExternalInput").ap()
    wqt = nc.dram_tensor("wqt", [D, D], F32, kind="ExternalInput").ap()
    wkt = nc.dram_tensor("wkt", [D, D], F32, kind="ExternalInput").ap()
    wvt = nc.dram_tensor("wvt", [D, D], F32, kind="ExternalInput").ap()
    wot = nc.dram_tensor("wot", [D, D], F32, kind="ExternalInput").ap()
    wgr = nc.dram_tensor("wgr", [D, 16], F32, kind="ExternalInput").ap()
    wg = nc.dram_tensor("wg", [8, 16], F32, kind="ExternalInput").ap()
    w1 = nc.dram_tensor("w1", [E, D, HID], F32, kind="ExternalInput").ap()
    b1r = nc.dram_tensor("b1r", [E, 128, 32], F32, kind="ExternalInput").ap()
    w2 = nc.dram_tensor("w2", [E, HID, D], F32, kind="ExternalInput").ap()
    b2f = nc.dram_tensor("b2f", [1, E * D], F32, kind="ExternalInput").ap()
    out = nc.dram_tensor("out", [T, D], F32, kind="ExternalOutput").ap()
    x2s = nc.dram_tensor("x2s", [T, D], F32, kind="Internal").ap()
    xl2s = nc.dram_tensor("xl2s", [NDC, 128, T], BF16, kind="Internal").ap()
    wab = nc.dram_tensor("wab", [4, 128, NDC * 1024], BF16, kind="Internal").ap()
    w1b = nc.dram_tensor("w1b", [E, NG, 128, 4096], BF16, kind="Internal").ap()
    w2b = nc.dram_tensor("w2b", [E, NHC, 128, 1024], BF16, kind="Internal").ap()

    with ExitStack() as st:
        S = st.enter_context
        ident = S(nc.sbuf_tensor("ident", [128, 128], BF16))
        ident32 = S(nc.sbuf_tensor("ident32", [128, 128], F32))
        onesb = S(nc.sbuf_tensor("onesb", [1, 128], BF16))
        eps_col = S(nc.sbuf_tensor("eps_col", [128, 1], F32))
        neg12_col = S(nc.sbuf_tensor("neg12_col", [128, 1], F32))
        wgr_sb = S(nc.sbuf_tensor("wgr_sb", [128, NDC * 16], BF16))
        wg_sb = S(nc.sbuf_tensor("wg_sb", [8, 16], F32))
        wgu_bf = S(nc.sbuf_tensor("wgu_bf", [8, 16], BF16))
        wguT = S(nc.sbuf_tensor("wguT", [16, 8], BF16))
        xrT = S(nc.sbuf_tensor("xrT", [16, T], BF16))
        oh_sb = S(nc.sbuf_tensor("oh_sb", [128, 32 * E], F32))
        ohT_sb = S(nc.sbuf_tensor("ohT_sb", [8, T], BF16))
        b2a_sb = S(nc.sbuf_tensor("b2a_sb", [8, 1024], BF16))
        sc_sb = S(nc.sbuf_tensor("sc_sb", [128, 32], F32))
        sq = S(nc.sbuf_tensor("sq", [8, 16], F32))
        nrm = S(nc.sbuf_tensor("nrm", [8, 1], F32))
        nrm2 = S(nc.sbuf_tensor("nrm2", [8, 1], F32))
        rn = S(nc.sbuf_tensor("rn", [8, 1], F32))

        s_ld = Sem(st, nc, "s_ld")
        s_x = [Sem(st, nc, f"s_x{i}") for i in range(4)]
        s_w = [Sem(st, nc, f"s_w{i}") for i in range(2)]
        s_w1 = [Sem(st, nc, f"s_w1{i}") for i in range(3)]
        s_w2 = [Sem(st, nc, f"s_w2{i}") for i in range(3)]
        s_xb = [Sem(st, nc, f"s_xb{i}") for i in range(2)]
        s_x2b = Sem(st, nc, "s_x2b")
        s_ppe = [Sem(st, nc, f"s_ppe{i}") for i in range(E)]
        s_ppwa = Sem(st, nc, "s_ppwa")
        s_xl2 = [Sem(st, nc, f"s_xl2{i}") for i in range(4)]
        s_x2w = [Sem(st, nc, f"s_x2w{i}") for i in range(4)]
        s_pp = Sem(st, nc, "s_pp")
        s_pps = Sem(st, nc, "s_pps")
        s_ldw = Sem(st, nc, "s_ldw")
        s_pe = Sem(st, nc, "s_pe")
        s_act = Sem(st, nc, "s_act")
        s_dve = Sem(st, nc, "s_dve")
        s_st = Sem(st, nc, "s_st")
        s_gp = Sem(st, nc, "s_gp")

        wait_hwm = {}
        def ewait(eng, sem, val):
            if val <= 0:
                return
            hw = wait_hwm.setdefault(id(eng), {})
            if hw.get(id(sem.h), 0) >= val:
                return
            eng.wait_ge(sem.h, val)
            hw[id(sem.h)] = val
        bank_release = {}
        def nbanks(h):
            return max(1, (int(h.ap().free_nbytes()) + 2047) // 2048)
        def acquire_banks(h):
            m = nc.lookup_mloc(h)
            for b in range(m.bank, m.bank + nbanks(h)):
                r = bank_release.get(b)
                if r is not None:
                    ewait(nc.tensor, r[0], r[1])
        def release_banks(h, sem, val):
            m = nc.lookup_mloc(h)
            for b in range(m.bank, m.bank + nbanks(h)):
                bank_release[b] = (sem, val)
        def acquire_bank_cols(h, c0, c1, elt):
            m = nc.lookup_mloc(h)
            b0 = m.bank + (c0 * elt) // 2048
            b1 = m.bank + ((c1 * elt) - 1) // 2048
            for b in range(b0, b1 + 1):
                r = bank_release.get(b)
                if r is not None:
                    ewait(nc.tensor, r[0], r[1])
        def release_bank_cols(h, c0, c1, elt, sem, val):
            m = nc.lookup_mloc(h)
            b0 = m.bank + (c0 * elt) // 2048
            b1 = m.bank + ((c1 * elt) - 1) // 2048
            for b in range(b0, b1 + 1):
                bank_release[b] = (sem, val)

        with ExitStack() as ast_:
            A = ast_.enter_context
            bmask = A(nc.sbuf_tensor("bmask", [128, 896], BF16))
            xlnT = A(nc.sbuf_tensor("xlnT", [128, NDC * SL], BF16))
            qT = A(nc.sbuf_tensor("qT", [128, NDC * SL], BF16))
            kT = A(nc.sbuf_tensor("kT", [128, NDC * SL], BF16))
            vst = A(nc.sbuf_tensor("vst", [128, NKC * NH * 65], BF16))
            wsb = [A(nc.sbuf_tensor(f"wsb{i}", [128, NDC * 1024], BF16)) for i in range(2)]
            st6 = [A(nc.sbuf_tensor(f"st6{i}", [128, 12], F32)) for i in range(4)]
            mv = [A(nc.sbuf_tensor(f"mv{i}", [128, 2], F32)) for i in range(4)]
            stdv = [A(nc.sbuf_tensor(f"stdv{i}", [128, 1], F32)) for i in range(4)]
            negm = [A(nc.sbuf_tensor(f"negm{i}", [128, 1], F32)) for i in range(4)]
            rstd = [A(nc.sbuf_tensor(f"rstd{i}", [128, 1], F32)) for i in range(4)]
            x_tile_h = [A(nc.sbuf_tensor(f"x_tile{i}", [128, D], F32)) for i in range(2)]
            xln_t_h = [A(nc.sbuf_tensor(f"xln_t{i}", [128, D], BF16)) for i in range(2)]
            x_tile = [x_tile_h[0][:, :], x_tile_h[1][:, :],
                      vst[:, 2048:4096].bitcast(F32), vst[:, 4096:6144].bitcast(F32)]
            xln_t = [xln_t_h[0][:, :], xln_t_h[1][:, :],
                     vst[:, 0:1024], vst[:, 1024:2048]]
            x2_sb = [kT[:, i * 2048:(i + 1) * 2048].bitcast(F32) for i in range(4)]
            trbuf = [kT[:, 8192 + i * 1024: 8192 + (i + 1) * 1024] for i in range(4)]

            with nc.sbuf_tensor("iota32", [128, 448], I32) as iota32:
                g1 = s_gp.inc(nc.gpsimd.iota(iota32[:, :], [[1, 448]], base=-384, channel_multiplier=-1))
                ewait(nc.vector, s_gp, g1)
                nc.vector.tensor_scalar(bmask[:, 0:448], iota32[:, :], 0, None, op0=ALU.is_ge)
                nc.vector.tensor_scalar(ident32[:, 0:64], iota32[:, 384:448], 0, None, op0=ALU.is_equal)
                d0 = s_dve.inc(nc.vector.tensor_scalar(ident[:, 0:64], iota32[:, 384:448], 0, None, op0=ALU.is_equal))
                ewait(nc.gpsimd, s_dve, d0)
                g2 = s_gp.inc(nc.gpsimd.iota(iota32[:, :], [[1, 448]], base=64, channel_multiplier=-1))
                ewait(nc.vector, s_gp, g2)
                nc.vector.tensor_scalar(bmask[:, 448:896], iota32[:, :], 0, None, op0=ALU.is_ge)
                nc.vector.tensor_scalar(ident32[:, 64:128], iota32[:, 0:64], 0, None, op0=ALU.is_equal)
                d1 = s_dve.inc(nc.vector.tensor_scalar(ident[:, 64:128], iota32[:, 0:64], 0, None, op0=ALU.is_equal))
                nc.vector.memset(onesb[:, :], 1.0)
                nc.vector.memset(eps_col[:, :], 1e-5)
                nc.vector.memset(neg12_col[:, :], -12.0)
                vst4 = vst.ap().rearrange("p (kc h c) -> p kc h c", kc=NKC, h=NH)
                setup_dve = s_dve.inc(nc.vector.engine_nop())

                s_ld.inc(nc.sync.dma_start(wg_sb[:, :], wg), 16)
                s_pp.inc(nc.gpsimd.dma_start(wgr_sb.ap().rearrange("p (dc c) -> p dc c", dc=NDC),
                                             wgr.rearrange("(dc p) c -> p dc c", p=128)), 16)
                ewait(nc.scalar, s_ld, s_ld.n)
                a = s_act.inc(nc.scalar.activation(sq[:, :], wg_sb[:, :], AF.Square))
                ewait(nc.vector, s_act, a)
                v1 = s_dve.inc(nc.vector.reduce_sum(nrm[:, :], sq[:, :], axis=AX.X))
                ewait(nc.scalar, s_dve, v1)
                a = s_act.inc(nc.scalar.activation(nrm2[:, :], nrm[:, :], AF.Sqrt))
                ewait(nc.vector, s_act, a)
                v2 = s_dve.inc(nc.vector.tensor_scalar(nrm2[:, :], nrm2[:, :], 1e-4, None, op0=ALU.max))
                ewait(nc.vector, s_dve, v2)
                v3 = s_dve.inc(nc.vector.reciprocal(rn[:, :], nrm2[:, :]))
                ewait(nc.vector, s_dve, v3)
                v4 = s_dve.inc(nc.vector.tensor_scalar(wgu_bf[:, :], wg_sb[:, :], rn[:, :], None, op0=ALU.mult))
                with nc.psum_tensor("ps_wgu", [16, 8], BF16) as ps_wgu:
                    ewait(nc.tensor, s_dve, v4)
                    ewait(nc.tensor, s_dve, d1)
                    t1 = s_pe.inc(nc.tensor.transpose(ps_wgu[:, :], wgu_bf[:, :], ident[0:8, 0:8]))
                    ewait(nc.scalar, s_pe, t1)
                    wgu_ready = s_act.inc(nc.scalar.copy(wguT[:, :], ps_wgu[:, :]))

            pp_ready = {}
            def run_prepass():
                for widx, wsrc in enumerate((wqt, wkt, wvt, wot)):
                    src = wsrc.rearrange("(dc p) o -> p dc o", p=128)
                    dst = wab[widx].rearrange("p (dc o) -> p dc o", dc=NDC)
                    s_ppwa.inc(nc.gpsimd.dma_start(dst, src), 16)
                pp_ready['wa'] = s_ppwa.n
                for e_ in range(E):
                    for g in range(NG):
                        src = w1[e_][:, g * 512:(g + 1) * 512].rearrange("(dc p) h -> p dc h", p=128)
                        dst = w1b[e_, g].rearrange("p (dc h) -> p dc h", dc=NDC)
                        s_ppe[e_].inc(nc.gpsimd.dma_start(dst, src), 16)
                    for j in range(NG):
                        src = w2[e_][j * 512:(j + 1) * 512, :].rearrange("(hc p) d -> p hc d", p=128)
                        dst = w2b[e_, 4 * j:4 * j + 4].rearrange("hc p d -> p hc d")
                        s_ppe[e_].inc(nc.gpsimd.dma_start(dst, src), 16)
                    pp_ready[('e', e_)] = s_ppe[e_].n
            run_prepass()

            w_done = {}
            def wait_w(eng, idx):
                ewait(eng, s_w[idx % 2], w_done[idx])
            w_last_use = {}
            def load_w(idx, widx):
                slot = idx % 2
                if w_last_use.get(slot):
                    ewait(nc.sync, s_pe, w_last_use[slot])
                ewait(nc.sync, s_ppwa, pp_ready['wa'])
                w_done[idx] = s_w[slot].inc(nc.sync.dma_start(wsb[slot][:, :], wab[widx]), 16)

            x_done = {}
            x_free = [0] * 4
            def load_x(seq, tt, slot):
                row0 = seq * SL + tt * 128
                if x_free[slot]:
                    ewait(nc.sync, s_dve, x_free[slot])
                x_done[(seq, tt)] = (slot, s_x[slot].inc(nc.sync.dma_start(x_tile[slot], x_in[row0:row0 + 128, :]), 16))

            def ln_tile(slot, src_sb, dst_bf):
                nc.vector.bn_stats(st6[slot][:, 0:6], src_sb[:, 0:512])
                b1_ = s_dve.inc(nc.vector.bn_stats(st6[slot][:, 6:12], src_sb[:, 512:1024]))
                ewait(nc.vector, s_dve, b1_)
                b2_ = s_dve.inc(nc.vector.bn_aggr(mv[slot][:, :], st6[slot][:, :]))
                ewait(nc.scalar, s_dve, b2_)
                s_act.inc(nc.scalar.activation(stdv[slot][:, :], mv[slot][:, 1:2], AF.Sqrt, bias=eps_col[:, :]))
                a2 = s_act.inc(nc.scalar.mul(negm[slot][:, :], mv[slot][:, 0:1], -1.0))
                ewait(nc.vector, s_act, a2)
                r1 = s_dve.inc(nc.vector.reciprocal(rstd[slot][:, :], stdv[slot][:, :]))
                ewait(nc.vector, s_dve, r1)
                r2 = s_dve.inc(nc.vector.tensor_scalar(
                    dst_bf, src_sb, negm[slot][:, :], rstd[slot][:, :],
                    op0=ALU.add, op1=ALU.mult))
                return r2

            xln_rel = [0] * 4
            x2_rel = [0] * 4
            x2_reader = [0] * 4
            tr_free = [0] * 4
            tr_read = [0] * 4

            for seq in range(NSEQ):
                with nc.psum_tensor(f"ps_tr0_{seq}", [128, 1024], BF16) as tr0, \
                     nc.psum_tensor(f"ps_tr1_{seq}", [128, 1024], BF16) as tr1:
                    ps_tr = [tr0, tr1]
                    for i in range(3):
                        load_x(seq, i, i % 4)
                    for tt in range(NTT):
                        slot = tt % 4
                        xs, xv = x_done[(seq, tt)]
                        ewait(nc.vector, s_x[xs], xv)
                        if xln_rel[slot]:
                            ewait(nc.vector, s_pe, xln_rel[slot])
                        r2 = ln_tile(slot, x_tile[slot], xln_t[slot])
                        x_free[slot] = r2
                        if tt + 3 < NTT:
                            load_x(seq, tt + 3, (tt + 3) % 4)
                        ewait(nc.tensor, s_dve, r2)
                        if seq == 0 and tt == 0:
                            ewait(nc.tensor, s_dve, setup_dve)
                        pb = ps_tr[tt % 2]
                        acquire_banks(pb)
                        for i in range(8):
                            mm = nc.tensor.transpose(pb[:, i * 128:(i + 1) * 128], xln_t[slot][:, i * 128:(i + 1) * 128], ident[:, :])
                        xln_rel[slot] = s_pe.inc(mm)
                        ewait(nc.scalar, s_pe, xln_rel[slot])
                        dst = xlnT.ap().rearrange("p (dc t) -> p dc t", dc=NDC)[:, :, tt * 128:(tt + 1) * 128]
                        src = pb.ap().rearrange("p (dc t) -> p dc t", dc=NDC)
                        ev = s_act.inc(nc.scalar.copy(dst, src))
                        release_banks(pb, s_act, ev)
                    ewait(nc.vector, s_pe, xln_rel[2])
                    ewait(nc.vector, s_pe, xln_rel[3])
                    ones_fix = s_dve.inc(nc.vector.memset(vst4[:, :, :, 64:65], 1.0))
                xlnT_ready = s_act.n

                for i_ in range(4):
                    if tr_free[i_]:
                        ewait(nc.scalar, s_xl2[i_], tr_free[i_])
                    if x2_rel[i_]:
                        ewait(nc.scalar, s_x2w[i_], x2_rel[i_])
                load_w(3 * seq + 0, 0)
                load_w(3 * seq + 1, 1)
                with nc.psum_tensor(f"ps_qk0_{seq}", [128, 512], F32) as qk0, \
                     nc.psum_tensor(f"ps_qk1_{seq}", [128, 512], F32) as qk1, \
                     nc.psum_tensor(f"ps_v0_{seq}", [128, 1024], F32) as pv0, \
                     nc.psum_tensor(f"ps_v1_{seq}", [128, 1024], F32) as pv1:
                    ps_qk = [qk0, qk1]
                    ps_v = [pv0, pv1]
                    for wi, dstT in ((0, qT), (1, kT)):
                        widx = 3 * seq + wi
                        wslot = widx % 2
                        wait_w(nc.tensor, widx)
                        ewait(nc.tensor, s_act, xlnT_ready)
                        n = 0
                        for oc in range(8):
                            for tb in range(4):
                                pb = ps_qk[n % 2]
                                n += 1
                                acquire_banks(pb)
                                for dc in range(NDC):
                                    mm = nc.tensor.matmul(
                                        pb[:, :],
                                        wsb[wslot][:, dc * 1024 + oc * 128: dc * 1024 + oc * 128 + 128],
                                        xlnT[:, dc * SL + tb * 512: dc * SL + tb * 512 + 512],
                                        start=(dc == 0), stop=(dc == NDC - 1))
                                pe_c = s_pe.inc(mm)
                                if oc == 7 and tb == 3:
                                    w_last_use[wslot] = pe_c
                                    if wi == 0:
                                        load_w(3 * seq + 2, 2)
                                ewait(nc.scalar, s_pe, pe_c)
                                ev = s_act.inc(nc.scalar.copy(dstT[:, oc * SL + tb * 512: oc * SL + tb * 512 + 512], pb[:, :]))
                                release_banks(pb, s_act, ev)
                    widx = 3 * seq + 2
                    wslot = widx % 2
                    wait_w(nc.tensor, widx)
                    for tt in range(NTT):
                        pb = ps_v[tt % 2]
                        acquire_banks(pb)
                        for dc in range(NDC):
                            for dh in range(2):
                                mm = nc.tensor.matmul(
                                    pb[:, dh * 512:(dh + 1) * 512],
                                    xlnT[:, dc * SL + tt * 128: dc * SL + tt * 128 + 128],
                                    wsb[wslot][:, dc * 1024 + dh * 512: dc * 1024 + dh * 512 + 512],
                                    start=(dc == 0), stop=(dc == NDC - 1))
                        pe_c = s_pe.inc(mm)
                        if tt == NTT - 1:
                            w_last_use[wslot] = pe_c
                        ewait(nc.scalar, s_pe, pe_c)
                        dst = vst4[:, tt, :, 0:64]
                        src = pb.ap().rearrange("p (h c) -> p h c", h=NH)
                        ev = s_act.inc(nc.scalar.copy(dst, src))
                        release_banks(pb, s_act, ev)

                load_w(6 + seq, 3)
                with nc.psum_tensor(f"ps_sc_{seq}", [128, 2048], F32) as ps_sc, \
                     nc.psum_tensor(f"ps_cx0_{seq}", [128, 260], F32) as pcx0, \
                     nc.psum_tensor(f"ps_cx1_{seq}", [128, 260], F32) as pcx1, \
                     nc.psum_tensor(f"ps_ctr_{seq}", [128, 256], BF16) as ptr:
                    ps_cx = [pcx0, pcx1]
                    pT = [xlnT[:, i * 1024:(i + 1) * 1024] for i in range(3)]
                    nrmc = [xlnT[:, 3072 + k * 128: 3072 + (k + 1) * 128] for k in range(4)]
                    rcol = [xlnT[:, 3584 + 2 * k: 3584 + 2 * k + 2].bitcast(F32) for k in range(8)]
                    QBS = 256
                    NQB2 = SL // QBS
                    blocks = [(hp, qb) for hp in range(NH // 2) for qb in range(NQB2)]
                    strips = []
                    for bi, (hp, qb) in enumerate(blocks):
                        for kc in range(2 * qb + 2):
                            strips.append((bi, hp, qb, kc))
                    NSTR = len(strips)
                    assert NSTR % 2 == 0
                    exp_done = {}
                    msk_done = {}
                    pT_rel = [0, 0, 0]
                    nslot_ctr = [0]
                    nrm_rel = [0] * 4

                    def issue_sT2(qi):
                        i0_, i1_ = 2 * qi, 2 * qi + 1
                        slot = qi % 3
                        sc0 = (qi % 2) * 1024
                        acquire_bank_cols(ps_sc, sc0, sc0 + 1024, 4)
                        for (i_, half) in ((i0_, 0), (i0_, 1), (i1_, 0), (i1_, 1)):
                            bi, hp, qb, kc = strips[i_]
                            oc = hp
                            q0 = oc * SL + qb * QBS
                            k0 = oc * SL + kc * 128
                            hb = half * 64
                            dst = ps_sc[:, sc0 + half * 512 + (i_ % 2) * 256: sc0 + half * 512 + (i_ % 2) * 256 + 256]
                            mm = nc.tensor.matmul(
                                dst,
                                kT[hb:hb + 64, k0:k0 + 128], qT[hb:hb + 64, q0:q0 + QBS],
                                start=True, stop=True)
                        done = s_pe.inc(mm)
                        ewait(nc.scalar, s_pe, done)
                        if pT_rel[slot]:
                            ewait(nc.scalar, s_pe, pT_rel[slot])
                        a = s_act.inc(nc.scalar.activation(pT[slot], ps_sc[:, sc0:sc0 + 1024], AF.Exp, bias=neg12_col[:, :], scale=0.125))
                        release_bank_cols(ps_sc, sc0, sc0 + 1024, 4, s_act, a)
                        exp_done[qi] = a
                        for i_ in (i0_, i1_):
                            bi, hp, qb, kc = strips[i_]
                            if kc >= 2 * qb:
                                o = qb * QBS - kc * 128 + 384
                                c0 = (i_ % 2) * 256
                                ewait(nc.vector, s_act, a)
                                nc.vector.tensor_tensor(
                                    pT[slot][:, c0:c0 + 256], pT[slot][:, c0:c0 + 256],
                                    bmask[:, o:o + 256], op=ALU.mult)
                                msk_done[i_] = s_dve.inc(nc.vector.tensor_tensor(
                                    pT[slot][:, 512 + c0:512 + c0 + 256], pT[slot][:, 512 + c0:512 + c0 + 256],
                                    bmask[:, o:o + 256], op=ALU.mult))

                    def issue_ctx(i):
                        bi, hp, qb, kc = strips[i]
                        slot = (i // 2) % 3
                        c0 = (i % 2) * 256
                        pc = ps_cx[bi % 2]
                        nkc = 2 * qb + 2
                        if kc == 0:
                            acquire_banks(pc)
                        if i in msk_done:
                            ewait(nc.tensor, s_dve, msk_done[i])
                        else:
                            ewait(nc.tensor, s_act, exp_done[i // 2])
                        v0 = kc * NH * 65
                        for half in (0, 1):
                            for hh in (0, 1):
                                mm = nc.tensor.matmul(
                                    pc[:, half * 130 + hh * 65: half * 130 + hh * 65 + 65],
                                    pT[slot][:, hh * 512 + c0 + half * 128: hh * 512 + c0 + half * 128 + 128],
                                    vst[:, v0 + (2 * hp + hh) * 65: v0 + (2 * hp + hh) * 65 + 65],
                                    start=(kc == 0 and half == 0 and hh == 0),
                                    stop=(kc == nkc - 1 and half == 1 and hh == 1))
                        r = s_pe.inc(mm)
                        if i % 2 == 1:
                            pT_rel[slot] = r
                        return r

                    def tail_dve(bi, last_pe):
                        pc = ps_cx[bi % 2]
                        ewait(nc.vector, s_pe, last_pe)
                        ks = []
                        nv = 0
                        for half in (0, 1):
                            k = nslot_ctr[0]
                            nslot_ctr[0] = (k + 1) % 4
                            ks.append(k)
                            if nrm_rel[k]:
                                ewait(nc.vector, s_pe, nrm_rel[k])
                            r0, r1 = rcol[2 * k], rcol[2 * k + 1]
                            nc.vector.reciprocal(r0, pc[:, half * 130 + 64: half * 130 + 65])
                            nc.vector.reciprocal(r1, pc[:, half * 130 + 129: half * 130 + 130])
                            nc.vector.tensor_scalar(
                                nrmc[k][:, 0:64], pc[:, half * 130: half * 130 + 64], r0, None, op0=ALU.mult)
                            nv = s_dve.inc(nc.vector.tensor_scalar(
                                nrmc[k][:, 64:128], pc[:, half * 130 + 65: half * 130 + 129], r1, None, op0=ALU.mult))
                        release_banks(pc, s_dve, nv)
                        return (bi, nv, ks)

                    def tail_pe(bi, nv, ks):
                        hp, qb = blocks[bi]
                        q0 = hp * SL + qb * QBS
                        acquire_banks(ptr)
                        ewait(nc.tensor, s_dve, nv)
                        for half in (0, 1):
                            mm = nc.tensor.transpose(ptr[:, half * 128:(half + 1) * 128], nrmc[ks[half]][:, :], ident[:, :])
                        tr = s_pe.inc(mm)
                        nrm_rel[ks[0]] = tr
                        nrm_rel[ks[1]] = tr
                        ewait(nc.scalar, s_pe, tr)
                        cc = s_act.inc(nc.scalar.copy(qT[:, q0:q0 + QBS], ptr[:, :]))
                        release_banks(ptr, s_act, cc)
                        return cc

                    NQUAD = NSTR // 2
                    ewait(nc.tensor, s_dve, ones_fix)
                    issue_sT2(0)
                    pend = []
                    ctxT_ready = 0
                    for qi in range(NQUAD + 1):
                        if qi + 1 < NQUAD:
                            issue_sT2(qi + 1)
                        if qi < 1:
                            continue
                        qc = qi - 1
                        for i in (2 * qc, 2 * qc + 1):
                            bi, hp, qb, kc = strips[i]
                            last_pe = issue_ctx(i)
                            if kc == 2 * qb + 1:
                                pend.append(tail_dve(bi, last_pe))
                                if len(pend) > 1:
                                    ctxT_ready = tail_pe(*pend.pop(0))
                    while pend:
                        ctxT_ready = tail_pe(*pend.pop(0))
                ctxT = qT

                widx = 6 + seq
                wslot = widx % 2
                with nc.psum_tensor(f"ps_o0_{seq}", [128, 1024], F32) as o0, \
                     nc.psum_tensor(f"ps_o1_{seq}", [128, 1024], F32) as o1, \
                     nc.psum_tensor(f"ps_tr2_{seq}", [128, 1024], BF16) as tr2, \
                     nc.psum_tensor(f"ps_tr3_{seq}", [128, 1024], BF16) as tr3, \
                     nc.psum_tensor(f"ps_xr0_{seq}", [16, 128], F32) as xr0, \
                     nc.psum_tensor(f"ps_xr1_{seq}", [16, 128], F32) as xr1:
                    ps_o = [o0, o1]
                    ps_tr = [tr2, tr3]
                    ps_xr = [xr0, xr1]
                    wait_w(nc.tensor, widx)
                    ewait(nc.tensor, s_act, ctxT_ready)
                    ewait(nc.sync, s_act, ctxT_ready)
                    LAGC = 3
                    r2_of = {}
                    for step in range(NTT + LAGC):
                        if step < NTT:
                            tt = step
                            slot = tt % 4
                            if tt == 0:
                                for i in range(3):
                                    load_x(seq, i, i)
                            if tt + 3 < NTT:
                                load_x(seq, tt + 3, (tt + 3) % 4)
                            pb = ps_o[tt % 2]
                            acquire_banks(pb)
                            for dc in range(NDC):
                                for dh in range(2):
                                    mm = nc.tensor.matmul(
                                        pb[:, dh * 512:(dh + 1) * 512],
                                        ctxT[:, dc * SL + tt * 128: dc * SL + tt * 128 + 128],
                                        wsb[wslot][:, dc * 1024 + dh * 512: dc * 1024 + dh * 512 + 512],
                                        start=(dc == 0), stop=(dc == NDC - 1))
                            pe_c = s_pe.inc(mm)
                            if tt == NTT - 1:
                                w_last_use[wslot] = pe_c
                            ewait(nc.vector, s_pe, pe_c)
                            xs, xv = x_done[(seq, tt)]
                            ewait(nc.vector, s_x[xs], xv)
                            if x2_rel[slot]:
                                ewait(nc.vector, s_x2w[slot], x2_rel[slot])
                            if x2_reader[slot]:
                                ewait(nc.vector, s_dve, x2_reader[slot])
                            rs = s_dve.inc(nc.vector.tensor_tensor(x2_sb[slot], pb[:, :], x_tile[slot], op=ALU.add))
                            x_free[slot] = rs
                            release_banks(pb, s_dve, rs)
                            row0 = seq * SL + tt * 128
                            ewait(nc.sync, s_dve, rs)
                            x2_rel[slot] = s_x2w[slot].inc(nc.sync.dma_start(x2s[row0:row0 + 128, :], x2_sb[slot]), 16)
                            if xln_rel[slot]:
                                ewait(nc.vector, s_pe, xln_rel[slot])
                            r2_of[tt] = ln_tile(slot, x2_sb[slot], xln_t[slot])
                            x2_reader[slot] = r2_of[tt]
                        if step >= LAGC:
                            tt = step - LAGC
                            slot = tt % 4
                            pt = ps_tr[tt % 2]
                            acquire_banks(pt)
                            ewait(nc.tensor, s_dve, r2_of[tt])
                            for i in range(8):
                                mm = nc.tensor.transpose(pt[:, i * 128:(i + 1) * 128], xln_t[slot][:, i * 128:(i + 1) * 128], ident[:, :])
                            xln_rel[slot] = s_pe.inc(mm)
                            ewait(nc.scalar, s_pe, xln_rel[slot])
                            if tr_free[slot]:
                                ewait(nc.scalar, s_xl2[slot], tr_free[slot])
                            if tr_read[slot]:
                                ewait(nc.scalar, s_pe, tr_read[slot])
                            ev = s_act.inc(nc.scalar.copy(trbuf[slot].rearrange("p (dc t) -> p dc t", dc=NDC),
                                                          pt.ap().rearrange("p (dc t) -> p dc t", dc=NDC)))
                            release_banks(pt, s_act, ev)
                            ewait(nc.sync, s_act, ev)
                            t0c = seq * SL + tt * 128
                            tr_free[slot] = s_xl2[slot].inc(nc.sync.dma_start(
                                xl2s[:, :, t0c:t0c + 128].rearrange("dc p t -> p dc t"),
                                trbuf[slot].rearrange("p (dc t) -> p dc t", dc=NDC)), 16)
                            pxr = ps_xr[tt % 2]
                            acquire_banks(pxr)
                            ewait(nc.tensor, s_act, ev)
                            ewait(nc.tensor, s_pp, 16)
                            for dc in range(NDC):
                                mm = nc.tensor.matmul(
                                    pxr[:, :],
                                    wgr_sb[:, dc * 16:(dc + 1) * 16],
                                    trbuf[slot][:, dc * 128:(dc + 1) * 128],
                                    start=(dc == 0), stop=(dc == NDC - 1))
                            xr_pe = s_pe.inc(mm)
                            tr_read[slot] = xr_pe
                            ewait(nc.scalar, s_pe, xr_pe)
                            ev2 = s_act.inc(nc.scalar.copy(xrT[:, t0c:t0c + 128], pxr[:, :]))
                            release_banks(pxr, s_act, ev2)

            NGS = 4
            gmax = [A(nc.sbuf_tensor(f"gmax{i}", [128, 1], F32)) for i in range(NGS)]
            gnegm = [A(nc.sbuf_tensor(f"gnegm{i}", [128, 1], F32)) for i in range(NGS)]
            gexp = [A(nc.sbuf_tensor(f"gexp{i}", [128, 8], F32)) for i in range(NGS)]
            gsum = [A(nc.sbuf_tensor(f"gsum{i}", [128, 1], F32)) for i in range(NGS)]
            xr_ready = s_act.n
            with nc.psum_tensor("ps_lg0", [128, 8], F32) as lg0, \
                 nc.psum_tensor("ps_lg1", [128, 8], F32) as lg1, \
                 nc.psum_tensor("ps_lg2", [128, 8], F32) as lg2, \
                 nc.psum_tensor("ps_lg3", [128, 8], F32) as lg3, \
                 nc.psum_tensor("ps_oh0", [8, 128], F32) as poh0, \
                 nc.psum_tensor("ps_oh1", [8, 128], F32) as poh1:
                ps_lg = [lg0, lg1, lg2, lg3]
                ps_oh = [poh0, poh1]
                ewait(nc.tensor, s_act, xr_ready)
                ewait(nc.tensor, s_act, wgu_ready)
                a1_slot = [0] * NGS
                sm_slot = [0] * NGS
                for tt in range(32):
                    slot = tt % NGS
                    pb = ps_lg[slot]
                    acquire_banks(pb)
                    mm = nc.tensor.matmul(pb[:, :], xrT[:, tt * 128:(tt + 1) * 128], wguT[:, :], start=True, stop=True)
                    pe_c = s_pe.inc(mm)
                    ewait(nc.vector, s_pe, pe_c)
                    if a1_slot[slot]:
                        ewait(nc.vector, s_act, a1_slot[slot])
                    m1 = s_dve.inc(nc.vector.reduce_max(gmax[slot][:, :], pb[:, :], axis=AX.X))
                    ewait(nc.scalar, s_dve, m1)
                    a1 = s_act.inc(nc.scalar.mul(gnegm[slot][:, :], gmax[slot][:, :], -1.0))
                    a1_slot[slot] = a1
                    ewait(nc.scalar, s_act, a1)
                    if sm_slot[slot]:
                        ewait(nc.scalar, s_dve, sm_slot[slot])
                    a2 = s_act.inc(nc.scalar.activation(gexp[slot][:, :], pb[:, :], AF.Exp, bias=gnegm[slot][:, :]))
                    ewait(nc.vector, s_act, a2)
                    oh1 = s_dve.inc(nc.vector.tensor_scalar(oh_sb[:, tt * 8:(tt + 1) * 8], pb[:, :], gmax[slot][:, :], None, op0=ALU.is_equal))
                    release_banks(pb, s_dve, oh1)
                    po = ps_oh[tt % 2]
                    acquire_banks(po)
                    ewait(nc.tensor, s_dve, oh1)
                    trh = s_pe.inc(nc.tensor.transpose(po[:, :], oh_sb[:, tt * 8:(tt + 1) * 8], ident32[:, :]))
                    ewait(nc.scalar, s_pe, trh)
                    ohc = s_act.inc(nc.scalar.copy(ohT_sb[0:8, tt * 128:(tt + 1) * 128], po[:, :]))
                    release_banks(po, s_act, ohc)
                    sm = s_dve.inc(nc.vector.reduce_sum(gsum[slot][:, :], gexp[slot][:, :], axis=AX.X))
                    sm_slot[slot] = sm
                    ewait(nc.vector, s_dve, sm)
                    s_dve.inc(nc.vector.reciprocal(sc_sb[:, tt:tt + 1], gsum[slot][:, :]))
            ohT_ready = s_act.n
            gating_done = s_dve.n
            act_fin = s_act.n
            ewait(nc.tensor, s_dve, gating_done)
            ewait(nc.scalar, s_dve, gating_done)
            ewait(nc.sync, s_st, s_st.n)
            for i_ in range(4):
                ewait(nc.sync, s_xl2[i_], s_xl2[i_].n)
                ewait(nc.sync, s_x2w[i_], s_x2w[i_].n)
            ewait(nc.sync, s_dve, gating_done)
            ewait(nc.sync, s_act, act_fin)
            ewait(nc.gpsimd, s_dve, gating_done)
            ewait(nc.gpsimd, s_act, act_fin)

        with ExitStack() as fst:
            F = fst.enter_context
            xbuf = [F(nc.sbuf_tensor(f"xbuf{i}", [128, NDC * TCH], BF16)) for i in range(2)]
            hT = F(nc.sbuf_tensor("hT", [128, NHC * TCH], BF16))
            w1s = [F(nc.sbuf_tensor(f"w1s{i}", [128, 4096], BF16)) for i in range(3)]
            w2s = [F(nc.sbuf_tensor(f"w2s{i}", [128, 4096], BF16)) for i in range(3)]
            acc = F(nc.sbuf_tensor("acc", [128, NSUB * 1024], F32))
            b1sb = F(nc.sbuf_tensor("b1sb", [128, E * 32], F32))

            x2b = [F(nc.sbuf_tensor(f"x2b{i}", [128, 1024], F32)) for i in range(NSUB)]
            osb = [F(nc.sbuf_tensor(f"osb{i}", [128, 1024], F32)) for i in range(NSUB)]

            s_ld.inc(nc.sync.dma_start(b1sb.ap().rearrange("p (e c) -> p e c", e=E), b1r.rearrange("e p c -> p e c")), 16)
            s_pp.inc(nc.gpsimd.dma_start(b2a_sb[:, :], b2f.rearrange("o (e d) -> (o e) d", e=E)), 16)
            const_ld = s_ld.n
            const_pp = s_pp.n

            xln_done = [0] * NTC
            def load_xln(tc):
                src = xl2s[:, :, tc * TCH:(tc + 1) * TCH].rearrange("dc p t -> p dc t")
                dst = xbuf[tc % 2].ap().rearrange("p (dc t) -> p dc t", dc=NDC)
                xln_done[tc] = s_xb[tc % 2].inc(nc.sync.dma_start(dst, src), 16)
            load_xln(0)

            xln_last_use = [0] * NTC
            w1_done = {}
            w1_last_use = {}
            w2_done = {}
            w2_last_use = {}
            gelu_done = [0] * NHC
            last_stt = 0
            acc_writer = [0] * NSUB
            x2b_free = [0] * NSUB
            osb_store_done = [0] * NSUB
            pending_stores = []
            x2_done = {}
            w1_issued = set()
            w2_issued = set()
            w1_slot_of = {}
            w2_slot_of = {}
            w1_ctr = [0]
            w2_ctr = [0]

            def issue_w1(tc_, e_, g_):
                if (tc_, e_, g_) in w1_issued or tc_ >= NTC:
                    return
                w1_issued.add((tc_, e_, g_))
                slot = w1_ctr[0] % 3
                w1_ctr[0] += 1
                w1_slot_of[(tc_, e_, g_)] = slot
                prev_use = w1_last_use.get(slot, 0)
                if prev_use:
                    ewait(nc.sync, s_pe, prev_use)
                ewait(nc.sync, s_ppe[e_], pp_ready[('e', e_)])
                w1_done[(tc_, e_, g_)] = (slot, s_w1[slot].inc(nc.sync.dma_start(w1s[slot][:, :], w1b[e_, g_]), 16))

            def issue_w2(tc_, e_, j_):
                if (tc_, e_, j_) in w2_issued or tc_ >= NTC:
                    return
                w2_issued.add((tc_, e_, j_))
                slot = w2_ctr[0] % 3
                w2_ctr[0] += 1
                w2_slot_of[(tc_, e_, j_)] = slot
                prev_use = w2_last_use.get(slot, 0)
                if prev_use:
                    ewait(nc.gpsimd, s_pe, prev_use)
                ewait(nc.gpsimd, s_ppe[e_], pp_ready[('e', e_)])
                src = w2b[e_, 4 * j_:4 * j_ + 4].rearrange("h p d -> p h d")
                dst = w2s[slot].ap().rearrange("p (h d) -> p h d", h=4)
                w2_done[(tc_, e_, j_)] = (slot, s_w2[slot].inc(nc.gpsimd.dma_start(dst, src), 16))

            def flush_stores():
                while pending_stores:
                    tc_, tsub_, w = pending_stores.pop(0)
                    row0 = tc_ * TCH + tsub_ * 128
                    ewait(nc.sync, s_dve, w)
                    osb_store_done[tsub_] = s_st.inc(nc.sync.dma_start(out[row0:row0 + 128, :], osb[tsub_][:, :]), 16)

            first_e_mm = True
            ewait(nc.tensor, s_ld, const_ld)
            ewait(nc.tensor, s_pp, const_pp)
            ewait(nc.vector, s_ld, const_ld)
            ewait(nc.scalar, s_ld, const_ld)

            for tc in range(NTC):
                if tc + 1 < NTC:
                    if tc >= 1:
                        ewait(nc.sync, s_pe, xln_last_use[tc - 1])
                    load_xln(tc + 1)
                xb = xbuf[tc % 2]
                acc_init = [0] * NSUB
                with nc.psum_tensor(f"ps_b2a_{tc}", [128, 1024], F32) as pb2a, \
                     nc.psum_tensor(f"ps_b2b_{tc}", [128, 1024], F32) as pb2b:
                    pb2l = [pb2a, pb2b]
                    ewait(nc.tensor, s_act, ohT_ready)
                    ewait(nc.tensor, s_pp, const_pp)
                    for tsub in range(NSUB):
                        pb = pb2l[tsub % 2]
                        acquire_banks(pb)
                        t0g = (tc * NSUB + tsub) * 128
                        for dh in range(2):
                            mm = nc.tensor.matmul(
                                pb[:, dh * 512:(dh + 1) * 512],
                                ohT_sb[0:8, t0g:t0g + 128],
                                b2a_sb[0:8, dh * 512:(dh + 1) * 512],
                                start=True, stop=True)
                        pe_b = s_pe.inc(mm)
                        ewait(nc.scalar, s_pe, pe_b)
                        if acc_writer[tsub]:
                            ewait(nc.scalar, s_dve, acc_writer[tsub])
                        ai = s_act.inc(nc.scalar.copy(acc[:, tsub * 1024:(tsub + 1) * 1024], pb[:, :]))
                        release_banks(pb, s_act, ai)
                        acc_init[tsub] = ai
                for e in range(E):
                    if e == 1:
                        flush_stores()
                    if e == E - 1:
                        for tsub in range(NSUB):
                            row0 = tc * TCH + tsub * 128
                            if x2b_free[tsub]:
                                ewait(nc.sync, s_dve, x2b_free[tsub])
                            x2_done[(tc, tsub)] = s_x2b.inc(nc.sync.dma_start(x2b[tsub][:, :], x2s[row0:row0 + 128, :]), 16)
                    with nc.psum_tensor(f"ps_f1a_{tc}_{e}", [128, 512], F32) as f1a, \
                         nc.psum_tensor(f"ps_f1b_{tc}_{e}", [128, 512], F32) as f1b:
                        ps_f1 = [f1a, f1b]
                        issue_w1(tc, e, 0)
                        issue_w1(tc, e, 1)
                        for g in range(NG):
                            slot = w1_slot_of[(tc, e, g)]
                            if g + 2 < NG:
                                issue_w1(tc, e, g + 2)
                            else:
                                ntc_ = tc + 1 if e == E - 1 else tc
                                issue_w1(ntc_, (e + 1) % E, g + 2 - NG)
                            w1s_, w1v_ = w1_done[(tc, e, g)]
                            ewait(nc.tensor, s_w1[w1s_], w1v_)
                            if g == 0:
                                ewait(nc.tensor, s_xb[tc % 2], xln_done[tc])
                            for hc in range(4 * g, 4 * g + 4):
                                pb = ps_f1[hc % 2]
                                acquire_banks(pb)
                                for dc in range(NDC):
                                    mm = nc.tensor.matmul(
                                        pb[:, :],
                                        w1s[slot][:, dc * 512 + (hc % 4) * 128: dc * 512 + (hc % 4) * 128 + 128],
                                        xb[:, dc * 512:(dc + 1) * 512],
                                        start=(dc == 0), stop=(dc == NDC - 1))
                                w1_last_use[slot] = s_pe.inc(mm)
                                if hc == NHC - 1:
                                    xln_last_use[tc] = w1_last_use[slot]
                                ewait(nc.scalar, s_pe, w1_last_use[slot])
                                g_i = nc.scalar.activation(
                                    hT[:, hc * TCH:(hc + 1) * TCH], pb[:, :], AF.Gelu,
                                    bias=b1sb[:, e * 32 + hc: e * 32 + hc + 1])
                                gelu_done[hc] = s_act.inc(g_i)
                                release_banks(pb, s_act, gelu_done[hc])
                        first_e_mm = False
                    with nc.psum_tensor(f"ps_f2a_{tc}_{e}", [128, 1024], F32) as f2a, \
                         nc.psum_tensor(f"ps_f2b_{tc}_{e}", [128, 1024], F32) as f2b, \
                         nc.psum_tensor(f"ps_f2c_{tc}_{e}", [128, 1024], F32) as f2c, \
                         nc.psum_tensor(f"ps_f2d_{tc}_{e}", [128, 1024], F32) as f2d:
                        ps_f2 = [f2a, f2b, f2c, f2d]
                        f2_stop = [0] * NSUB
                        issue_w2(tc, e, 0)
                        issue_w2(tc, e, 1)
                        for j in range(NHC // 4):
                            slot = w2_slot_of[(tc, e, j)]
                            if j + 2 < NHC // 4:
                                issue_w2(tc, e, j + 2)
                            else:
                                ntc_ = tc + 1 if e == E - 1 else tc
                                issue_w2(ntc_, (e + 1) % E, j + 2 - NHC // 4)
                            w2s_, w2v_ = w2_done[(tc, e, j)]
                            ewait(nc.tensor, s_w2[w2s_], w2v_)
                            for hc in range(4 * j, 4 * j + 4):
                                for tsub in range(NSUB):
                                    if hc == 0:
                                        acquire_banks(ps_f2[tsub])
                                    for dh in range(2):
                                        mm = nc.tensor.matmul(
                                            ps_f2[tsub][:, dh * 512:(dh + 1) * 512],
                                            hT[:, hc * TCH + tsub * 128: hc * TCH + tsub * 128 + 128],
                                            w2s[slot][:, (hc % 4) * 1024 + dh * 512: (hc % 4) * 1024 + (dh + 1) * 512],
                                            start=(hc == 0), stop=(hc == NHC - 1 and dh == 1))
                                    if hc == NHC - 1:
                                        f2_stop[tsub] = s_pe.inc(mm)
                            if f2_stop[NSUB - 1]:
                                w2_last_use[slot] = f2_stop[NSUB - 1] if 4 * j + 3 == NHC - 1 else s_pe.inc(mm)
                            else:
                                w2_last_use[slot] = s_pe.inc(mm)
                        for tsub in range(NSUB):
                            ewait(nc.vector, s_pe, f2_stop[tsub])
                            mcol = oh_sb[:, (tc * NSUB + tsub) * E + e: (tc * NSUB + tsub) * E + e + 1]
                            if e == 0:
                                ewait(nc.vector, s_act, acc_init[tsub])
                            halves = ((0, 512), (512, 1024)) if tsub == 0 else ((0, 1024),)
                            for (c0, c1) in halves:
                                a_sl = acc[:, tsub * 1024 + c0: tsub * 1024 + c1]
                                i = nc.vector.scalar_tensor_tensor(a_sl, ps_f2[tsub][:, c0:c1], mcol, a_sl, op0=ALU.mult, op1=ALU.add)
                                w = s_dve.inc(i)
                                release_bank_cols(ps_f2[tsub], c0, c1, 4, s_dve, w)
                            acc_writer[tsub] = w
                        last_stt = acc_writer[NSUB - 1]
                for tsub in range(NSUB):
                    ewait(nc.vector, s_x2b, x2_done[(tc, NSUB - 1)])
                    ewait(nc.vector, s_dve, acc_writer[tsub])
                    if osb_store_done[tsub]:
                        ewait(nc.vector, s_st, osb_store_done[tsub])
                    sc = sc_sb[:, tc * NSUB + tsub: tc * NSUB + tsub + 1]
                    i = nc.vector.scalar_tensor_tensor(osb[tsub][:, :], acc[:, tsub * 1024:(tsub + 1) * 1024], sc, x2b[tsub][:, :], op0=ALU.mult, op1=ALU.add)
                    w = s_dve.inc(i)
                    pending_stores.append((tc, tsub, w))
                    acc_writer[tsub] = w
                    x2b_free[tsub] = w
            flush_stores()
            nc.sync.wait_ge(s_st.h, s_st.n)
    return nc


_NC_CACHE = {}

def _get_nc():
    if "nc" not in _NC_CACHE:
        _NC_CACHE["nc"] = build()
    return _NC_CACHE["nc"]


def kernel(L0_attn_x, L0_attn_W_Q, L0_attn_W_K, L0_attn_W_V, L0_attn_W_O,
           L0_ffn_wg_red, L0_ffn_wg, L0_ffn_w1, L0_ffn_b1, L0_ffn_w2, L0_ffn_b2):
    x = np.asarray(L0_attn_x, dtype=np.float32)
    WQ = np.asarray(L0_attn_W_Q, dtype=np.float32)
    WK = np.asarray(L0_attn_W_K, dtype=np.float32)
    WV = np.asarray(L0_attn_W_V, dtype=np.float32)
    WO = np.asarray(L0_attn_W_O, dtype=np.float32)
    wg_red = np.asarray(L0_ffn_wg_red, dtype=np.float32)
    wg = np.asarray(L0_ffn_wg, dtype=np.float32)
    w1 = np.ascontiguousarray(np.asarray(L0_ffn_w1, dtype=np.float32))
    b1 = np.asarray(L0_ffn_b1, dtype=np.float32)
    w2 = np.ascontiguousarray(np.asarray(L0_ffn_w2, dtype=np.float32))
    b2 = np.asarray(L0_ffn_b2, dtype=np.float32)

    wqt = np.ascontiguousarray(WQ.T)
    wkt = np.ascontiguousarray(WK.T)
    wvt = np.ascontiguousarray(WV.T)
    wot = np.ascontiguousarray(WO.T)
    wgr = np.ascontiguousarray(wg_red.T)
    b1r = np.ascontiguousarray(b1.reshape(E, 32, 128).transpose(0, 2, 1))
    b2f = np.ascontiguousarray(b2.reshape(1, E * D))

    nc = _get_nc()
    shared = dict(wqt=wqt, wkt=wkt, wvt=wvt, wot=wot, wgr=wgr, wg=wg,
                  w1=w1, b1r=b1r, w2=w2, b2f=b2f)
    in_maps = []
    for c in range(8):
        shard = np.ascontiguousarray(x[2 * c:2 * c + 2].reshape(T, D))
        m = dict(shared)
        m["x_in"] = shard
        in_maps.append(m)
    res = run_bass_kernel_spmd(nc, in_maps, core_ids=list(range(8)))
    outs = [np.asarray(res.results[c]["out"], dtype=np.float32).reshape(2, SL, D) for c in range(8)]
    return np.concatenate(outs, axis=0)


if __name__ == "__main__":
    import importlib.util
    spec = importlib.util.spec_from_file_location("_ref", "/root/problem/reference.py")
    ref = importlib.util.module_from_spec(spec)
    spec.loader.exec_module(ref)
    inputs = {k: np.asarray(v) for k, v in ref.setup_inputs().items()}
    got = kernel(**inputs)
    want = np.asarray(ref.reference(**inputs))
    err = got - want
    ratio = float((err ** 2).mean() / (want ** 2).mean())
    print("residual variance ratio:", ratio, "(bar 2.27e-04)")
    print("PASS" if ratio < 2.27e-4 else "FAIL")


# version 25
# speedup vs baseline: 1.0387x; 1.0124x over previous
import numpy as np
from contextlib import ExitStack
import concourse.bass as bass
import concourse.mybir as mybir
from concourse.bass_utils import run_bass_kernel_spmd

F32 = mybir.dt.float32
BF16 = mybir.dt.bfloat16
I32 = mybir.dt.int32
AF = mybir.ActivationFunctionType
ALU = mybir.AluOpType
AX = mybir.AxisListType

NSEQ, SL, D, NH, DK = 2, 2048, 1024, 16, 64
T = NSEQ * SL
NDC = D // 128
NTT = SL // 128
NQB = SL // 512
NKC = SL // 128
E = 8
HID = 4096
NHC = HID // 128
NG = 8
TCH = 512
NTC = T // TCH
NSUB = TCH // 128


class Sem:
    def __init__(self, stack, nc, name):
        self.h = stack.enter_context(nc.semaphore(name))
        self.n = 0
    def inc(self, instr, k=1):
        instr.then_inc(self.h, k)
        self.n += k
        return self.n


def build():
    nc = bass.Bass("TRN2", target_bir_lowering=False, debug=False)
    x_in = nc.dram_tensor("x_in", [T, D], F32, kind="ExternalInput").ap()
    wqt = nc.dram_tensor("wqt", [D, D], F32, kind="ExternalInput").ap()
    wkt = nc.dram_tensor("wkt", [D, D], F32, kind="ExternalInput").ap()
    wvt = nc.dram_tensor("wvt", [D, D], F32, kind="ExternalInput").ap()
    wot = nc.dram_tensor("wot", [D, D], F32, kind="ExternalInput").ap()
    wgr = nc.dram_tensor("wgr", [D, 16], F32, kind="ExternalInput").ap()
    wg = nc.dram_tensor("wg", [8, 16], F32, kind="ExternalInput").ap()
    w1 = nc.dram_tensor("w1", [E, D, HID], F32, kind="ExternalInput").ap()
    b1r = nc.dram_tensor("b1r", [E, 128, 32], F32, kind="ExternalInput").ap()
    w2 = nc.dram_tensor("w2", [E, HID, D], F32, kind="ExternalInput").ap()
    b2f = nc.dram_tensor("b2f", [1, E * D], F32, kind="ExternalInput").ap()
    out = nc.dram_tensor("out", [T, D], F32, kind="ExternalOutput").ap()
    x2s = nc.dram_tensor("x2s", [T, D], F32, kind="Internal").ap()
    xl2s = nc.dram_tensor("xl2s", [NDC, 128, T], BF16, kind="Internal").ap()
    wab = nc.dram_tensor("wab", [4, 128, NDC * 1024], BF16, kind="Internal").ap()
    w1b = nc.dram_tensor("w1b", [E, NG, 128, 4096], BF16, kind="Internal").ap()
    w2b = nc.dram_tensor("w2b", [E, NHC, 128, 1024], BF16, kind="Internal").ap()

    with ExitStack() as st:
        S = st.enter_context
        ident = S(nc.sbuf_tensor("ident", [128, 128], BF16))
        ident32 = S(nc.sbuf_tensor("ident32", [128, 128], F32))
        onesb = S(nc.sbuf_tensor("onesb", [1, 128], BF16))
        eps_col = S(nc.sbuf_tensor("eps_col", [128, 1], F32))
        neg12_col = S(nc.sbuf_tensor("neg12_col", [128, 1], F32))
        wgr_sb = S(nc.sbuf_tensor("wgr_sb", [128, NDC * 16], BF16))
        wg_sb = S(nc.sbuf_tensor("wg_sb", [8, 16], F32))
        wgu_bf = S(nc.sbuf_tensor("wgu_bf", [8, 16], BF16))
        wguT = S(nc.sbuf_tensor("wguT", [16, 8], BF16))
        xrT = S(nc.sbuf_tensor("xrT", [16, T], BF16))
        oh_sb = S(nc.sbuf_tensor("oh_sb", [128, 32 * E], F32))
        ohT_sb = S(nc.sbuf_tensor("ohT_sb", [8, T], BF16))
        b2a_sb = S(nc.sbuf_tensor("b2a_sb", [8, 1024], BF16))
        sc_sb = S(nc.sbuf_tensor("sc_sb", [128, 32], F32))
        sq = S(nc.sbuf_tensor("sq", [8, 16], F32))
        nrm = S(nc.sbuf_tensor("nrm", [8, 1], F32))
        nrm2 = S(nc.sbuf_tensor("nrm2", [8, 1], F32))
        rn = S(nc.sbuf_tensor("rn", [8, 1], F32))

        s_ld = Sem(st, nc, "s_ld")
        s_x = [Sem(st, nc, f"s_x{i}") for i in range(4)]
        s_w = [Sem(st, nc, f"s_w{i}") for i in range(2)]
        s_w1 = [Sem(st, nc, f"s_w1{i}") for i in range(3)]
        s_w2 = [Sem(st, nc, f"s_w2{i}") for i in range(3)]
        s_xb = [Sem(st, nc, f"s_xb{i}") for i in range(2)]
        s_x2b = Sem(st, nc, "s_x2b")
        s_ppe = [Sem(st, nc, f"s_ppe{i}") for i in range(E)]
        s_ppwa = Sem(st, nc, "s_ppwa")
        s_xl2 = [Sem(st, nc, f"s_xl2{i}") for i in range(4)]
        s_x2w = [Sem(st, nc, f"s_x2w{i}") for i in range(4)]
        s_pp = Sem(st, nc, "s_pp")
        s_pps = Sem(st, nc, "s_pps")
        s_ldw = Sem(st, nc, "s_ldw")
        s_pe = Sem(st, nc, "s_pe")
        s_act = Sem(st, nc, "s_act")
        s_dve = Sem(st, nc, "s_dve")
        s_st = Sem(st, nc, "s_st")
        s_gp = Sem(st, nc, "s_gp")

        wait_hwm = {}
        def ewait(eng, sem, val):
            if val <= 0:
                return
            hw = wait_hwm.setdefault(id(eng), {})
            if hw.get(id(sem.h), 0) >= val:
                return
            eng.wait_ge(sem.h, val)
            hw[id(sem.h)] = val
        bank_release = {}
        def nbanks(h):
            return max(1, (int(h.ap().free_nbytes()) + 2047) // 2048)
        def acquire_banks(h):
            m = nc.lookup_mloc(h)
            for b in range(m.bank, m.bank + nbanks(h)):
                r = bank_release.get(b)
                if r is not None:
                    ewait(nc.tensor, r[0], r[1])
        def release_banks(h, sem, val):
            m = nc.lookup_mloc(h)
            for b in range(m.bank, m.bank + nbanks(h)):
                bank_release[b] = (sem, val)
        def acquire_bank_cols(h, c0, c1, elt):
            m = nc.lookup_mloc(h)
            b0 = m.bank + (c0 * elt) // 2048
            b1 = m.bank + ((c1 * elt) - 1) // 2048
            for b in range(b0, b1 + 1):
                r = bank_release.get(b)
                if r is not None:
                    ewait(nc.tensor, r[0], r[1])
        def release_bank_cols(h, c0, c1, elt, sem, val):
            m = nc.lookup_mloc(h)
            b0 = m.bank + (c0 * elt) // 2048
            b1 = m.bank + ((c1 * elt) - 1) // 2048
            for b in range(b0, b1 + 1):
                bank_release[b] = (sem, val)

        with ExitStack() as ast_:
            A = ast_.enter_context
            bmask = A(nc.sbuf_tensor("bmask", [128, 896], BF16))
            xlnT = A(nc.sbuf_tensor("xlnT", [128, NDC * SL], BF16))
            qT = A(nc.sbuf_tensor("qT", [128, NDC * SL], BF16))
            kT = A(nc.sbuf_tensor("kT", [128, NDC * SL], BF16))
            vst = A(nc.sbuf_tensor("vst", [128, NKC * NH * 65], BF16))
            wsb = [A(nc.sbuf_tensor(f"wsb{i}", [128, NDC * 1024], BF16)) for i in range(2)]
            st6 = [A(nc.sbuf_tensor(f"st6{i}", [128, 12], F32)) for i in range(4)]
            mv = [A(nc.sbuf_tensor(f"mv{i}", [128, 2], F32)) for i in range(4)]
            stdv = [A(nc.sbuf_tensor(f"stdv{i}", [128, 1], F32)) for i in range(4)]
            negm = [A(nc.sbuf_tensor(f"negm{i}", [128, 1], F32)) for i in range(4)]
            rstd = [A(nc.sbuf_tensor(f"rstd{i}", [128, 1], F32)) for i in range(4)]
            x_tile_h = [A(nc.sbuf_tensor(f"x_tile{i}", [128, D], F32)) for i in range(2)]
            xln_t_h = [A(nc.sbuf_tensor(f"xln_t{i}", [128, D], BF16)) for i in range(2)]
            x_tile = [x_tile_h[0][:, :], x_tile_h[1][:, :],
                      vst[:, 2048:4096].bitcast(F32), vst[:, 4096:6144].bitcast(F32)]
            xln_t = [xln_t_h[0][:, :], xln_t_h[1][:, :],
                     vst[:, 0:1024], vst[:, 1024:2048]]
            x2_sb = [kT[:, i * 2048:(i + 1) * 2048].bitcast(F32) for i in range(4)]
            trbuf = [kT[:, 8192 + i * 1024: 8192 + (i + 1) * 1024] for i in range(4)]

            with nc.sbuf_tensor("iota32", [128, 448], I32) as iota32:
                g1 = s_gp.inc(nc.gpsimd.iota(iota32[:, :], [[1, 448]], base=-384, channel_multiplier=-1))
                ewait(nc.vector, s_gp, g1)
                nc.vector.tensor_scalar(bmask[:, 0:448], iota32[:, :], 0, None, op0=ALU.is_ge)
                nc.vector.tensor_scalar(ident32[:, 0:64], iota32[:, 384:448], 0, None, op0=ALU.is_equal)
                d0 = s_dve.inc(nc.vector.tensor_scalar(ident[:, 0:64], iota32[:, 384:448], 0, None, op0=ALU.is_equal))
                ewait(nc.gpsimd, s_dve, d0)
                g2 = s_gp.inc(nc.gpsimd.iota(iota32[:, :], [[1, 448]], base=64, channel_multiplier=-1))
                ewait(nc.vector, s_gp, g2)
                nc.vector.tensor_scalar(bmask[:, 448:896], iota32[:, :], 0, None, op0=ALU.is_ge)
                nc.vector.tensor_scalar(ident32[:, 64:128], iota32[:, 0:64], 0, None, op0=ALU.is_equal)
                d1 = s_dve.inc(nc.vector.tensor_scalar(ident[:, 64:128], iota32[:, 0:64], 0, None, op0=ALU.is_equal))
                nc.vector.memset(onesb[:, :], 1.0)
                nc.vector.memset(eps_col[:, :], 1e-5)
                nc.vector.memset(neg12_col[:, :], -12.0)
                vst4 = vst.ap().rearrange("p (kc h c) -> p kc h c", kc=NKC, h=NH)
                setup_dve = s_dve.inc(nc.vector.engine_nop())

                s_ld.inc(nc.sync.dma_start(wg_sb[:, :], wg), 16)
                s_pp.inc(nc.gpsimd.dma_start(wgr_sb.ap().rearrange("p (dc c) -> p dc c", dc=NDC),
                                             wgr.rearrange("(dc p) c -> p dc c", p=128)), 16)
                ewait(nc.scalar, s_ld, s_ld.n)
                a = s_act.inc(nc.scalar.activation(sq[:, :], wg_sb[:, :], AF.Square))
                ewait(nc.vector, s_act, a)
                v1 = s_dve.inc(nc.vector.reduce_sum(nrm[:, :], sq[:, :], axis=AX.X))
                ewait(nc.scalar, s_dve, v1)
                a = s_act.inc(nc.scalar.activation(nrm2[:, :], nrm[:, :], AF.Sqrt))
                ewait(nc.vector, s_act, a)
                v2 = s_dve.inc(nc.vector.tensor_scalar(nrm2[:, :], nrm2[:, :], 1e-4, None, op0=ALU.max))
                ewait(nc.vector, s_dve, v2)
                v3 = s_dve.inc(nc.vector.reciprocal(rn[:, :], nrm2[:, :]))
                ewait(nc.vector, s_dve, v3)
                v4 = s_dve.inc(nc.vector.tensor_scalar(wgu_bf[:, :], wg_sb[:, :], rn[:, :], None, op0=ALU.mult))
                with nc.psum_tensor("ps_wgu", [16, 8], BF16) as ps_wgu:
                    ewait(nc.tensor, s_dve, v4)
                    ewait(nc.tensor, s_dve, d1)
                    t1 = s_pe.inc(nc.tensor.transpose(ps_wgu[:, :], wgu_bf[:, :], ident[0:8, 0:8]))
                    ewait(nc.scalar, s_pe, t1)
                    wgu_ready = s_act.inc(nc.scalar.copy(wguT[:, :], ps_wgu[:, :]))

            pp_ready = {}
            def run_prepass():
                for widx, wsrc in enumerate((wqt, wkt, wvt, wot)):
                    src = wsrc.rearrange("(dc p) o -> p dc o", p=128)
                    dst = wab[widx].rearrange("p (dc o) -> p dc o", dc=NDC)
                    s_ppwa.inc(nc.gpsimd.dma_start(dst, src), 16)
                pp_ready['wa'] = s_ppwa.n
                for e_ in range(E):
                    for g in range(NG):
                        src = w1[e_][:, g * 512:(g + 1) * 512].rearrange("(dc p) h -> p dc h", p=128)
                        dst = w1b[e_, g].rearrange("p (dc h) -> p dc h", dc=NDC)
                        s_ppe[e_].inc(nc.gpsimd.dma_start(dst, src), 16)
                    for j in range(NG):
                        src = w2[e_][j * 512:(j + 1) * 512, :].rearrange("(hc p) d -> p hc d", p=128)
                        dst = w2b[e_, 4 * j:4 * j + 4].rearrange("hc p d -> p hc d")
                        s_ppe[e_].inc(nc.gpsimd.dma_start(dst, src), 16)
                    pp_ready[('e', e_)] = s_ppe[e_].n
            run_prepass()

            w_done = {}
            def wait_w(eng, idx):
                ewait(eng, s_w[idx % 2], w_done[idx])
            w_last_use = {}
            def load_w(idx, widx):
                slot = idx % 2
                if w_last_use.get(slot):
                    ewait(nc.sync, s_pe, w_last_use[slot])
                ewait(nc.sync, s_ppwa, pp_ready['wa'])
                w_done[idx] = s_w[slot].inc(nc.sync.dma_start(wsb[slot][:, :], wab[widx]), 16)

            x_done = {}
            x_free = [0] * 4
            def load_x(seq, tt, slot):
                row0 = seq * SL + tt * 128
                if x_free[slot]:
                    ewait(nc.sync, s_dve, x_free[slot])
                x_done[(seq, tt)] = (slot, s_x[slot].inc(nc.sync.dma_start(x_tile[slot], x_in[row0:row0 + 128, :]), 16))

            def ln_tile(slot, src_sb, dst_bf):
                nc.vector.bn_stats(st6[slot][:, 0:6], src_sb[:, 0:512])
                b1_ = s_dve.inc(nc.vector.bn_stats(st6[slot][:, 6:12], src_sb[:, 512:1024]))
                ewait(nc.vector, s_dve, b1_)
                b2_ = s_dve.inc(nc.vector.bn_aggr(mv[slot][:, :], st6[slot][:, :]))
                ewait(nc.scalar, s_dve, b2_)
                s_act.inc(nc.scalar.activation(stdv[slot][:, :], mv[slot][:, 1:2], AF.Sqrt, bias=eps_col[:, :]))
                a2 = s_act.inc(nc.scalar.mul(negm[slot][:, :], mv[slot][:, 0:1], -1.0))
                ewait(nc.vector, s_act, a2)
                r1 = s_dve.inc(nc.vector.reciprocal(rstd[slot][:, :], stdv[slot][:, :]))
                ewait(nc.vector, s_dve, r1)
                r2 = s_dve.inc(nc.vector.tensor_scalar(
                    dst_bf, src_sb, negm[slot][:, :], rstd[slot][:, :],
                    op0=ALU.add, op1=ALU.mult))
                return r2

            xln_rel = [0] * 4
            x2_rel = [0] * 4
            x2_reader = [0] * 4
            tr_free = [0] * 4
            tr_read = [0] * 4

            for seq in range(NSEQ):
                for i_ in range(4):
                    if tr_free[i_]:
                        ewait(nc.scalar, s_xl2[i_], tr_free[i_])
                    if x2_rel[i_]:
                        ewait(nc.scalar, s_x2w[i_], x2_rel[i_])
                load_w(3 * seq + 0, 0)
                load_w(3 * seq + 1, 1)
                with nc.psum_tensor(f"ps_tr0_{seq}", [128, 1024], BF16) as tr0, \
                     nc.psum_tensor(f"ps_tr1_{seq}", [128, 1024], BF16) as tr1, \
                     nc.psum_tensor(f"ps_qk0_{seq}", [128, 512], F32) as qk0, \
                     nc.psum_tensor(f"ps_qk1_{seq}", [128, 512], F32) as qk1, \
                     nc.psum_tensor(f"ps_v0_{seq}", [128, 1024], F32) as pv0, \
                     nc.psum_tensor(f"ps_v1_{seq}", [128, 1024], F32) as pv1:
                    ps_tr = [tr0, tr1]
                    ps_qk = [qk0, qk1]
                    ps_v = [pv0, pv1]
                    for i in range(3):
                        load_x(seq, i, i % 4)
                    n = 0
                    for tb in range(4):
                        for tt in range(4 * tb, 4 * tb + 4):
                            slot = tt % 4
                            xs, xv = x_done[(seq, tt)]
                            ewait(nc.vector, s_x[xs], xv)
                            if xln_rel[slot]:
                                ewait(nc.vector, s_pe, xln_rel[slot])
                            r2 = ln_tile(slot, x_tile[slot], xln_t[slot])
                            x_free[slot] = r2
                            if tt + 3 < NTT:
                                load_x(seq, tt + 3, (tt + 3) % 4)
                            ewait(nc.tensor, s_dve, r2)
                            if seq == 0 and tt == 0:
                                ewait(nc.tensor, s_dve, setup_dve)
                            pb = ps_tr[tt % 2]
                            acquire_banks(pb)
                            for i in range(8):
                                mm = nc.tensor.transpose(pb[:, i * 128:(i + 1) * 128], xln_t[slot][:, i * 128:(i + 1) * 128], ident[:, :])
                            xln_rel[slot] = s_pe.inc(mm)
                            ewait(nc.scalar, s_pe, xln_rel[slot])
                            dst = xlnT.ap().rearrange("p (dc t) -> p dc t", dc=NDC)[:, :, tt * 128:(tt + 1) * 128]
                            src = pb.ap().rearrange("p (dc t) -> p dc t", dc=NDC)
                            ev = s_act.inc(nc.scalar.copy(dst, src))
                            release_banks(pb, s_act, ev)
                        col_ready = s_act.n
                        for wi, dstT in ((0, qT), (1, kT)):
                            widx = 3 * seq + wi
                            wslot = widx % 2
                            wait_w(nc.tensor, widx)
                            ewait(nc.tensor, s_act, col_ready)
                            for oc in range(8):
                                pb = ps_qk[n % 2]
                                n += 1
                                acquire_banks(pb)
                                for dc in range(NDC):
                                    mm = nc.tensor.matmul(
                                        pb[:, :],
                                        wsb[wslot][:, dc * 1024 + oc * 128: dc * 1024 + oc * 128 + 128],
                                        xlnT[:, dc * SL + tb * 512: dc * SL + tb * 512 + 512],
                                        start=(dc == 0), stop=(dc == NDC - 1))
                                pe_c = s_pe.inc(mm)
                                if oc == 7 and tb == 3:
                                    w_last_use[wslot] = pe_c
                                    if wi == 0:
                                        load_w(3 * seq + 2, 2)
                                ewait(nc.scalar, s_pe, pe_c)
                                ev = s_act.inc(nc.scalar.copy(dstT[:, oc * SL + tb * 512: oc * SL + tb * 512 + 512], pb[:, :]))
                                release_banks(pb, s_act, ev)
                    ewait(nc.vector, s_pe, xln_rel[2])
                    ewait(nc.vector, s_pe, xln_rel[3])
                    ones_fix = s_dve.inc(nc.vector.memset(vst4[:, :, :, 64:65], 1.0))
                    widx = 3 * seq + 2
                    wslot = widx % 2
                    wait_w(nc.tensor, widx)
                    for tt in range(NTT):
                        pb = ps_v[tt % 2]
                        acquire_banks(pb)
                        for dc in range(NDC):
                            for dh in range(2):
                                mm = nc.tensor.matmul(
                                    pb[:, dh * 512:(dh + 1) * 512],
                                    xlnT[:, dc * SL + tt * 128: dc * SL + tt * 128 + 128],
                                    wsb[wslot][:, dc * 1024 + dh * 512: dc * 1024 + dh * 512 + 512],
                                    start=(dc == 0), stop=(dc == NDC - 1))
                        pe_c = s_pe.inc(mm)
                        if tt == NTT - 1:
                            w_last_use[wslot] = pe_c
                        ewait(nc.scalar, s_pe, pe_c)
                        dst = vst4[:, tt, :, 0:64]
                        src = pb.ap().rearrange("p (h c) -> p h c", h=NH)
                        ev = s_act.inc(nc.scalar.copy(dst, src))
                        release_banks(pb, s_act, ev)

                load_w(6 + seq, 3)
                with nc.psum_tensor(f"ps_sc_{seq}", [128, 2048], F32) as ps_sc, \
                     nc.psum_tensor(f"ps_cx0_{seq}", [128, 260], F32) as pcx0, \
                     nc.psum_tensor(f"ps_cx1_{seq}", [128, 260], F32) as pcx1, \
                     nc.psum_tensor(f"ps_ctr_{seq}", [128, 256], BF16) as ptr:
                    ps_cx = [pcx0, pcx1]
                    pT = [xlnT[:, i * 1024:(i + 1) * 1024] for i in range(3)]
                    nrmc = [xlnT[:, 3072 + k * 128: 3072 + (k + 1) * 128] for k in range(4)]
                    rcol = [xlnT[:, 3584 + 2 * k: 3584 + 2 * k + 2].bitcast(F32) for k in range(8)]
                    QBS = 256
                    NQB2 = SL // QBS
                    blocks = [(hp, qb) for hp in range(NH // 2) for qb in range(NQB2)]
                    strips = []
                    for bi, (hp, qb) in enumerate(blocks):
                        for kc in range(2 * qb + 2):
                            strips.append((bi, hp, qb, kc))
                    NSTR = len(strips)
                    assert NSTR % 2 == 0
                    exp_done = {}
                    msk_done = {}
                    pT_rel = [0, 0, 0]
                    nslot_ctr = [0]
                    nrm_rel = [0] * 4

                    def issue_sT2(qi):
                        i0_, i1_ = 2 * qi, 2 * qi + 1
                        slot = qi % 3
                        sc0 = (qi % 2) * 1024
                        acquire_bank_cols(ps_sc, sc0, sc0 + 1024, 4)
                        for (i_, half) in ((i0_, 0), (i0_, 1), (i1_, 0), (i1_, 1)):
                            bi, hp, qb, kc = strips[i_]
                            oc = hp
                            q0 = oc * SL + qb * QBS
                            k0 = oc * SL + kc * 128
                            hb = half * 64
                            dst = ps_sc[:, sc0 + half * 512 + (i_ % 2) * 256: sc0 + half * 512 + (i_ % 2) * 256 + 256]
                            mm = nc.tensor.matmul(
                                dst,
                                kT[hb:hb + 64, k0:k0 + 128], qT[hb:hb + 64, q0:q0 + QBS],
                                start=True, stop=True)
                        done = s_pe.inc(mm)
                        ewait(nc.scalar, s_pe, done)
                        if pT_rel[slot]:
                            ewait(nc.scalar, s_pe, pT_rel[slot])
                        a = s_act.inc(nc.scalar.activation(pT[slot], ps_sc[:, sc0:sc0 + 1024], AF.Exp, bias=neg12_col[:, :], scale=0.125))
                        release_bank_cols(ps_sc, sc0, sc0 + 1024, 4, s_act, a)
                        exp_done[qi] = a
                        for i_ in (i0_, i1_):
                            bi, hp, qb, kc = strips[i_]
                            if kc >= 2 * qb:
                                o = qb * QBS - kc * 128 + 384
                                c0 = (i_ % 2) * 256
                                ewait(nc.vector, s_act, a)
                                nc.vector.tensor_tensor(
                                    pT[slot][:, c0:c0 + 256], pT[slot][:, c0:c0 + 256],
                                    bmask[:, o:o + 256], op=ALU.mult)
                                msk_done[i_] = s_dve.inc(nc.vector.tensor_tensor(
                                    pT[slot][:, 512 + c0:512 + c0 + 256], pT[slot][:, 512 + c0:512 + c0 + 256],
                                    bmask[:, o:o + 256], op=ALU.mult))

                    def issue_ctx(i):
                        bi, hp, qb, kc = strips[i]
                        slot = (i // 2) % 3
                        c0 = (i % 2) * 256
                        pc = ps_cx[bi % 2]
                        nkc = 2 * qb + 2
                        if kc == 0:
                            acquire_banks(pc)
                        if i in msk_done:
                            ewait(nc.tensor, s_dve, msk_done[i])
                        else:
                            ewait(nc.tensor, s_act, exp_done[i // 2])
                        v0 = kc * NH * 65
                        for half in (0, 1):
                            for hh in (0, 1):
                                mm = nc.tensor.matmul(
                                    pc[:, half * 130 + hh * 65: half * 130 + hh * 65 + 65],
                                    pT[slot][:, hh * 512 + c0 + half * 128: hh * 512 + c0 + half * 128 + 128],
                                    vst[:, v0 + (2 * hp + hh) * 65: v0 + (2 * hp + hh) * 65 + 65],
                                    start=(kc == 0 and half == 0 and hh == 0),
                                    stop=(kc == nkc - 1 and half == 1 and hh == 1))
                        r = s_pe.inc(mm)
                        if i % 2 == 1:
                            pT_rel[slot] = r
                        return r

                    def tail_dve(bi, last_pe):
                        pc = ps_cx[bi % 2]
                        ewait(nc.vector, s_pe, last_pe)
                        ks = []
                        nv = 0
                        for half in (0, 1):
                            k = nslot_ctr[0]
                            nslot_ctr[0] = (k + 1) % 4
                            ks.append(k)
                            if nrm_rel[k]:
                                ewait(nc.vector, s_pe, nrm_rel[k])
                            r0, r1 = rcol[2 * k], rcol[2 * k + 1]
                            nc.vector.reciprocal(r0, pc[:, half * 130 + 64: half * 130 + 65])
                            nc.vector.reciprocal(r1, pc[:, half * 130 + 129: half * 130 + 130])
                            nc.vector.tensor_scalar(
                                nrmc[k][:, 0:64], pc[:, half * 130: half * 130 + 64], r0, None, op0=ALU.mult)
                            nv = s_dve.inc(nc.vector.tensor_scalar(
                                nrmc[k][:, 64:128], pc[:, half * 130 + 65: half * 130 + 129], r1, None, op0=ALU.mult))
                        release_banks(pc, s_dve, nv)
                        return (bi, nv, ks)

                    def tail_pe(bi, nv, ks):
                        hp, qb = blocks[bi]
                        q0 = hp * SL + qb * QBS
                        acquire_banks(ptr)
                        ewait(nc.tensor, s_dve, nv)
                        for half in (0, 1):
                            mm = nc.tensor.transpose(ptr[:, half * 128:(half + 1) * 128], nrmc[ks[half]][:, :], ident[:, :])
                        tr = s_pe.inc(mm)
                        nrm_rel[ks[0]] = tr
                        nrm_rel[ks[1]] = tr
                        ewait(nc.scalar, s_pe, tr)
                        cc = s_act.inc(nc.scalar.copy(qT[:, q0:q0 + QBS], ptr[:, :]))
                        release_banks(ptr, s_act, cc)
                        return cc

                    NQUAD = NSTR // 2
                    ewait(nc.tensor, s_dve, ones_fix)
                    issue_sT2(0)
                    pend = []
                    ctxT_ready = 0
                    for qi in range(NQUAD + 1):
                        if qi + 1 < NQUAD:
                            issue_sT2(qi + 1)
                        if qi < 1:
                            continue
                        qc = qi - 1
                        for i in (2 * qc, 2 * qc + 1):
                            bi, hp, qb, kc = strips[i]
                            last_pe = issue_ctx(i)
                            if kc == 2 * qb + 1:
                                pend.append(tail_dve(bi, last_pe))
                                if len(pend) > 1:
                                    ctxT_ready = tail_pe(*pend.pop(0))
                    while pend:
                        ctxT_ready = tail_pe(*pend.pop(0))
                ctxT = qT

                widx = 6 + seq
                wslot = widx % 2
                with nc.psum_tensor(f"ps_o0_{seq}", [128, 1024], F32) as o0, \
                     nc.psum_tensor(f"ps_o1_{seq}", [128, 1024], F32) as o1, \
                     nc.psum_tensor(f"ps_tr2_{seq}", [128, 1024], BF16) as tr2, \
                     nc.psum_tensor(f"ps_tr3_{seq}", [128, 1024], BF16) as tr3, \
                     nc.psum_tensor(f"ps_xr0_{seq}", [16, 128], F32) as xr0, \
                     nc.psum_tensor(f"ps_xr1_{seq}", [16, 128], F32) as xr1:
                    ps_o = [o0, o1]
                    ps_tr = [tr2, tr3]
                    ps_xr = [xr0, xr1]
                    wait_w(nc.tensor, widx)
                    ewait(nc.tensor, s_act, ctxT_ready)
                    ewait(nc.sync, s_act, ctxT_ready)
                    LAGC = 3
                    r2_of = {}
                    for step in range(NTT + LAGC):
                        if step < NTT:
                            tt = step
                            slot = tt % 4
                            if tt == 0:
                                for i in range(3):
                                    load_x(seq, i, i)
                            if tt + 3 < NTT:
                                load_x(seq, tt + 3, (tt + 3) % 4)
                            pb = ps_o[tt % 2]
                            acquire_banks(pb)
                            for dc in range(NDC):
                                for dh in range(2):
                                    mm = nc.tensor.matmul(
                                        pb[:, dh * 512:(dh + 1) * 512],
                                        ctxT[:, dc * SL + tt * 128: dc * SL + tt * 128 + 128],
                                        wsb[wslot][:, dc * 1024 + dh * 512: dc * 1024 + dh * 512 + 512],
                                        start=(dc == 0), stop=(dc == NDC - 1))
                            pe_c = s_pe.inc(mm)
                            if tt == NTT - 1:
                                w_last_use[wslot] = pe_c
                            ewait(nc.vector, s_pe, pe_c)
                            xs, xv = x_done[(seq, tt)]
                            ewait(nc.vector, s_x[xs], xv)
                            if x2_rel[slot]:
                                ewait(nc.vector, s_x2w[slot], x2_rel[slot])
                            if x2_reader[slot]:
                                ewait(nc.vector, s_dve, x2_reader[slot])
                            rs = s_dve.inc(nc.vector.tensor_tensor(x2_sb[slot], pb[:, :], x_tile[slot], op=ALU.add))
                            x_free[slot] = rs
                            release_banks(pb, s_dve, rs)
                            row0 = seq * SL + tt * 128
                            ewait(nc.sync, s_dve, rs)
                            x2_rel[slot] = s_x2w[slot].inc(nc.sync.dma_start(x2s[row0:row0 + 128, :], x2_sb[slot]), 16)
                            if xln_rel[slot]:
                                ewait(nc.vector, s_pe, xln_rel[slot])
                            r2_of[tt] = ln_tile(slot, x2_sb[slot], xln_t[slot])
                            x2_reader[slot] = r2_of[tt]
                        if step >= LAGC:
                            tt = step - LAGC
                            slot = tt % 4
                            pt = ps_tr[tt % 2]
                            acquire_banks(pt)
                            ewait(nc.tensor, s_dve, r2_of[tt])
                            for i in range(8):
                                mm = nc.tensor.transpose(pt[:, i * 128:(i + 1) * 128], xln_t[slot][:, i * 128:(i + 1) * 128], ident[:, :])
                            xln_rel[slot] = s_pe.inc(mm)
                            ewait(nc.scalar, s_pe, xln_rel[slot])
                            if tr_free[slot]:
                                ewait(nc.scalar, s_xl2[slot], tr_free[slot])
                            if tr_read[slot]:
                                ewait(nc.scalar, s_pe, tr_read[slot])
                            ev = s_act.inc(nc.scalar.copy(trbuf[slot].rearrange("p (dc t) -> p dc t", dc=NDC),
                                                          pt.ap().rearrange("p (dc t) -> p dc t", dc=NDC)))
                            release_banks(pt, s_act, ev)
                            ewait(nc.sync, s_act, ev)
                            t0c = seq * SL + tt * 128
                            tr_free[slot] = s_xl2[slot].inc(nc.sync.dma_start(
                                xl2s[:, :, t0c:t0c + 128].rearrange("dc p t -> p dc t"),
                                trbuf[slot].rearrange("p (dc t) -> p dc t", dc=NDC)), 16)
                            pxr = ps_xr[tt % 2]
                            acquire_banks(pxr)
                            ewait(nc.tensor, s_act, ev)
                            ewait(nc.tensor, s_pp, 16)
                            for dc in range(NDC):
                                mm = nc.tensor.matmul(
                                    pxr[:, :],
                                    wgr_sb[:, dc * 16:(dc + 1) * 16],
                                    trbuf[slot][:, dc * 128:(dc + 1) * 128],
                                    start=(dc == 0), stop=(dc == NDC - 1))
                            xr_pe = s_pe.inc(mm)
                            tr_read[slot] = xr_pe
                            ewait(nc.scalar, s_pe, xr_pe)
                            ev2 = s_act.inc(nc.scalar.copy(xrT[:, t0c:t0c + 128], pxr[:, :]))
                            release_banks(pxr, s_act, ev2)

            NGS = 4
            gmax = [A(nc.sbuf_tensor(f"gmax{i}", [128, 1], F32)) for i in range(NGS)]
            gnegm = [A(nc.sbuf_tensor(f"gnegm{i}", [128, 1], F32)) for i in range(NGS)]
            gexp = [A(nc.sbuf_tensor(f"gexp{i}", [128, 8], F32)) for i in range(NGS)]
            gsum = [A(nc.sbuf_tensor(f"gsum{i}", [128, 1], F32)) for i in range(NGS)]
            xr_ready = s_act.n
            with nc.psum_tensor("ps_lg0", [128, 8], F32) as lg0, \
                 nc.psum_tensor("ps_lg1", [128, 8], F32) as lg1, \
                 nc.psum_tensor("ps_lg2", [128, 8], F32) as lg2, \
                 nc.psum_tensor("ps_lg3", [128, 8], F32) as lg3, \
                 nc.psum_tensor("ps_oh0", [8, 128], F32) as poh0, \
                 nc.psum_tensor("ps_oh1", [8, 128], F32) as poh1:
                ps_lg = [lg0, lg1, lg2, lg3]
                ps_oh = [poh0, poh1]
                ewait(nc.tensor, s_act, xr_ready)
                ewait(nc.tensor, s_act, wgu_ready)
                a1_slot = [0] * NGS
                sm_slot = [0] * NGS
                for tt in range(32):
                    slot = tt % NGS
                    pb = ps_lg[slot]
                    acquire_banks(pb)
                    mm = nc.tensor.matmul(pb[:, :], xrT[:, tt * 128:(tt + 1) * 128], wguT[:, :], start=True, stop=True)
                    pe_c = s_pe.inc(mm)
                    ewait(nc.vector, s_pe, pe_c)
                    if a1_slot[slot]:
                        ewait(nc.vector, s_act, a1_slot[slot])
                    m1 = s_dve.inc(nc.vector.reduce_max(gmax[slot][:, :], pb[:, :], axis=AX.X))
                    ewait(nc.scalar, s_dve, m1)
                    a1 = s_act.inc(nc.scalar.mul(gnegm[slot][:, :], gmax[slot][:, :], -1.0))
                    a1_slot[slot] = a1
                    ewait(nc.scalar, s_act, a1)
                    if sm_slot[slot]:
                        ewait(nc.scalar, s_dve, sm_slot[slot])
                    a2 = s_act.inc(nc.scalar.activation(gexp[slot][:, :], pb[:, :], AF.Exp, bias=gnegm[slot][:, :]))
                    ewait(nc.vector, s_act, a2)
                    oh1 = s_dve.inc(nc.vector.tensor_scalar(oh_sb[:, tt * 8:(tt + 1) * 8], pb[:, :], gmax[slot][:, :], None, op0=ALU.is_equal))
                    release_banks(pb, s_dve, oh1)
                    po = ps_oh[tt % 2]
                    acquire_banks(po)
                    ewait(nc.tensor, s_dve, oh1)
                    trh = s_pe.inc(nc.tensor.transpose(po[:, :], oh_sb[:, tt * 8:(tt + 1) * 8], ident32[:, :]))
                    ewait(nc.scalar, s_pe, trh)
                    ohc = s_act.inc(nc.scalar.copy(ohT_sb[0:8, tt * 128:(tt + 1) * 128], po[:, :]))
                    release_banks(po, s_act, ohc)
                    sm = s_dve.inc(nc.vector.reduce_sum(gsum[slot][:, :], gexp[slot][:, :], axis=AX.X))
                    sm_slot[slot] = sm
                    ewait(nc.vector, s_dve, sm)
                    s_dve.inc(nc.vector.reciprocal(sc_sb[:, tt:tt + 1], gsum[slot][:, :]))
            ohT_ready = s_act.n
            gating_done = s_dve.n
            act_fin = s_act.n
            ewait(nc.tensor, s_dve, gating_done)
            ewait(nc.scalar, s_dve, gating_done)
            ewait(nc.sync, s_st, s_st.n)
            for i_ in range(4):
                ewait(nc.sync, s_xl2[i_], s_xl2[i_].n)
                ewait(nc.sync, s_x2w[i_], s_x2w[i_].n)
            ewait(nc.sync, s_dve, gating_done)
            ewait(nc.sync, s_act, act_fin)
            ewait(nc.gpsimd, s_dve, gating_done)
            ewait(nc.gpsimd, s_act, act_fin)

        with ExitStack() as fst:
            F = fst.enter_context
            xbuf = [F(nc.sbuf_tensor(f"xbuf{i}", [128, NDC * TCH], BF16)) for i in range(2)]
            hT = F(nc.sbuf_tensor("hT", [128, NHC * TCH], BF16))
            w1s = [F(nc.sbuf_tensor(f"w1s{i}", [128, 4096], BF16)) for i in range(3)]
            w2s = [F(nc.sbuf_tensor(f"w2s{i}", [128, 4096], BF16)) for i in range(3)]
            acc = F(nc.sbuf_tensor("acc", [128, NSUB * 1024], F32))
            b1sb = F(nc.sbuf_tensor("b1sb", [128, E * 32], F32))

            x2b = [F(nc.sbuf_tensor(f"x2b{i}", [128, 1024], F32)) for i in range(NSUB)]
            osb = [F(nc.sbuf_tensor(f"osb{i}", [128, 1024], F32)) for i in range(NSUB)]

            s_ld.inc(nc.sync.dma_start(b1sb.ap().rearrange("p (e c) -> p e c", e=E), b1r.rearrange("e p c -> p e c")), 16)
            s_pp.inc(nc.gpsimd.dma_start(b2a_sb[:, :], b2f.rearrange("o (e d) -> (o e) d", e=E)), 16)
            const_ld = s_ld.n
            const_pp = s_pp.n

            xln_done = [0] * NTC
            def load_xln(tc):
                src = xl2s[:, :, tc * TCH:(tc + 1) * TCH].rearrange("dc p t -> p dc t")
                dst = xbuf[tc % 2].ap().rearrange("p (dc t) -> p dc t", dc=NDC)
                xln_done[tc] = s_xb[tc % 2].inc(nc.sync.dma_start(dst, src), 16)
            load_xln(0)

            xln_last_use = [0] * NTC
            w1_done = {}
            w1_last_use = {}
            w2_done = {}
            w2_last_use = {}
            gelu_done = [0] * NHC
            last_stt = 0
            acc_writer = [0] * NSUB
            x2b_free = [0] * NSUB
            osb_store_done = [0] * NSUB
            pending_stores = []
            x2_done = {}
            w1_issued = set()
            w2_issued = set()
            w1_slot_of = {}
            w2_slot_of = {}
            w1_ctr = [0]
            w2_ctr = [0]

            def issue_w1(tc_, e_, g_):
                if (tc_, e_, g_) in w1_issued or tc_ >= NTC:
                    return
                w1_issued.add((tc_, e_, g_))
                slot = w1_ctr[0] % 3
                w1_ctr[0] += 1
                w1_slot_of[(tc_, e_, g_)] = slot
                prev_use = w1_last_use.get(slot, 0)
                if prev_use:
                    ewait(nc.sync, s_pe, prev_use)
                ewait(nc.sync, s_ppe[e_], pp_ready[('e', e_)])
                w1_done[(tc_, e_, g_)] = (slot, s_w1[slot].inc(nc.sync.dma_start(w1s[slot][:, :], w1b[e_, g_]), 16))

            def issue_w2(tc_, e_, j_):
                if (tc_, e_, j_) in w2_issued or tc_ >= NTC:
                    return
                w2_issued.add((tc_, e_, j_))
                slot = w2_ctr[0] % 3
                w2_ctr[0] += 1
                w2_slot_of[(tc_, e_, j_)] = slot
                prev_use = w2_last_use.get(slot, 0)
                if prev_use:
                    ewait(nc.gpsimd, s_pe, prev_use)
                ewait(nc.gpsimd, s_ppe[e_], pp_ready[('e', e_)])
                src = w2b[e_, 4 * j_:4 * j_ + 4].rearrange("h p d -> p h d")
                dst = w2s[slot].ap().rearrange("p (h d) -> p h d", h=4)
                w2_done[(tc_, e_, j_)] = (slot, s_w2[slot].inc(nc.gpsimd.dma_start(dst, src), 16))

            def flush_stores():
                while pending_stores:
                    tc_, tsub_, w = pending_stores.pop(0)
                    row0 = tc_ * TCH + tsub_ * 128
                    ewait(nc.sync, s_dve, w)
                    osb_store_done[tsub_] = s_st.inc(nc.sync.dma_start(out[row0:row0 + 128, :], osb[tsub_][:, :]), 16)

            first_e_mm = True
            ewait(nc.tensor, s_ld, const_ld)
            ewait(nc.tensor, s_pp, const_pp)
            ewait(nc.vector, s_ld, const_ld)
            ewait(nc.scalar, s_ld, const_ld)

            for tc in range(NTC):
                if tc + 1 < NTC:
                    if tc >= 1:
                        ewait(nc.sync, s_pe, xln_last_use[tc - 1])
                    load_xln(tc + 1)
                xb = xbuf[tc % 2]
                acc_init = [0] * NSUB
                with nc.psum_tensor(f"ps_b2a_{tc}", [128, 1024], F32) as pb2a, \
                     nc.psum_tensor(f"ps_b2b_{tc}", [128, 1024], F32) as pb2b:
                    pb2l = [pb2a, pb2b]
                    ewait(nc.tensor, s_act, ohT_ready)
                    ewait(nc.tensor, s_pp, const_pp)
                    for tsub in range(NSUB):
                        pb = pb2l[tsub % 2]
                        acquire_banks(pb)
                        t0g = (tc * NSUB + tsub) * 128
                        for dh in range(2):
                            mm = nc.tensor.matmul(
                                pb[:, dh * 512:(dh + 1) * 512],
                                ohT_sb[0:8, t0g:t0g + 128],
                                b2a_sb[0:8, dh * 512:(dh + 1) * 512],
                                start=True, stop=True)
                        pe_b = s_pe.inc(mm)
                        ewait(nc.scalar, s_pe, pe_b)
                        if acc_writer[tsub]:
                            ewait(nc.scalar, s_dve, acc_writer[tsub])
                        ai = s_act.inc(nc.scalar.copy(acc[:, tsub * 1024:(tsub + 1) * 1024], pb[:, :]))
                        release_banks(pb, s_act, ai)
                        acc_init[tsub] = ai
                for e in range(E):
                    if e == 1:
                        flush_stores()
                    if e == E - 1:
                        for tsub in range(NSUB):
                            row0 = tc * TCH + tsub * 128
                            if x2b_free[tsub]:
                                ewait(nc.sync, s_dve, x2b_free[tsub])
                            x2_done[(tc, tsub)] = s_x2b.inc(nc.sync.dma_start(x2b[tsub][:, :], x2s[row0:row0 + 128, :]), 16)
                    with nc.psum_tensor(f"ps_f1a_{tc}_{e}", [128, 512], F32) as f1a, \
                         nc.psum_tensor(f"ps_f1b_{tc}_{e}", [128, 512], F32) as f1b:
                        ps_f1 = [f1a, f1b]
                        issue_w1(tc, e, 0)
                        issue_w1(tc, e, 1)
                        for g in range(NG):
                            slot = w1_slot_of[(tc, e, g)]
                            if g + 2 < NG:
                                issue_w1(tc, e, g + 2)
                            else:
                                ntc_ = tc + 1 if e == E - 1 else tc
                                issue_w1(ntc_, (e + 1) % E, g + 2 - NG)
                            w1s_, w1v_ = w1_done[(tc, e, g)]
                            ewait(nc.tensor, s_w1[w1s_], w1v_)
                            if g == 0:
                                ewait(nc.tensor, s_xb[tc % 2], xln_done[tc])
                            for hc in range(4 * g, 4 * g + 4):
                                pb = ps_f1[hc % 2]
                                acquire_banks(pb)
                                for dc in range(NDC):
                                    mm = nc.tensor.matmul(
                                        pb[:, :],
                                        w1s[slot][:, dc * 512 + (hc % 4) * 128: dc * 512 + (hc % 4) * 128 + 128],
                                        xb[:, dc * 512:(dc + 1) * 512],
                                        start=(dc == 0), stop=(dc == NDC - 1))
                                w1_last_use[slot] = s_pe.inc(mm)
                                if hc == NHC - 1:
                                    xln_last_use[tc] = w1_last_use[slot]
                                ewait(nc.scalar, s_pe, w1_last_use[slot])
                                g_i = nc.scalar.activation(
                                    hT[:, hc * TCH:(hc + 1) * TCH], pb[:, :], AF.Gelu,
                                    bias=b1sb[:, e * 32 + hc: e * 32 + hc + 1])
                                gelu_done[hc] = s_act.inc(g_i)
                                release_banks(pb, s_act, gelu_done[hc])
                        first_e_mm = False
                    with nc.psum_tensor(f"ps_f2a_{tc}_{e}", [128, 1024], F32) as f2a, \
                         nc.psum_tensor(f"ps_f2b_{tc}_{e}", [128, 1024], F32) as f2b, \
                         nc.psum_tensor(f"ps_f2c_{tc}_{e}", [128, 1024], F32) as f2c, \
                         nc.psum_tensor(f"ps_f2d_{tc}_{e}", [128, 1024], F32) as f2d:
                        ps_f2 = [f2a, f2b, f2c, f2d]
                        f2_stop = [0] * NSUB
                        issue_w2(tc, e, 0)
                        issue_w2(tc, e, 1)
                        for j in range(NHC // 4):
                            slot = w2_slot_of[(tc, e, j)]
                            if j + 2 < NHC // 4:
                                issue_w2(tc, e, j + 2)
                            else:
                                ntc_ = tc + 1 if e == E - 1 else tc
                                issue_w2(ntc_, (e + 1) % E, j + 2 - NHC // 4)
                            w2s_, w2v_ = w2_done[(tc, e, j)]
                            ewait(nc.tensor, s_w2[w2s_], w2v_)
                            for hc in range(4 * j, 4 * j + 4):
                                for tsub in range(NSUB):
                                    if hc == 0:
                                        acquire_banks(ps_f2[tsub])
                                    for dh in range(2):
                                        mm = nc.tensor.matmul(
                                            ps_f2[tsub][:, dh * 512:(dh + 1) * 512],
                                            hT[:, hc * TCH + tsub * 128: hc * TCH + tsub * 128 + 128],
                                            w2s[slot][:, (hc % 4) * 1024 + dh * 512: (hc % 4) * 1024 + (dh + 1) * 512],
                                            start=(hc == 0), stop=(hc == NHC - 1 and dh == 1))
                                    if hc == NHC - 1:
                                        f2_stop[tsub] = s_pe.inc(mm)
                            if f2_stop[NSUB - 1]:
                                w2_last_use[slot] = f2_stop[NSUB - 1] if 4 * j + 3 == NHC - 1 else s_pe.inc(mm)
                            else:
                                w2_last_use[slot] = s_pe.inc(mm)
                        for tsub in range(NSUB):
                            ewait(nc.vector, s_pe, f2_stop[tsub])
                            mcol = oh_sb[:, (tc * NSUB + tsub) * E + e: (tc * NSUB + tsub) * E + e + 1]
                            if e == 0:
                                ewait(nc.vector, s_act, acc_init[tsub])
                            halves = ((0, 512), (512, 1024)) if tsub == 0 else ((0, 1024),)
                            for (c0, c1) in halves:
                                a_sl = acc[:, tsub * 1024 + c0: tsub * 1024 + c1]
                                i = nc.vector.scalar_tensor_tensor(a_sl, ps_f2[tsub][:, c0:c1], mcol, a_sl, op0=ALU.mult, op1=ALU.add)
                                w = s_dve.inc(i)
                                release_bank_cols(ps_f2[tsub], c0, c1, 4, s_dve, w)
                            acc_writer[tsub] = w
                        last_stt = acc_writer[NSUB - 1]
                for tsub in range(NSUB):
                    ewait(nc.vector, s_x2b, x2_done[(tc, NSUB - 1)])
                    ewait(nc.vector, s_dve, acc_writer[tsub])
                    if osb_store_done[tsub]:
                        ewait(nc.vector, s_st, osb_store_done[tsub])
                    sc = sc_sb[:, tc * NSUB + tsub: tc * NSUB + tsub + 1]
                    i = nc.vector.scalar_tensor_tensor(osb[tsub][:, :], acc[:, tsub * 1024:(tsub + 1) * 1024], sc, x2b[tsub][:, :], op0=ALU.mult, op1=ALU.add)
                    w = s_dve.inc(i)
                    pending_stores.append((tc, tsub, w))
                    acc_writer[tsub] = w
                    x2b_free[tsub] = w
            flush_stores()
            nc.sync.wait_ge(s_st.h, s_st.n)
    return nc


_NC_CACHE = {}

def _get_nc():
    if "nc" not in _NC_CACHE:
        _NC_CACHE["nc"] = build()
    return _NC_CACHE["nc"]


def kernel(L0_attn_x, L0_attn_W_Q, L0_attn_W_K, L0_attn_W_V, L0_attn_W_O,
           L0_ffn_wg_red, L0_ffn_wg, L0_ffn_w1, L0_ffn_b1, L0_ffn_w2, L0_ffn_b2):
    x = np.asarray(L0_attn_x, dtype=np.float32)
    WQ = np.asarray(L0_attn_W_Q, dtype=np.float32)
    WK = np.asarray(L0_attn_W_K, dtype=np.float32)
    WV = np.asarray(L0_attn_W_V, dtype=np.float32)
    WO = np.asarray(L0_attn_W_O, dtype=np.float32)
    wg_red = np.asarray(L0_ffn_wg_red, dtype=np.float32)
    wg = np.asarray(L0_ffn_wg, dtype=np.float32)
    w1 = np.ascontiguousarray(np.asarray(L0_ffn_w1, dtype=np.float32))
    b1 = np.asarray(L0_ffn_b1, dtype=np.float32)
    w2 = np.ascontiguousarray(np.asarray(L0_ffn_w2, dtype=np.float32))
    b2 = np.asarray(L0_ffn_b2, dtype=np.float32)

    wqt = np.ascontiguousarray(WQ.T)
    wkt = np.ascontiguousarray(WK.T)
    wvt = np.ascontiguousarray(WV.T)
    wot = np.ascontiguousarray(WO.T)
    wgr = np.ascontiguousarray(wg_red.T)
    b1r = np.ascontiguousarray(b1.reshape(E, 32, 128).transpose(0, 2, 1))
    b2f = np.ascontiguousarray(b2.reshape(1, E * D))

    nc = _get_nc()
    shared = dict(wqt=wqt, wkt=wkt, wvt=wvt, wot=wot, wgr=wgr, wg=wg,
                  w1=w1, b1r=b1r, w2=w2, b2f=b2f)
    in_maps = []
    for c in range(8):
        shard = np.ascontiguousarray(x[2 * c:2 * c + 2].reshape(T, D))
        m = dict(shared)
        m["x_in"] = shard
        in_maps.append(m)
    res = run_bass_kernel_spmd(nc, in_maps, core_ids=list(range(8)))
    outs = [np.asarray(res.results[c]["out"], dtype=np.float32).reshape(2, SL, D) for c in range(8)]
    return np.concatenate(outs, axis=0)


if __name__ == "__main__":
    import importlib.util
    spec = importlib.util.spec_from_file_location("_ref", "/root/problem/reference.py")
    ref = importlib.util.module_from_spec(spec)
    spec.loader.exec_module(ref)
    inputs = {k: np.asarray(v) for k, v in ref.setup_inputs().items()}
    got = kernel(**inputs)
    want = np.asarray(ref.reference(**inputs))
    err = got - want
    ratio = float((err ** 2).mean() / (want ** 2).mean())
    print("residual variance ratio:", ratio, "(bar 2.27e-04)")
    print("PASS" if ratio < 2.27e-4 else "FAIL")


# version 26
# speedup vs baseline: 1.0698x; 1.0299x over previous
import numpy as np
from contextlib import ExitStack
import concourse.bass as bass
import concourse.mybir as mybir
from concourse.bass_utils import run_bass_kernel_spmd

F32 = mybir.dt.float32
BF16 = mybir.dt.bfloat16
I32 = mybir.dt.int32
AF = mybir.ActivationFunctionType
ALU = mybir.AluOpType
AX = mybir.AxisListType

NSEQ, SL, D, NH, DK = 2, 2048, 1024, 16, 64
T = NSEQ * SL
NDC = D // 128
NTT = SL // 128
NQB = SL // 512
NKC = SL // 128
E = 8
HID = 4096
NHC = HID // 128
NG = 8
TCH = 512
NTC = T // TCH
NSUB = TCH // 128


class Sem:
    def __init__(self, stack, nc, name):
        self.h = stack.enter_context(nc.semaphore(name))
        self.n = 0
    def inc(self, instr, k=1):
        instr.then_inc(self.h, k)
        self.n += k
        return self.n


def build():
    nc = bass.Bass("TRN2", target_bir_lowering=False, debug=False)
    x_in = nc.dram_tensor("x_in", [T, D], F32, kind="ExternalInput").ap()
    wqt = nc.dram_tensor("wqt", [D, D], F32, kind="ExternalInput").ap()
    wkt = nc.dram_tensor("wkt", [D, D], F32, kind="ExternalInput").ap()
    wvt = nc.dram_tensor("wvt", [D, D], F32, kind="ExternalInput").ap()
    wot = nc.dram_tensor("wot", [D, D], F32, kind="ExternalInput").ap()
    wgr = nc.dram_tensor("wgr", [D, 16], F32, kind="ExternalInput").ap()
    wg = nc.dram_tensor("wg", [8, 16], F32, kind="ExternalInput").ap()
    w1 = nc.dram_tensor("w1", [E, D, HID], F32, kind="ExternalInput").ap()
    b1r = nc.dram_tensor("b1r", [E, 128, 32], F32, kind="ExternalInput").ap()
    w2 = nc.dram_tensor("w2", [E, HID, D], F32, kind="ExternalInput").ap()
    b2f = nc.dram_tensor("b2f", [1, E * D], F32, kind="ExternalInput").ap()
    out = nc.dram_tensor("out", [T, D], F32, kind="ExternalOutput").ap()
    x2s = nc.dram_tensor("x2s", [T, D], F32, kind="Internal").ap()
    xl2s = nc.dram_tensor("xl2s", [NDC, 128, T], BF16, kind="Internal").ap()
    wab = nc.dram_tensor("wab", [4, 128, NDC * 1024], BF16, kind="Internal").ap()
    w1b = nc.dram_tensor("w1b", [E, NG, 128, 4096], BF16, kind="Internal").ap()
    w2b = nc.dram_tensor("w2b", [E, NHC, 128, 1024], BF16, kind="Internal").ap()

    with ExitStack() as st:
        S = st.enter_context
        ident = S(nc.sbuf_tensor("ident", [128, 128], BF16))
        ident32 = S(nc.sbuf_tensor("ident32", [128, 128], F32))
        onesb = S(nc.sbuf_tensor("onesb", [1, 128], BF16))
        eps_col = S(nc.sbuf_tensor("eps_col", [128, 1], F32))
        neg12_col = S(nc.sbuf_tensor("neg12_col", [128, 1], F32))
        wgr_sb = S(nc.sbuf_tensor("wgr_sb", [128, NDC * 16], BF16))
        wg_sb = S(nc.sbuf_tensor("wg_sb", [8, 16], F32))
        wgu_bf = S(nc.sbuf_tensor("wgu_bf", [8, 16], BF16))
        wguT = S(nc.sbuf_tensor("wguT", [16, 8], BF16))
        xrT = S(nc.sbuf_tensor("xrT", [16, T], BF16))
        oh_sb = S(nc.sbuf_tensor("oh_sb", [128, 32 * E], F32))
        ohT_sb = S(nc.sbuf_tensor("ohT_sb", [8, T], BF16))
        b2a_sb = S(nc.sbuf_tensor("b2a_sb", [8, 1024], BF16))
        sc_sb = S(nc.sbuf_tensor("sc_sb", [128, 32], F32))
        sq = S(nc.sbuf_tensor("sq", [8, 16], F32))
        nrm = S(nc.sbuf_tensor("nrm", [8, 1], F32))
        nrm2 = S(nc.sbuf_tensor("nrm2", [8, 1], F32))
        rn = S(nc.sbuf_tensor("rn", [8, 1], F32))

        s_ld = Sem(st, nc, "s_ld")
        s_x = [Sem(st, nc, f"s_x{i}") for i in range(4)]
        s_w = [Sem(st, nc, f"s_w{i}") for i in range(2)]
        s_w1 = [Sem(st, nc, f"s_w1{i}") for i in range(3)]
        s_w2 = [Sem(st, nc, f"s_w2{i}") for i in range(3)]
        s_xb = [Sem(st, nc, f"s_xb{i}") for i in range(2)]
        s_x2b = Sem(st, nc, "s_x2b")
        s_ppe = [Sem(st, nc, f"s_ppe{i}") for i in range(E)]
        s_ppwa = Sem(st, nc, "s_ppwa")
        s_xl2 = [Sem(st, nc, f"s_xl2{i}") for i in range(4)]
        s_x2w = [Sem(st, nc, f"s_x2w{i}") for i in range(4)]
        s_pp = Sem(st, nc, "s_pp")
        s_pps = Sem(st, nc, "s_pps")
        s_ldw = Sem(st, nc, "s_ldw")
        s_pe = Sem(st, nc, "s_pe")
        s_act = Sem(st, nc, "s_act")
        s_dve = Sem(st, nc, "s_dve")
        s_st = Sem(st, nc, "s_st")
        s_gp = Sem(st, nc, "s_gp")

        wait_hwm = {}
        def ewait(eng, sem, val):
            if val <= 0:
                return
            hw = wait_hwm.setdefault(id(eng), {})
            if hw.get(id(sem.h), 0) >= val:
                return
            eng.wait_ge(sem.h, val)
            hw[id(sem.h)] = val
        bank_release = {}
        def nbanks(h):
            return max(1, (int(h.ap().free_nbytes()) + 2047) // 2048)
        def acquire_banks(h):
            m = nc.lookup_mloc(h)
            for b in range(m.bank, m.bank + nbanks(h)):
                r = bank_release.get(b)
                if r is not None:
                    ewait(nc.tensor, r[0], r[1])
        def release_banks(h, sem, val):
            m = nc.lookup_mloc(h)
            for b in range(m.bank, m.bank + nbanks(h)):
                bank_release[b] = (sem, val)
        def acquire_bank_cols(h, c0, c1, elt):
            m = nc.lookup_mloc(h)
            b0 = m.bank + (c0 * elt) // 2048
            b1 = m.bank + ((c1 * elt) - 1) // 2048
            for b in range(b0, b1 + 1):
                r = bank_release.get(b)
                if r is not None:
                    ewait(nc.tensor, r[0], r[1])
        def release_bank_cols(h, c0, c1, elt, sem, val):
            m = nc.lookup_mloc(h)
            b0 = m.bank + (c0 * elt) // 2048
            b1 = m.bank + ((c1 * elt) - 1) // 2048
            for b in range(b0, b1 + 1):
                bank_release[b] = (sem, val)

        with ExitStack() as ast_:
            A = ast_.enter_context
            bmask = A(nc.sbuf_tensor("bmask", [128, 896], BF16))
            xlnT = A(nc.sbuf_tensor("xlnT", [128, NDC * SL], BF16))
            qT = A(nc.sbuf_tensor("qT", [128, NDC * SL], BF16))
            kT = A(nc.sbuf_tensor("kT", [128, NDC * SL], BF16))
            vst = A(nc.sbuf_tensor("vst", [128, NKC * NH * 65], BF16))
            wsb = [A(nc.sbuf_tensor(f"wsb{i}", [128, NDC * 1024], BF16)) for i in range(2)]
            st6 = [A(nc.sbuf_tensor(f"st6{i}", [128, 12], F32)) for i in range(4)]
            mv = [A(nc.sbuf_tensor(f"mv{i}", [128, 2], F32)) for i in range(4)]
            stdv = [A(nc.sbuf_tensor(f"stdv{i}", [128, 1], F32)) for i in range(4)]
            negm = [A(nc.sbuf_tensor(f"negm{i}", [128, 1], F32)) for i in range(4)]
            rstd = [A(nc.sbuf_tensor(f"rstd{i}", [128, 1], F32)) for i in range(4)]
            x_tile_h = [A(nc.sbuf_tensor(f"x_tile{i}", [128, D], F32)) for i in range(2)]
            xln_t_h = [A(nc.sbuf_tensor(f"xln_t{i}", [128, D], BF16)) for i in range(2)]
            x_tile = [x_tile_h[0][:, :], x_tile_h[1][:, :],
                      vst[:, 2048:4096].bitcast(F32), vst[:, 4096:6144].bitcast(F32)]
            xln_t = [xln_t_h[0][:, :], xln_t_h[1][:, :],
                     vst[:, 0:1024], vst[:, 1024:2048]]
            x2_sb = [kT[:, i * 2048:(i + 1) * 2048].bitcast(F32) for i in range(4)]
            trbuf = [kT[:, 8192 + i * 1024: 8192 + (i + 1) * 1024] for i in range(4)]

            with nc.sbuf_tensor("iota32", [128, 448], I32) as iota32:
                g1 = s_gp.inc(nc.gpsimd.iota(iota32[:, :], [[1, 448]], base=-384, channel_multiplier=-1))
                ewait(nc.vector, s_gp, g1)
                nc.vector.tensor_scalar(bmask[:, 0:448], iota32[:, :], 0, None, op0=ALU.is_ge)
                nc.vector.tensor_scalar(ident32[:, 0:64], iota32[:, 384:448], 0, None, op0=ALU.is_equal)
                d0 = s_dve.inc(nc.vector.tensor_scalar(ident[:, 0:64], iota32[:, 384:448], 0, None, op0=ALU.is_equal))
                ewait(nc.gpsimd, s_dve, d0)
                g2 = s_gp.inc(nc.gpsimd.iota(iota32[:, :], [[1, 448]], base=64, channel_multiplier=-1))
                ewait(nc.vector, s_gp, g2)
                nc.vector.tensor_scalar(bmask[:, 448:896], iota32[:, :], 0, None, op0=ALU.is_ge)
                nc.vector.tensor_scalar(ident32[:, 64:128], iota32[:, 0:64], 0, None, op0=ALU.is_equal)
                d1 = s_dve.inc(nc.vector.tensor_scalar(ident[:, 64:128], iota32[:, 0:64], 0, None, op0=ALU.is_equal))
                nc.vector.memset(onesb[:, :], 1.0)
                nc.vector.memset(eps_col[:, :], 1e-5)
                nc.vector.memset(neg12_col[:, :], -12.0)
                vst4 = vst.ap().rearrange("p (kc h c) -> p kc h c", kc=NKC, h=NH)
                setup_dve = s_dve.inc(nc.vector.engine_nop())

                s_ld.inc(nc.sync.dma_start(wg_sb[:, :], wg), 16)
                s_pp.inc(nc.gpsimd.dma_start(wgr_sb.ap().rearrange("p (dc c) -> p dc c", dc=NDC),
                                             wgr.rearrange("(dc p) c -> p dc c", p=128)), 16)
                ewait(nc.scalar, s_ld, s_ld.n)
                a = s_act.inc(nc.scalar.activation(sq[:, :], wg_sb[:, :], AF.Square))
                ewait(nc.vector, s_act, a)
                v1 = s_dve.inc(nc.vector.reduce_sum(nrm[:, :], sq[:, :], axis=AX.X))
                ewait(nc.scalar, s_dve, v1)
                a = s_act.inc(nc.scalar.activation(nrm2[:, :], nrm[:, :], AF.Sqrt))
                ewait(nc.vector, s_act, a)
                v2 = s_dve.inc(nc.vector.tensor_scalar(nrm2[:, :], nrm2[:, :], 1e-4, None, op0=ALU.max))
                ewait(nc.vector, s_dve, v2)
                v3 = s_dve.inc(nc.vector.reciprocal(rn[:, :], nrm2[:, :]))
                ewait(nc.vector, s_dve, v3)
                v4 = s_dve.inc(nc.vector.tensor_scalar(wgu_bf[:, :], wg_sb[:, :], rn[:, :], None, op0=ALU.mult))
                with nc.psum_tensor("ps_wgu", [16, 8], BF16) as ps_wgu:
                    ewait(nc.tensor, s_dve, v4)
                    ewait(nc.tensor, s_dve, d1)
                    t1 = s_pe.inc(nc.tensor.transpose(ps_wgu[:, :], wgu_bf[:, :], ident[0:8, 0:8]))
                    ewait(nc.scalar, s_pe, t1)
                    wgu_ready = s_act.inc(nc.scalar.copy(wguT[:, :], ps_wgu[:, :]))

            pp_ready = {}
            def run_prepass():
                for widx, wsrc in enumerate((wqt, wkt, wvt, wot)):
                    src = wsrc.rearrange("(dc p) o -> p dc o", p=128)
                    dst = wab[widx].rearrange("p (dc o) -> p dc o", dc=NDC)
                    s_ppwa.inc(nc.gpsimd.dma_start(dst, src), 16)
                pp_ready['wa'] = s_ppwa.n
                for e_ in range(E):
                    for g in range(NG):
                        src = w1[e_][:, g * 512:(g + 1) * 512].rearrange("(dc p) h -> p dc h", p=128)
                        dst = w1b[e_, g].rearrange("p (dc h) -> p dc h", dc=NDC)
                        s_ppe[e_].inc(nc.gpsimd.dma_start(dst, src), 16)
                    for j in range(NG):
                        src = w2[e_][j * 512:(j + 1) * 512, :].rearrange("(hc p) d -> p hc d", p=128)
                        dst = w2b[e_, 4 * j:4 * j + 4].rearrange("hc p d -> p hc d")
                        s_ppe[e_].inc(nc.gpsimd.dma_start(dst, src), 16)
                    pp_ready[('e', e_)] = s_ppe[e_].n
            run_prepass()

            w_done = {}
            def wait_w(eng, idx):
                ewait(eng, s_w[idx % 2], w_done[idx])
            w_last_use = {}
            def load_w(idx, widx):
                slot = idx % 2
                if w_last_use.get(slot):
                    ewait(nc.sync, s_pe, w_last_use[slot])
                ewait(nc.sync, s_ppwa, pp_ready['wa'])
                w_done[idx] = s_w[slot].inc(nc.sync.dma_start(wsb[slot][:, :], wab[widx]), 16)

            x_done = {}
            x_free = [0] * 4
            def load_x(seq, tt, slot):
                row0 = seq * SL + tt * 128
                if x_free[slot]:
                    ewait(nc.sync, s_dve, x_free[slot])
                x_done[(seq, tt)] = (slot, s_x[slot].inc(nc.sync.dma_start(x_tile[slot], x_in[row0:row0 + 128, :]), 16))

            def ln_tile(slot, src_sb, dst_bf):
                nc.vector.bn_stats(st6[slot][:, 0:6], src_sb[:, 0:512])
                b1_ = s_dve.inc(nc.vector.bn_stats(st6[slot][:, 6:12], src_sb[:, 512:1024]))
                ewait(nc.vector, s_dve, b1_)
                b2_ = s_dve.inc(nc.vector.bn_aggr(mv[slot][:, :], st6[slot][:, :]))
                ewait(nc.scalar, s_dve, b2_)
                s_act.inc(nc.scalar.activation(stdv[slot][:, :], mv[slot][:, 1:2], AF.Sqrt, bias=eps_col[:, :]))
                a2 = s_act.inc(nc.scalar.mul(negm[slot][:, :], mv[slot][:, 0:1], -1.0))
                ewait(nc.vector, s_act, a2)
                r1 = s_dve.inc(nc.vector.reciprocal(rstd[slot][:, :], stdv[slot][:, :]))
                ewait(nc.vector, s_dve, r1)
                r2 = s_dve.inc(nc.vector.tensor_scalar(
                    dst_bf, src_sb, negm[slot][:, :], rstd[slot][:, :],
                    op0=ALU.add, op1=ALU.mult))
                return r2

            xln_rel = [0] * 4
            x2_rel = [0] * 4
            x2_reader = [0] * 4
            tr_free = [0] * 4
            tr_read = [0] * 4

            for seq in range(NSEQ):
                for i_ in range(4):
                    if tr_free[i_]:
                        ewait(nc.scalar, s_xl2[i_], tr_free[i_])
                    if x2_rel[i_]:
                        ewait(nc.scalar, s_x2w[i_], x2_rel[i_])
                load_w(3 * seq + 0, 0)
                load_w(3 * seq + 1, 1)
                with nc.psum_tensor(f"ps_tr0_{seq}", [128, 1024], BF16) as tr0, \
                     nc.psum_tensor(f"ps_tr1_{seq}", [128, 1024], BF16) as tr1, \
                     nc.psum_tensor(f"ps_qk0_{seq}", [128, 512], F32) as qk0, \
                     nc.psum_tensor(f"ps_qk1_{seq}", [128, 512], F32) as qk1, \
                     nc.psum_tensor(f"ps_v0_{seq}", [128, 1024], F32) as pv0, \
                     nc.psum_tensor(f"ps_v1_{seq}", [128, 1024], F32) as pv1:
                    ps_tr = [tr0, tr1]
                    ps_qk = [qk0, qk1]
                    ps_v = [pv0, pv1]
                    for i in range(3):
                        load_x(seq, i, i % 4)
                    r2s = {}

                    def emit_ln(tt):
                        slot = tt % 4
                        xs, xv = x_done[(seq, tt)]
                        ewait(nc.vector, s_x[xs], xv)
                        if xln_rel[slot]:
                            ewait(nc.vector, s_pe, xln_rel[slot])
                        r2s[tt] = ln_tile(slot, x_tile[slot], xln_t[slot])
                        x_free[slot] = r2s[tt]
                        if tt + 3 < NTT:
                            load_x(seq, tt + 3, (tt + 3) % 4)

                    def emit_tr(tt):
                        slot = tt % 4
                        ewait(nc.tensor, s_dve, r2s[tt])
                        if seq == 0 and tt == 0:
                            ewait(nc.tensor, s_dve, setup_dve)
                        pb = ps_tr[tt % 2]
                        acquire_banks(pb)
                        for i in range(8):
                            mm = nc.tensor.transpose(pb[:, i * 128:(i + 1) * 128], xln_t[slot][:, i * 128:(i + 1) * 128], ident[:, :])
                        xln_rel[slot] = s_pe.inc(mm)
                        ewait(nc.scalar, s_pe, xln_rel[slot])
                        dst = xlnT.ap().rearrange("p (dc t) -> p dc t", dc=NDC)[:, :, tt * 128:(tt + 1) * 128]
                        src = pb.ap().rearrange("p (dc t) -> p dc t", dc=NDC)
                        ev = s_act.inc(nc.scalar.copy(dst, src))
                        release_banks(pb, s_act, ev)

                    n = 0
                    for tt in range(4):
                        emit_ln(tt)
                    for tb in range(4):
                        for tt in range(4 * tb, 4 * tb + 4):
                            emit_tr(tt)
                        if tb + 1 < 4:
                            for tt in range(4 * (tb + 1), 4 * (tb + 1) + 4):
                                emit_ln(tt)
                        col_ready = s_act.n
                        for wi, dstT in ((0, qT), (1, kT)):
                            widx = 3 * seq + wi
                            wslot = widx % 2
                            wait_w(nc.tensor, widx)
                            ewait(nc.tensor, s_act, col_ready)
                            for oc in range(8):
                                pb = ps_qk[n % 2]
                                n += 1
                                acquire_banks(pb)
                                for dc in range(NDC):
                                    mm = nc.tensor.matmul(
                                        pb[:, :],
                                        wsb[wslot][:, dc * 1024 + oc * 128: dc * 1024 + oc * 128 + 128],
                                        xlnT[:, dc * SL + tb * 512: dc * SL + tb * 512 + 512],
                                        start=(dc == 0), stop=(dc == NDC - 1))
                                pe_c = s_pe.inc(mm)
                                if oc == 7 and tb == 3:
                                    w_last_use[wslot] = pe_c
                                    if wi == 0:
                                        load_w(3 * seq + 2, 2)
                                ewait(nc.scalar, s_pe, pe_c)
                                ev = s_act.inc(nc.scalar.copy(dstT[:, oc * SL + tb * 512: oc * SL + tb * 512 + 512], pb[:, :]))
                                release_banks(pb, s_act, ev)
                    ewait(nc.vector, s_pe, xln_rel[2])
                    ewait(nc.vector, s_pe, xln_rel[3])
                    ones_fix = s_dve.inc(nc.vector.memset(vst4[:, :, :, 64:65], 1.0))
                    widx = 3 * seq + 2
                    wslot = widx % 2
                    wait_w(nc.tensor, widx)
                    for tt in range(NTT):
                        pb = ps_v[tt % 2]
                        acquire_banks(pb)
                        for dc in range(NDC):
                            for dh in range(2):
                                mm = nc.tensor.matmul(
                                    pb[:, dh * 512:(dh + 1) * 512],
                                    xlnT[:, dc * SL + tt * 128: dc * SL + tt * 128 + 128],
                                    wsb[wslot][:, dc * 1024 + dh * 512: dc * 1024 + dh * 512 + 512],
                                    start=(dc == 0), stop=(dc == NDC - 1))
                        pe_c = s_pe.inc(mm)
                        if tt == NTT - 1:
                            w_last_use[wslot] = pe_c
                        ewait(nc.scalar, s_pe, pe_c)
                        dst = vst4[:, tt, :, 0:64]
                        src = pb.ap().rearrange("p (h c) -> p h c", h=NH)
                        ev = s_act.inc(nc.scalar.copy(dst, src))
                        release_banks(pb, s_act, ev)

                load_w(6 + seq, 3)
                with nc.psum_tensor(f"ps_sc_{seq}", [128, 2048], F32) as ps_sc, \
                     nc.psum_tensor(f"ps_cx0_{seq}", [128, 260], F32) as pcx0, \
                     nc.psum_tensor(f"ps_cx1_{seq}", [128, 260], F32) as pcx1, \
                     nc.psum_tensor(f"ps_ctr_{seq}", [128, 256], BF16) as ptr:
                    ps_cx = [pcx0, pcx1]
                    pT = [xlnT[:, i * 1024:(i + 1) * 1024] for i in range(3)]
                    nrmc = [xlnT[:, 3072 + k * 128: 3072 + (k + 1) * 128] for k in range(4)]
                    rcol = [xlnT[:, 3584 + 2 * k: 3584 + 2 * k + 2].bitcast(F32) for k in range(8)]
                    QBS = 256
                    NQB2 = SL // QBS
                    blocks = [(hp, qb) for hp in range(NH // 2) for qb in range(NQB2)]
                    strips = []
                    for bi, (hp, qb) in enumerate(blocks):
                        for kc in range(2 * qb + 2):
                            strips.append((bi, hp, qb, kc))
                    NSTR = len(strips)
                    assert NSTR % 2 == 0
                    exp_done = {}
                    msk_done = {}
                    pT_rel = [0, 0, 0]
                    nslot_ctr = [0]
                    nrm_rel = [0] * 4

                    def issue_sT2(qi):
                        i0_, i1_ = 2 * qi, 2 * qi + 1
                        slot = qi % 3
                        sc0 = (qi % 2) * 1024
                        acquire_bank_cols(ps_sc, sc0, sc0 + 1024, 4)
                        for (i_, half) in ((i0_, 0), (i0_, 1), (i1_, 0), (i1_, 1)):
                            bi, hp, qb, kc = strips[i_]
                            oc = hp
                            q0 = oc * SL + qb * QBS
                            k0 = oc * SL + kc * 128
                            hb = half * 64
                            dst = ps_sc[:, sc0 + half * 512 + (i_ % 2) * 256: sc0 + half * 512 + (i_ % 2) * 256 + 256]
                            mm = nc.tensor.matmul(
                                dst,
                                kT[hb:hb + 64, k0:k0 + 128], qT[hb:hb + 64, q0:q0 + QBS],
                                start=True, stop=True)
                        done = s_pe.inc(mm)
                        ewait(nc.scalar, s_pe, done)
                        if pT_rel[slot]:
                            ewait(nc.scalar, s_pe, pT_rel[slot])
                        a = s_act.inc(nc.scalar.activation(pT[slot], ps_sc[:, sc0:sc0 + 1024], AF.Exp, bias=neg12_col[:, :], scale=0.125))
                        release_bank_cols(ps_sc, sc0, sc0 + 1024, 4, s_act, a)
                        exp_done[qi] = a
                        for i_ in (i0_, i1_):
                            bi, hp, qb, kc = strips[i_]
                            if kc >= 2 * qb:
                                o = qb * QBS - kc * 128 + 384
                                c0 = (i_ % 2) * 256
                                ewait(nc.vector, s_act, a)
                                nc.vector.tensor_tensor(
                                    pT[slot][:, c0:c0 + 256], pT[slot][:, c0:c0 + 256],
                                    bmask[:, o:o + 256], op=ALU.mult)
                                msk_done[i_] = s_dve.inc(nc.vector.tensor_tensor(
                                    pT[slot][:, 512 + c0:512 + c0 + 256], pT[slot][:, 512 + c0:512 + c0 + 256],
                                    bmask[:, o:o + 256], op=ALU.mult))

                    def issue_ctx(i):
                        bi, hp, qb, kc = strips[i]
                        slot = (i // 2) % 3
                        c0 = (i % 2) * 256
                        pc = ps_cx[bi % 2]
                        nkc = 2 * qb + 2
                        if kc == 0:
                            acquire_banks(pc)
                        if i in msk_done:
                            ewait(nc.tensor, s_dve, msk_done[i])
                        else:
                            ewait(nc.tensor, s_act, exp_done[i // 2])
                        v0 = kc * NH * 65
                        for half in (0, 1):
                            for hh in (0, 1):
                                mm = nc.tensor.matmul(
                                    pc[:, half * 130 + hh * 65: half * 130 + hh * 65 + 65],
                                    pT[slot][:, hh * 512 + c0 + half * 128: hh * 512 + c0 + half * 128 + 128],
                                    vst[:, v0 + (2 * hp + hh) * 65: v0 + (2 * hp + hh) * 65 + 65],
                                    start=(kc == 0 and half == 0 and hh == 0),
                                    stop=(kc == nkc - 1 and half == 1 and hh == 1))
                        r = s_pe.inc(mm)
                        if i % 2 == 1:
                            pT_rel[slot] = r
                        return r

                    def tail_dve(bi, last_pe):
                        pc = ps_cx[bi % 2]
                        ewait(nc.vector, s_pe, last_pe)
                        ks = []
                        nv = 0
                        for half in (0, 1):
                            k = nslot_ctr[0]
                            nslot_ctr[0] = (k + 1) % 4
                            ks.append(k)
                            if nrm_rel[k]:
                                ewait(nc.vector, s_pe, nrm_rel[k])
                            r0, r1 = rcol[2 * k], rcol[2 * k + 1]
                            nc.vector.reciprocal(r0, pc[:, half * 130 + 64: half * 130 + 65])
                            nc.vector.reciprocal(r1, pc[:, half * 130 + 129: half * 130 + 130])
                            nc.vector.tensor_scalar(
                                nrmc[k][:, 0:64], pc[:, half * 130: half * 130 + 64], r0, None, op0=ALU.mult)
                            nv = s_dve.inc(nc.vector.tensor_scalar(
                                nrmc[k][:, 64:128], pc[:, half * 130 + 65: half * 130 + 129], r1, None, op0=ALU.mult))
                        release_banks(pc, s_dve, nv)
                        return (bi, nv, ks)

                    def tail_pe(bi, nv, ks):
                        hp, qb = blocks[bi]
                        q0 = hp * SL + qb * QBS
                        acquire_banks(ptr)
                        ewait(nc.tensor, s_dve, nv)
                        for half in (0, 1):
                            mm = nc.tensor.transpose(ptr[:, half * 128:(half + 1) * 128], nrmc[ks[half]][:, :], ident[:, :])
                        tr = s_pe.inc(mm)
                        nrm_rel[ks[0]] = tr
                        nrm_rel[ks[1]] = tr
                        ewait(nc.scalar, s_pe, tr)
                        cc = s_act.inc(nc.scalar.copy(qT[:, q0:q0 + QBS], ptr[:, :]))
                        release_banks(ptr, s_act, cc)
                        return cc

                    NQUAD = NSTR // 2
                    ewait(nc.tensor, s_dve, ones_fix)
                    issue_sT2(0)
                    pend = []
                    ctxT_ready = 0
                    for qi in range(NQUAD + 1):
                        if qi + 1 < NQUAD:
                            issue_sT2(qi + 1)
                        if qi < 1:
                            continue
                        qc = qi - 1
                        for i in (2 * qc, 2 * qc + 1):
                            bi, hp, qb, kc = strips[i]
                            last_pe = issue_ctx(i)
                            if kc == 2 * qb + 1:
                                pend.append(tail_dve(bi, last_pe))
                                if len(pend) > 1:
                                    ctxT_ready = tail_pe(*pend.pop(0))
                    while pend:
                        ctxT_ready = tail_pe(*pend.pop(0))
                ctxT = qT

                widx = 6 + seq
                wslot = widx % 2
                with nc.psum_tensor(f"ps_o0_{seq}", [128, 1024], F32) as o0, \
                     nc.psum_tensor(f"ps_o1_{seq}", [128, 1024], F32) as o1, \
                     nc.psum_tensor(f"ps_tr2_{seq}", [128, 1024], BF16) as tr2, \
                     nc.psum_tensor(f"ps_tr3_{seq}", [128, 1024], BF16) as tr3, \
                     nc.psum_tensor(f"ps_xr0_{seq}", [16, 128], F32) as xr0, \
                     nc.psum_tensor(f"ps_xr1_{seq}", [16, 128], F32) as xr1:
                    ps_o = [o0, o1]
                    ps_tr = [tr2, tr3]
                    ps_xr = [xr0, xr1]
                    wait_w(nc.tensor, widx)
                    ewait(nc.tensor, s_act, ctxT_ready)
                    ewait(nc.sync, s_act, ctxT_ready)
                    LAGC = 3
                    r2_of = {}
                    for step in range(NTT + LAGC):
                        if step < NTT:
                            tt = step
                            slot = tt % 4
                            if tt == 0:
                                for i in range(3):
                                    load_x(seq, i, i)
                            if tt + 3 < NTT:
                                load_x(seq, tt + 3, (tt + 3) % 4)
                            pb = ps_o[tt % 2]
                            acquire_banks(pb)
                            for dc in range(NDC):
                                for dh in range(2):
                                    mm = nc.tensor.matmul(
                                        pb[:, dh * 512:(dh + 1) * 512],
                                        ctxT[:, dc * SL + tt * 128: dc * SL + tt * 128 + 128],
                                        wsb[wslot][:, dc * 1024 + dh * 512: dc * 1024 + dh * 512 + 512],
                                        start=(dc == 0), stop=(dc == NDC - 1))
                            pe_c = s_pe.inc(mm)
                            if tt == NTT - 1:
                                w_last_use[wslot] = pe_c
                            ewait(nc.vector, s_pe, pe_c)
                            xs, xv = x_done[(seq, tt)]
                            ewait(nc.vector, s_x[xs], xv)
                            if x2_rel[slot]:
                                ewait(nc.vector, s_x2w[slot], x2_rel[slot])
                            if x2_reader[slot]:
                                ewait(nc.vector, s_dve, x2_reader[slot])
                            rs = s_dve.inc(nc.vector.tensor_tensor(x2_sb[slot], pb[:, :], x_tile[slot], op=ALU.add))
                            x_free[slot] = rs
                            release_banks(pb, s_dve, rs)
                            row0 = seq * SL + tt * 128
                            ewait(nc.sync, s_dve, rs)
                            x2_rel[slot] = s_x2w[slot].inc(nc.sync.dma_start(x2s[row0:row0 + 128, :], x2_sb[slot]), 16)
                            if xln_rel[slot]:
                                ewait(nc.vector, s_pe, xln_rel[slot])
                            r2_of[tt] = ln_tile(slot, x2_sb[slot], xln_t[slot])
                            x2_reader[slot] = r2_of[tt]
                        if step >= LAGC:
                            tt = step - LAGC
                            slot = tt % 4
                            pt = ps_tr[tt % 2]
                            acquire_banks(pt)
                            ewait(nc.tensor, s_dve, r2_of[tt])
                            for i in range(8):
                                mm = nc.tensor.transpose(pt[:, i * 128:(i + 1) * 128], xln_t[slot][:, i * 128:(i + 1) * 128], ident[:, :])
                            xln_rel[slot] = s_pe.inc(mm)
                            ewait(nc.scalar, s_pe, xln_rel[slot])
                            if tr_free[slot]:
                                ewait(nc.scalar, s_xl2[slot], tr_free[slot])
                            if tr_read[slot]:
                                ewait(nc.scalar, s_pe, tr_read[slot])
                            ev = s_act.inc(nc.scalar.copy(trbuf[slot].rearrange("p (dc t) -> p dc t", dc=NDC),
                                                          pt.ap().rearrange("p (dc t) -> p dc t", dc=NDC)))
                            release_banks(pt, s_act, ev)
                            ewait(nc.sync, s_act, ev)
                            t0c = seq * SL + tt * 128
                            tr_free[slot] = s_xl2[slot].inc(nc.sync.dma_start(
                                xl2s[:, :, t0c:t0c + 128].rearrange("dc p t -> p dc t"),
                                trbuf[slot].rearrange("p (dc t) -> p dc t", dc=NDC)), 16)
                            pxr = ps_xr[tt % 2]
                            acquire_banks(pxr)
                            ewait(nc.tensor, s_act, ev)
                            ewait(nc.tensor, s_pp, 16)
                            for dc in range(NDC):
                                mm = nc.tensor.matmul(
                                    pxr[:, :],
                                    wgr_sb[:, dc * 16:(dc + 1) * 16],
                                    trbuf[slot][:, dc * 128:(dc + 1) * 128],
                                    start=(dc == 0), stop=(dc == NDC - 1))
                            xr_pe = s_pe.inc(mm)
                            tr_read[slot] = xr_pe
                            ewait(nc.scalar, s_pe, xr_pe)
                            ev2 = s_act.inc(nc.scalar.copy(xrT[:, t0c:t0c + 128], pxr[:, :]))
                            release_banks(pxr, s_act, ev2)

            NGS = 4
            gmax = [A(nc.sbuf_tensor(f"gmax{i}", [128, 1], F32)) for i in range(NGS)]
            gnegm = [A(nc.sbuf_tensor(f"gnegm{i}", [128, 1], F32)) for i in range(NGS)]
            gexp = [A(nc.sbuf_tensor(f"gexp{i}", [128, 8], F32)) for i in range(NGS)]
            gsum = [A(nc.sbuf_tensor(f"gsum{i}", [128, 1], F32)) for i in range(NGS)]
            xr_ready = s_act.n
            with nc.psum_tensor("ps_lg0", [128, 8], F32) as lg0, \
                 nc.psum_tensor("ps_lg1", [128, 8], F32) as lg1, \
                 nc.psum_tensor("ps_lg2", [128, 8], F32) as lg2, \
                 nc.psum_tensor("ps_lg3", [128, 8], F32) as lg3, \
                 nc.psum_tensor("ps_oh0", [8, 128], F32) as poh0, \
                 nc.psum_tensor("ps_oh1", [8, 128], F32) as poh1:
                ps_lg = [lg0, lg1, lg2, lg3]
                ps_oh = [poh0, poh1]
                ewait(nc.tensor, s_act, xr_ready)
                ewait(nc.tensor, s_act, wgu_ready)
                a1_slot = [0] * NGS
                sm_slot = [0] * NGS
                for tt in range(32):
                    slot = tt % NGS
                    pb = ps_lg[slot]
                    acquire_banks(pb)
                    mm = nc.tensor.matmul(pb[:, :], xrT[:, tt * 128:(tt + 1) * 128], wguT[:, :], start=True, stop=True)
                    pe_c = s_pe.inc(mm)
                    ewait(nc.vector, s_pe, pe_c)
                    if a1_slot[slot]:
                        ewait(nc.vector, s_act, a1_slot[slot])
                    m1 = s_dve.inc(nc.vector.reduce_max(gmax[slot][:, :], pb[:, :], axis=AX.X))
                    ewait(nc.scalar, s_dve, m1)
                    a1 = s_act.inc(nc.scalar.mul(gnegm[slot][:, :], gmax[slot][:, :], -1.0))
                    a1_slot[slot] = a1
                    ewait(nc.scalar, s_act, a1)
                    if sm_slot[slot]:
                        ewait(nc.scalar, s_dve, sm_slot[slot])
                    a2 = s_act.inc(nc.scalar.activation(gexp[slot][:, :], pb[:, :], AF.Exp, bias=gnegm[slot][:, :]))
                    ewait(nc.vector, s_act, a2)
                    oh1 = s_dve.inc(nc.vector.tensor_scalar(oh_sb[:, tt * 8:(tt + 1) * 8], pb[:, :], gmax[slot][:, :], None, op0=ALU.is_equal))
                    release_banks(pb, s_dve, oh1)
                    po = ps_oh[tt % 2]
                    acquire_banks(po)
                    ewait(nc.tensor, s_dve, oh1)
                    trh = s_pe.inc(nc.tensor.transpose(po[:, :], oh_sb[:, tt * 8:(tt + 1) * 8], ident32[:, :]))
                    ewait(nc.scalar, s_pe, trh)
                    ohc = s_act.inc(nc.scalar.copy(ohT_sb[0:8, tt * 128:(tt + 1) * 128], po[:, :]))
                    release_banks(po, s_act, ohc)
                    sm = s_dve.inc(nc.vector.reduce_sum(gsum[slot][:, :], gexp[slot][:, :], axis=AX.X))
                    sm_slot[slot] = sm
                    ewait(nc.vector, s_dve, sm)
                    s_dve.inc(nc.vector.reciprocal(sc_sb[:, tt:tt + 1], gsum[slot][:, :]))
            ohT_ready = s_act.n
            gating_done = s_dve.n
            act_fin = s_act.n
            ewait(nc.tensor, s_dve, gating_done)
            ewait(nc.scalar, s_dve, gating_done)
            ewait(nc.sync, s_st, s_st.n)
            for i_ in range(4):
                ewait(nc.sync, s_xl2[i_], s_xl2[i_].n)
                ewait(nc.sync, s_x2w[i_], s_x2w[i_].n)
            ewait(nc.sync, s_dve, gating_done)
            ewait(nc.sync, s_act, act_fin)
            ewait(nc.gpsimd, s_dve, gating_done)
            ewait(nc.gpsimd, s_act, act_fin)

        with ExitStack() as fst:
            F = fst.enter_context
            xbuf = [F(nc.sbuf_tensor(f"xbuf{i}", [128, NDC * TCH], BF16)) for i in range(2)]
            hT = F(nc.sbuf_tensor("hT", [128, NHC * TCH], BF16))
            w1s = [F(nc.sbuf_tensor(f"w1s{i}", [128, 4096], BF16)) for i in range(3)]
            w2s = [F(nc.sbuf_tensor(f"w2s{i}", [128, 4096], BF16)) for i in range(3)]
            acc = F(nc.sbuf_tensor("acc", [128, NSUB * 1024], F32))
            b1sb = F(nc.sbuf_tensor("b1sb", [128, E * 32], F32))

            x2b = [F(nc.sbuf_tensor(f"x2b{i}", [128, 1024], F32)) for i in range(NSUB)]
            osb = [F(nc.sbuf_tensor(f"osb{i}", [128, 1024], F32)) for i in range(NSUB)]

            s_ld.inc(nc.sync.dma_start(b1sb.ap().rearrange("p (e c) -> p e c", e=E), b1r.rearrange("e p c -> p e c")), 16)
            s_pp.inc(nc.gpsimd.dma_start(b2a_sb[:, :], b2f.rearrange("o (e d) -> (o e) d", e=E)), 16)
            const_ld = s_ld.n
            const_pp = s_pp.n

            xln_done = [0] * NTC
            def load_xln(tc):
                src = xl2s[:, :, tc * TCH:(tc + 1) * TCH].rearrange("dc p t -> p dc t")
                dst = xbuf[tc % 2].ap().rearrange("p (dc t) -> p dc t", dc=NDC)
                xln_done[tc] = s_xb[tc % 2].inc(nc.sync.dma_start(dst, src), 16)
            load_xln(0)

            xln_last_use = [0] * NTC
            w1_done = {}
            w1_last_use = {}
            w2_done = {}
            w2_last_use = {}
            gelu_done = [0] * NHC
            last_stt = 0
            acc_writer = [0] * NSUB
            x2b_free = [0] * NSUB
            osb_store_done = [0] * NSUB
            pending_stores = []
            x2_done = {}
            w1_issued = set()
            w2_issued = set()
            w1_slot_of = {}
            w2_slot_of = {}
            w1_ctr = [0]
            w2_ctr = [0]

            def issue_w1(tc_, e_, g_):
                if (tc_, e_, g_) in w1_issued or tc_ >= NTC:
                    return
                w1_issued.add((tc_, e_, g_))
                slot = w1_ctr[0] % 3
                w1_ctr[0] += 1
                w1_slot_of[(tc_, e_, g_)] = slot
                prev_use = w1_last_use.get(slot, 0)
                if prev_use:
                    ewait(nc.sync, s_pe, prev_use)
                ewait(nc.sync, s_ppe[e_], pp_ready[('e', e_)])
                w1_done[(tc_, e_, g_)] = (slot, s_w1[slot].inc(nc.sync.dma_start(w1s[slot][:, :], w1b[e_, g_]), 16))

            def issue_w2(tc_, e_, j_):
                if (tc_, e_, j_) in w2_issued or tc_ >= NTC:
                    return
                w2_issued.add((tc_, e_, j_))
                slot = w2_ctr[0] % 3
                w2_ctr[0] += 1
                w2_slot_of[(tc_, e_, j_)] = slot
                prev_use = w2_last_use.get(slot, 0)
                if prev_use:
                    ewait(nc.gpsimd, s_pe, prev_use)
                ewait(nc.gpsimd, s_ppe[e_], pp_ready[('e', e_)])
                src = w2b[e_, 4 * j_:4 * j_ + 4].rearrange("h p d -> p h d")
                dst = w2s[slot].ap().rearrange("p (h d) -> p h d", h=4)
                w2_done[(tc_, e_, j_)] = (slot, s_w2[slot].inc(nc.gpsimd.dma_start(dst, src), 16))

            def flush_stores():
                while pending_stores:
                    tc_, tsub_, w = pending_stores.pop(0)
                    row0 = tc_ * TCH + tsub_ * 128
                    ewait(nc.sync, s_dve, w)
                    osb_store_done[tsub_] = s_st.inc(nc.sync.dma_start(out[row0:row0 + 128, :], osb[tsub_][:, :]), 16)

            first_e_mm = True
            ewait(nc.tensor, s_ld, const_ld)
            ewait(nc.tensor, s_pp, const_pp)
            ewait(nc.vector, s_ld, const_ld)
            ewait(nc.scalar, s_ld, const_ld)

            for tc in range(NTC):
                if tc + 1 < NTC:
                    if tc >= 1:
                        ewait(nc.sync, s_pe, xln_last_use[tc - 1])
                    load_xln(tc + 1)
                xb = xbuf[tc % 2]
                acc_init = [0] * NSUB
                with nc.psum_tensor(f"ps_b2a_{tc}", [128, 1024], F32) as pb2a, \
                     nc.psum_tensor(f"ps_b2b_{tc}", [128, 1024], F32) as pb2b:
                    pb2l = [pb2a, pb2b]
                    ewait(nc.tensor, s_act, ohT_ready)
                    ewait(nc.tensor, s_pp, const_pp)
                    for tsub in range(NSUB):
                        pb = pb2l[tsub % 2]
                        acquire_banks(pb)
                        t0g = (tc * NSUB + tsub) * 128
                        for dh in range(2):
                            mm = nc.tensor.matmul(
                                pb[:, dh * 512:(dh + 1) * 512],
                                ohT_sb[0:8, t0g:t0g + 128],
                                b2a_sb[0:8, dh * 512:(dh + 1) * 512],
                                start=True, stop=True)
                        pe_b = s_pe.inc(mm)
                        ewait(nc.scalar, s_pe, pe_b)
                        if acc_writer[tsub]:
                            ewait(nc.scalar, s_dve, acc_writer[tsub])
                        ai = s_act.inc(nc.scalar.copy(acc[:, tsub * 1024:(tsub + 1) * 1024], pb[:, :]))
                        release_banks(pb, s_act, ai)
                        acc_init[tsub] = ai
                for e in range(E):
                    if e == 1:
                        flush_stores()
                    if e == E - 1:
                        for tsub in range(NSUB):
                            row0 = tc * TCH + tsub * 128
                            if x2b_free[tsub]:
                                ewait(nc.sync, s_dve, x2b_free[tsub])
                            x2_done[(tc, tsub)] = s_x2b.inc(nc.sync.dma_start(x2b[tsub][:, :], x2s[row0:row0 + 128, :]), 16)
                    with nc.psum_tensor(f"ps_f1a_{tc}_{e}", [128, 512], F32) as f1a, \
                         nc.psum_tensor(f"ps_f1b_{tc}_{e}", [128, 512], F32) as f1b:
                        ps_f1 = [f1a, f1b]
                        issue_w1(tc, e, 0)
                        issue_w1(tc, e, 1)
                        for g in range(NG):
                            slot = w1_slot_of[(tc, e, g)]
                            if g + 2 < NG:
                                issue_w1(tc, e, g + 2)
                            else:
                                ntc_ = tc + 1 if e == E - 1 else tc
                                issue_w1(ntc_, (e + 1) % E, g + 2 - NG)
                            w1s_, w1v_ = w1_done[(tc, e, g)]
                            ewait(nc.tensor, s_w1[w1s_], w1v_)
                            if g == 0:
                                ewait(nc.tensor, s_xb[tc % 2], xln_done[tc])
                            for hc in range(4 * g, 4 * g + 4):
                                pb = ps_f1[hc % 2]
                                acquire_banks(pb)
                                for dc in range(NDC):
                                    mm = nc.tensor.matmul(
                                        pb[:, :],
                                        w1s[slot][:, dc * 512 + (hc % 4) * 128: dc * 512 + (hc % 4) * 128 + 128],
                                        xb[:, dc * 512:(dc + 1) * 512],
                                        start=(dc == 0), stop=(dc == NDC - 1))
                                w1_last_use[slot] = s_pe.inc(mm)
                                if hc == NHC - 1:
                                    xln_last_use[tc] = w1_last_use[slot]
                                ewait(nc.scalar, s_pe, w1_last_use[slot])
                                g_i = nc.scalar.activation(
                                    hT[:, hc * TCH:(hc + 1) * TCH], pb[:, :], AF.Gelu,
                                    bias=b1sb[:, e * 32 + hc: e * 32 + hc + 1])
                                gelu_done[hc] = s_act.inc(g_i)
                                release_banks(pb, s_act, gelu_done[hc])
                        first_e_mm = False
                    with nc.psum_tensor(f"ps_f2a_{tc}_{e}", [128, 1024], F32) as f2a, \
                         nc.psum_tensor(f"ps_f2b_{tc}_{e}", [128, 1024], F32) as f2b, \
                         nc.psum_tensor(f"ps_f2c_{tc}_{e}", [128, 1024], F32) as f2c, \
                         nc.psum_tensor(f"ps_f2d_{tc}_{e}", [128, 1024], F32) as f2d:
                        ps_f2 = [f2a, f2b, f2c, f2d]
                        f2_stop = [0] * NSUB
                        issue_w2(tc, e, 0)
                        issue_w2(tc, e, 1)
                        for j in range(NHC // 4):
                            slot = w2_slot_of[(tc, e, j)]
                            if j + 2 < NHC // 4:
                                issue_w2(tc, e, j + 2)
                            else:
                                ntc_ = tc + 1 if e == E - 1 else tc
                                issue_w2(ntc_, (e + 1) % E, j + 2 - NHC // 4)
                            w2s_, w2v_ = w2_done[(tc, e, j)]
                            ewait(nc.tensor, s_w2[w2s_], w2v_)
                            for hc in range(4 * j, 4 * j + 4):
                                for tsub in range(NSUB):
                                    if hc == 0:
                                        acquire_banks(ps_f2[tsub])
                                    for dh in range(2):
                                        mm = nc.tensor.matmul(
                                            ps_f2[tsub][:, dh * 512:(dh + 1) * 512],
                                            hT[:, hc * TCH + tsub * 128: hc * TCH + tsub * 128 + 128],
                                            w2s[slot][:, (hc % 4) * 1024 + dh * 512: (hc % 4) * 1024 + (dh + 1) * 512],
                                            start=(hc == 0), stop=(hc == NHC - 1 and dh == 1))
                                    if hc == NHC - 1:
                                        f2_stop[tsub] = s_pe.inc(mm)
                            if f2_stop[NSUB - 1]:
                                w2_last_use[slot] = f2_stop[NSUB - 1] if 4 * j + 3 == NHC - 1 else s_pe.inc(mm)
                            else:
                                w2_last_use[slot] = s_pe.inc(mm)
                        for tsub in range(NSUB):
                            ewait(nc.vector, s_pe, f2_stop[tsub])
                            mcol = oh_sb[:, (tc * NSUB + tsub) * E + e: (tc * NSUB + tsub) * E + e + 1]
                            if e == 0:
                                ewait(nc.vector, s_act, acc_init[tsub])
                            halves = ((0, 512), (512, 1024)) if tsub == 0 else ((0, 1024),)
                            for (c0, c1) in halves:
                                a_sl = acc[:, tsub * 1024 + c0: tsub * 1024 + c1]
                                i = nc.vector.scalar_tensor_tensor(a_sl, ps_f2[tsub][:, c0:c1], mcol, a_sl, op0=ALU.mult, op1=ALU.add)
                                w = s_dve.inc(i)
                                release_bank_cols(ps_f2[tsub], c0, c1, 4, s_dve, w)
                            acc_writer[tsub] = w
                        last_stt = acc_writer[NSUB - 1]
                for tsub in range(NSUB):
                    ewait(nc.vector, s_x2b, x2_done[(tc, NSUB - 1)])
                    ewait(nc.vector, s_dve, acc_writer[tsub])
                    if osb_store_done[tsub]:
                        ewait(nc.vector, s_st, osb_store_done[tsub])
                    sc = sc_sb[:, tc * NSUB + tsub: tc * NSUB + tsub + 1]
                    i = nc.vector.scalar_tensor_tensor(osb[tsub][:, :], acc[:, tsub * 1024:(tsub + 1) * 1024], sc, x2b[tsub][:, :], op0=ALU.mult, op1=ALU.add)
                    w = s_dve.inc(i)
                    pending_stores.append((tc, tsub, w))
                    acc_writer[tsub] = w
                    x2b_free[tsub] = w
            flush_stores()
            nc.sync.wait_ge(s_st.h, s_st.n)
    return nc


_NC_CACHE = {}

def _get_nc():
    if "nc" not in _NC_CACHE:
        _NC_CACHE["nc"] = build()
    return _NC_CACHE["nc"]


def kernel(L0_attn_x, L0_attn_W_Q, L0_attn_W_K, L0_attn_W_V, L0_attn_W_O,
           L0_ffn_wg_red, L0_ffn_wg, L0_ffn_w1, L0_ffn_b1, L0_ffn_w2, L0_ffn_b2):
    x = np.asarray(L0_attn_x, dtype=np.float32)
    WQ = np.asarray(L0_attn_W_Q, dtype=np.float32)
    WK = np.asarray(L0_attn_W_K, dtype=np.float32)
    WV = np.asarray(L0_attn_W_V, dtype=np.float32)
    WO = np.asarray(L0_attn_W_O, dtype=np.float32)
    wg_red = np.asarray(L0_ffn_wg_red, dtype=np.float32)
    wg = np.asarray(L0_ffn_wg, dtype=np.float32)
    w1 = np.ascontiguousarray(np.asarray(L0_ffn_w1, dtype=np.float32))
    b1 = np.asarray(L0_ffn_b1, dtype=np.float32)
    w2 = np.ascontiguousarray(np.asarray(L0_ffn_w2, dtype=np.float32))
    b2 = np.asarray(L0_ffn_b2, dtype=np.float32)

    wqt = np.ascontiguousarray(WQ.T)
    wkt = np.ascontiguousarray(WK.T)
    wvt = np.ascontiguousarray(WV.T)
    wot = np.ascontiguousarray(WO.T)
    wgr = np.ascontiguousarray(wg_red.T)
    b1r = np.ascontiguousarray(b1.reshape(E, 32, 128).transpose(0, 2, 1))
    b2f = np.ascontiguousarray(b2.reshape(1, E * D))

    nc = _get_nc()
    shared = dict(wqt=wqt, wkt=wkt, wvt=wvt, wot=wot, wgr=wgr, wg=wg,
                  w1=w1, b1r=b1r, w2=w2, b2f=b2f)
    in_maps = []
    for c in range(8):
        shard = np.ascontiguousarray(x[2 * c:2 * c + 2].reshape(T, D))
        m = dict(shared)
        m["x_in"] = shard
        in_maps.append(m)
    res = run_bass_kernel_spmd(nc, in_maps, core_ids=list(range(8)))
    outs = [np.asarray(res.results[c]["out"], dtype=np.float32).reshape(2, SL, D) for c in range(8)]
    return np.concatenate(outs, axis=0)


if __name__ == "__main__":
    import importlib.util
    spec = importlib.util.spec_from_file_location("_ref", "/root/problem/reference.py")
    ref = importlib.util.module_from_spec(spec)
    spec.loader.exec_module(ref)
    inputs = {k: np.asarray(v) for k, v in ref.setup_inputs().items()}
    got = kernel(**inputs)
    want = np.asarray(ref.reference(**inputs))
    err = got - want
    ratio = float((err ** 2).mean() / (want ** 2).mean())
    print("residual variance ratio:", ratio, "(bar 2.27e-04)")
    print("PASS" if ratio < 2.27e-4 else "FAIL")


# version 27
# speedup vs baseline: 1.1317x; 1.0579x over previous
import numpy as np
from contextlib import ExitStack
import concourse.bass as bass
import concourse.mybir as mybir
from concourse.bass_utils import run_bass_kernel_spmd

F32 = mybir.dt.float32
BF16 = mybir.dt.bfloat16
I32 = mybir.dt.int32
AF = mybir.ActivationFunctionType
ALU = mybir.AluOpType
AX = mybir.AxisListType

NSEQ, SL, D, NH, DK = 2, 2048, 1024, 16, 64
T = NSEQ * SL
NDC = D // 128
NTT = SL // 128
NQB = SL // 512
NKC = SL // 128
E = 8
HID = 4096
NHC = HID // 128
NG = 8
TCH = 512
NTC = T // TCH
NSUB = TCH // 128


class Sem:
    def __init__(self, stack, nc, name):
        self.h = stack.enter_context(nc.semaphore(name))
        self.n = 0
    def inc(self, instr, k=1):
        instr.then_inc(self.h, k)
        self.n += k
        return self.n


def build():
    nc = bass.Bass("TRN2", target_bir_lowering=False, debug=False)
    x_in = nc.dram_tensor("x_in", [T, D], F32, kind="ExternalInput").ap()
    wqt = nc.dram_tensor("wqt", [D, D], F32, kind="ExternalInput").ap()
    wkt = nc.dram_tensor("wkt", [D, D], F32, kind="ExternalInput").ap()
    wvt = nc.dram_tensor("wvt", [D, D], F32, kind="ExternalInput").ap()
    wot = nc.dram_tensor("wot", [D, D], F32, kind="ExternalInput").ap()
    wgr = nc.dram_tensor("wgr", [D, 16], F32, kind="ExternalInput").ap()
    wg = nc.dram_tensor("wg", [8, 16], F32, kind="ExternalInput").ap()
    w1 = nc.dram_tensor("w1", [E, D, HID], F32, kind="ExternalInput").ap()
    b1r = nc.dram_tensor("b1r", [E, 128, 32], F32, kind="ExternalInput").ap()
    w2 = nc.dram_tensor("w2", [E, HID, D], F32, kind="ExternalInput").ap()
    b2f = nc.dram_tensor("b2f", [1, E * D], F32, kind="ExternalInput").ap()
    out = nc.dram_tensor("out", [T, D], F32, kind="ExternalOutput").ap()
    x2s = nc.dram_tensor("x2s", [T, D], F32, kind="Internal").ap()
    xl2s = nc.dram_tensor("xl2s", [NDC, 128, T], BF16, kind="Internal").ap()
    wab = nc.dram_tensor("wab", [4, 128, NDC * 1024], BF16, kind="Internal").ap()
    w1b = nc.dram_tensor("w1b", [E, NG, 128, 4096], BF16, kind="Internal").ap()
    w2b = nc.dram_tensor("w2b", [E, NHC, 128, 1024], BF16, kind="Internal").ap()

    with ExitStack() as st:
        S = st.enter_context
        ident = S(nc.sbuf_tensor("ident", [128, 128], BF16))
        ident32 = S(nc.sbuf_tensor("ident32", [128, 128], F32))
        onesb = S(nc.sbuf_tensor("onesb", [1, 128], BF16))
        eps_col = S(nc.sbuf_tensor("eps_col", [128, 1], F32))
        neg12_col = S(nc.sbuf_tensor("neg12_col", [128, 1], F32))
        wgr_sb = S(nc.sbuf_tensor("wgr_sb", [128, NDC * 16], BF16))
        wg_sb = S(nc.sbuf_tensor("wg_sb", [8, 16], F32))
        wgu_bf = S(nc.sbuf_tensor("wgu_bf", [8, 16], BF16))
        wguT = S(nc.sbuf_tensor("wguT", [16, 8], BF16))
        xrT = S(nc.sbuf_tensor("xrT", [16, T], BF16))
        oh_sb = S(nc.sbuf_tensor("oh_sb", [128, 32 * E], F32))
        ohT_sb = S(nc.sbuf_tensor("ohT_sb", [8, T], BF16))
        b2a_sb = S(nc.sbuf_tensor("b2a_sb", [8, 1024], BF16))
        sc_sb = S(nc.sbuf_tensor("sc_sb", [128, 32], F32))
        sq = S(nc.sbuf_tensor("sq", [8, 16], F32))
        nrm = S(nc.sbuf_tensor("nrm", [8, 1], F32))
        nrm2 = S(nc.sbuf_tensor("nrm2", [8, 1], F32))
        rn = S(nc.sbuf_tensor("rn", [8, 1], F32))

        s_ld = Sem(st, nc, "s_ld")
        s_x = [Sem(st, nc, f"s_x{i}") for i in range(4)]
        s_w = [Sem(st, nc, f"s_w{i}") for i in range(2)]
        s_w1 = [Sem(st, nc, f"s_w1{i}") for i in range(3)]
        s_w2 = [Sem(st, nc, f"s_w2{i}") for i in range(3)]
        s_xb = [Sem(st, nc, f"s_xb{i}") for i in range(2)]
        s_x2b = Sem(st, nc, "s_x2b")
        s_ppe = [Sem(st, nc, f"s_ppe{i}") for i in range(E)]
        s_ppwa = Sem(st, nc, "s_ppwa")
        s_xl2 = [Sem(st, nc, f"s_xl2{i}") for i in range(4)]
        s_x2w = [Sem(st, nc, f"s_x2w{i}") for i in range(4)]
        s_pp = Sem(st, nc, "s_pp")
        s_pps = Sem(st, nc, "s_pps")
        s_ldw = Sem(st, nc, "s_ldw")
        s_pe = Sem(st, nc, "s_pe")
        s_act = Sem(st, nc, "s_act")
        s_dve = Sem(st, nc, "s_dve")
        s_st = Sem(st, nc, "s_st")
        s_gp = Sem(st, nc, "s_gp")

        wait_hwm = {}
        def ewait(eng, sem, val):
            if val <= 0:
                return
            hw = wait_hwm.setdefault(id(eng), {})
            if hw.get(id(sem.h), 0) >= val:
                return
            eng.wait_ge(sem.h, val)
            hw[id(sem.h)] = val
        bank_release = {}
        def nbanks(h):
            return max(1, (int(h.ap().free_nbytes()) + 2047) // 2048)
        def acquire_banks(h):
            m = nc.lookup_mloc(h)
            for b in range(m.bank, m.bank + nbanks(h)):
                r = bank_release.get(b)
                if r is not None:
                    ewait(nc.tensor, r[0], r[1])
        def release_banks(h, sem, val):
            m = nc.lookup_mloc(h)
            for b in range(m.bank, m.bank + nbanks(h)):
                bank_release[b] = (sem, val)
        def acquire_bank_cols(h, c0, c1, elt):
            m = nc.lookup_mloc(h)
            b0 = m.bank + (c0 * elt) // 2048
            b1 = m.bank + ((c1 * elt) - 1) // 2048
            for b in range(b0, b1 + 1):
                r = bank_release.get(b)
                if r is not None:
                    ewait(nc.tensor, r[0], r[1])
        def release_bank_cols(h, c0, c1, elt, sem, val):
            m = nc.lookup_mloc(h)
            b0 = m.bank + (c0 * elt) // 2048
            b1 = m.bank + ((c1 * elt) - 1) // 2048
            for b in range(b0, b1 + 1):
                bank_release[b] = (sem, val)

        with ExitStack() as ast_:
            A = ast_.enter_context
            bmask = A(nc.sbuf_tensor("bmask", [128, 896], BF16))
            xlnT = A(nc.sbuf_tensor("xlnT", [128, NDC * SL], BF16))
            qT = A(nc.sbuf_tensor("qT", [128, NDC * SL], BF16))
            kT = A(nc.sbuf_tensor("kT", [128, NDC * SL], BF16))
            vst = A(nc.sbuf_tensor("vst", [128, NKC * NH * 65], BF16))
            wsb = [A(nc.sbuf_tensor(f"wsb{i}", [128, NDC * 1024], BF16)) for i in range(2)]
            st6 = [A(nc.sbuf_tensor(f"st6{i}", [128, 12], F32)) for i in range(4)]
            mv = [A(nc.sbuf_tensor(f"mv{i}", [128, 2], F32)) for i in range(4)]
            stdv = [A(nc.sbuf_tensor(f"stdv{i}", [128, 1], F32)) for i in range(4)]
            negm = [A(nc.sbuf_tensor(f"negm{i}", [128, 1], F32)) for i in range(4)]
            rstd = [A(nc.sbuf_tensor(f"rstd{i}", [128, 1], F32)) for i in range(4)]
            x_tile_h = [A(nc.sbuf_tensor(f"x_tile{i}", [128, D], F32)) for i in range(2)]
            xln_t_h = [A(nc.sbuf_tensor(f"xln_t{i}", [128, D], BF16)) for i in range(2)]
            x_tile = [x_tile_h[0][:, :], x_tile_h[1][:, :],
                      vst[:, 2048:4096].bitcast(F32), vst[:, 4096:6144].bitcast(F32)]
            xln_t = [xln_t_h[0][:, :], xln_t_h[1][:, :],
                     vst[:, 0:1024], vst[:, 1024:2048]]
            x2_sb = [kT[:, i * 2048:(i + 1) * 2048].bitcast(F32) for i in range(4)]
            trbuf = [kT[:, 8192 + i * 1024: 8192 + (i + 1) * 1024] for i in range(4)]

            with nc.sbuf_tensor("iota32", [128, 448], I32) as iota32:
                g1 = s_gp.inc(nc.gpsimd.iota(iota32[:, :], [[1, 448]], base=-384, channel_multiplier=-1))
                ewait(nc.vector, s_gp, g1)
                nc.vector.tensor_scalar(bmask[:, 0:448], iota32[:, :], 0, None, op0=ALU.is_ge)
                nc.vector.tensor_scalar(ident32[:, 0:64], iota32[:, 384:448], 0, None, op0=ALU.is_equal)
                d0 = s_dve.inc(nc.vector.tensor_scalar(ident[:, 0:64], iota32[:, 384:448], 0, None, op0=ALU.is_equal))
                ewait(nc.gpsimd, s_dve, d0)
                g2 = s_gp.inc(nc.gpsimd.iota(iota32[:, :], [[1, 448]], base=64, channel_multiplier=-1))
                ewait(nc.vector, s_gp, g2)
                nc.vector.tensor_scalar(bmask[:, 448:896], iota32[:, :], 0, None, op0=ALU.is_ge)
                nc.vector.tensor_scalar(ident32[:, 64:128], iota32[:, 0:64], 0, None, op0=ALU.is_equal)
                d1 = s_dve.inc(nc.vector.tensor_scalar(ident[:, 64:128], iota32[:, 0:64], 0, None, op0=ALU.is_equal))
                nc.vector.memset(onesb[:, :], 1.0)
                nc.vector.memset(eps_col[:, :], 1e-5)
                nc.vector.memset(neg12_col[:, :], -12.0)
                vst4 = vst.ap().rearrange("p (kc h c) -> p kc h c", kc=NKC, h=NH)
                setup_dve = s_dve.inc(nc.vector.engine_nop())

                s_ld.inc(nc.sync.dma_start(wg_sb[:, :], wg), 16)
                s_pp.inc(nc.gpsimd.dma_start(wgr_sb.ap().rearrange("p (dc c) -> p dc c", dc=NDC),
                                             wgr.rearrange("(dc p) c -> p dc c", p=128)), 16)
                ewait(nc.scalar, s_ld, s_ld.n)
                a = s_act.inc(nc.scalar.activation(sq[:, :], wg_sb[:, :], AF.Square))
                ewait(nc.vector, s_act, a)
                v1 = s_dve.inc(nc.vector.reduce_sum(nrm[:, :], sq[:, :], axis=AX.X))
                ewait(nc.scalar, s_dve, v1)
                a = s_act.inc(nc.scalar.activation(nrm2[:, :], nrm[:, :], AF.Sqrt))
                ewait(nc.vector, s_act, a)
                v2 = s_dve.inc(nc.vector.tensor_scalar(nrm2[:, :], nrm2[:, :], 1e-4, None, op0=ALU.max))
                ewait(nc.vector, s_dve, v2)
                v3 = s_dve.inc(nc.vector.reciprocal(rn[:, :], nrm2[:, :]))
                ewait(nc.vector, s_dve, v3)
                v4 = s_dve.inc(nc.vector.tensor_scalar(wgu_bf[:, :], wg_sb[:, :], rn[:, :], None, op0=ALU.mult))
                with nc.psum_tensor("ps_wgu", [16, 8], BF16) as ps_wgu:
                    ewait(nc.tensor, s_dve, v4)
                    ewait(nc.tensor, s_dve, d1)
                    t1 = s_pe.inc(nc.tensor.transpose(ps_wgu[:, :], wgu_bf[:, :], ident[0:8, 0:8]))
                    ewait(nc.scalar, s_pe, t1)
                    wgu_ready = s_act.inc(nc.scalar.copy(wguT[:, :], ps_wgu[:, :]))

            pp_ready = {}
            def run_prepass():
                for widx, wsrc in enumerate((wqt, wkt, wvt, wot)):
                    src = wsrc.rearrange("(dc p) o -> p dc o", p=128)
                    dst = wab[widx].rearrange("p (dc o) -> p dc o", dc=NDC)
                    s_ppwa.inc(nc.gpsimd.dma_start(dst, src), 16)
                pp_ready['wa'] = s_ppwa.n
                for e_ in range(E):
                    for g in range(NG):
                        src = w1[e_][:, g * 512:(g + 1) * 512].rearrange("(dc p) h -> p dc h", p=128)
                        dst = w1b[e_, g].rearrange("p (dc h) -> p dc h", dc=NDC)
                        s_ppe[e_].inc(nc.gpsimd.dma_start(dst, src), 16)
                    for j in range(NG):
                        src = w2[e_][j * 512:(j + 1) * 512, :].rearrange("(hc p) d -> p hc d", p=128)
                        dst = w2b[e_, 4 * j:4 * j + 4].rearrange("hc p d -> p hc d")
                        s_ppe[e_].inc(nc.gpsimd.dma_start(dst, src), 16)
                    pp_ready[('e', e_)] = s_ppe[e_].n
            run_prepass()

            w_done = {}
            def wait_w(eng, idx):
                ewait(eng, s_w[idx % 2], w_done[idx])
            w_last_use = {}
            def load_w(idx, widx):
                slot = idx % 2
                if w_last_use.get(slot):
                    ewait(nc.sync, s_pe, w_last_use[slot])
                ewait(nc.sync, s_ppwa, pp_ready['wa'])
                w_done[idx] = s_w[slot].inc(nc.sync.dma_start(wsb[slot][:, :], wab[widx]), 16)

            x_done = {}
            x_free = [0] * 4
            def load_x(seq, tt, slot):
                row0 = seq * SL + tt * 128
                if x_free[slot]:
                    ewait(nc.sync, s_dve, x_free[slot])
                x_done[(seq, tt)] = (slot, s_x[slot].inc(nc.sync.dma_start(x_tile[slot], x_in[row0:row0 + 128, :]), 16))

            def ln_tile(slot, src_sb, dst_bf):
                nc.vector.bn_stats(st6[slot][:, 0:6], src_sb[:, 0:512])
                b1_ = s_dve.inc(nc.vector.bn_stats(st6[slot][:, 6:12], src_sb[:, 512:1024]))
                ewait(nc.vector, s_dve, b1_)
                b2_ = s_dve.inc(nc.vector.bn_aggr(mv[slot][:, :], st6[slot][:, :]))
                ewait(nc.scalar, s_dve, b2_)
                s_act.inc(nc.scalar.activation(stdv[slot][:, :], mv[slot][:, 1:2], AF.Sqrt, bias=eps_col[:, :]))
                a2 = s_act.inc(nc.scalar.mul(negm[slot][:, :], mv[slot][:, 0:1], -1.0))
                ewait(nc.vector, s_act, a2)
                r1 = s_dve.inc(nc.vector.reciprocal(rstd[slot][:, :], stdv[slot][:, :]))
                ewait(nc.vector, s_dve, r1)
                r2 = s_dve.inc(nc.vector.tensor_scalar(
                    dst_bf, src_sb, negm[slot][:, :], rstd[slot][:, :],
                    op0=ALU.add, op1=ALU.mult))
                return r2

            xln_rel = [0] * 4
            x2_rel = [0] * 4
            x2_reader = [0] * 4
            tr_free = [0] * 4
            tr_read = [0] * 4

            for seq in range(NSEQ):
                for i_ in range(4):
                    if tr_free[i_]:
                        ewait(nc.scalar, s_xl2[i_], tr_free[i_])
                    if x2_rel[i_]:
                        ewait(nc.scalar, s_x2w[i_], x2_rel[i_])
                load_w(3 * seq + 0, 0)
                load_w(3 * seq + 1, 1)
                with nc.psum_tensor(f"ps_tr0_{seq}", [128, 1024], BF16) as tr0, \
                     nc.psum_tensor(f"ps_tr1_{seq}", [128, 1024], BF16) as tr1, \
                     nc.psum_tensor(f"ps_qk0_{seq}", [128, 512], F32) as qk0, \
                     nc.psum_tensor(f"ps_qk1_{seq}", [128, 512], F32) as qk1, \
                     nc.psum_tensor(f"ps_v0_{seq}", [128, 1024], F32) as pv0, \
                     nc.psum_tensor(f"ps_v1_{seq}", [128, 1024], F32) as pv1:
                    ps_tr = [tr0, tr1]
                    ps_qk = [qk0, qk1]
                    ps_v = [pv0, pv1]
                    for i in range(3):
                        load_x(seq, i, i % 4)
                    r2s = {}

                    def emit_ln(tt):
                        slot = tt % 4
                        xs, xv = x_done[(seq, tt)]
                        ewait(nc.vector, s_x[xs], xv)
                        if xln_rel[slot]:
                            ewait(nc.vector, s_pe, xln_rel[slot])
                        r2s[tt] = ln_tile(slot, x_tile[slot], xln_t[slot])
                        x_free[slot] = r2s[tt]
                        if tt + 3 < NTT:
                            load_x(seq, tt + 3, (tt + 3) % 4)

                    def emit_tr(tt):
                        slot = tt % 4
                        ewait(nc.tensor, s_dve, r2s[tt])
                        if seq == 0 and tt == 0:
                            ewait(nc.tensor, s_dve, setup_dve)
                        pb = ps_tr[tt % 2]
                        acquire_banks(pb)
                        for i in range(8):
                            mm = nc.tensor.transpose(pb[:, i * 128:(i + 1) * 128], xln_t[slot][:, i * 128:(i + 1) * 128], ident[:, :])
                        xln_rel[slot] = s_pe.inc(mm)
                        ewait(nc.scalar, s_pe, xln_rel[slot])
                        dst = xlnT.ap().rearrange("p (dc t) -> p dc t", dc=NDC)[:, :, tt * 128:(tt + 1) * 128]
                        src = pb.ap().rearrange("p (dc t) -> p dc t", dc=NDC)
                        ev = s_act.inc(nc.scalar.copy(dst, src))
                        release_banks(pb, s_act, ev)

                    n = 0
                    for tt in range(4):
                        emit_ln(tt)
                    for tb in range(4):
                        for tt in range(4 * tb, 4 * tb + 4):
                            emit_tr(tt)
                        if tb + 1 < 4:
                            for tt in range(4 * (tb + 1), 4 * (tb + 1) + 4):
                                emit_ln(tt)
                        col_ready = s_act.n
                        for wi, dstT in ((0, qT), (1, kT)):
                            widx = 3 * seq + wi
                            wslot = widx % 2
                            wait_w(nc.tensor, widx)
                            ewait(nc.tensor, s_act, col_ready)
                            for oc in range(8):
                                pb = ps_qk[n % 2]
                                n += 1
                                acquire_banks(pb)
                                for dc in range(NDC):
                                    mm = nc.tensor.matmul(
                                        pb[:, :],
                                        wsb[wslot][:, dc * 1024 + oc * 128: dc * 1024 + oc * 128 + 128],
                                        xlnT[:, dc * SL + tb * 512: dc * SL + tb * 512 + 512],
                                        start=(dc == 0), stop=(dc == NDC - 1))
                                pe_c = s_pe.inc(mm)
                                if oc == 7 and tb == 3:
                                    w_last_use[wslot] = pe_c
                                    if wi == 0:
                                        load_w(3 * seq + 2, 2)
                                ewait(nc.scalar, s_pe, pe_c)
                                ev = s_act.inc(nc.scalar.copy(dstT[:, oc * SL + tb * 512: oc * SL + tb * 512 + 512], pb[:, :]))
                                release_banks(pb, s_act, ev)
                    ewait(nc.vector, s_pe, xln_rel[2])
                    ewait(nc.vector, s_pe, xln_rel[3])
                    ones_fix = s_dve.inc(nc.vector.memset(vst4[:, :, :, 64:65], 1.0))
                    widx = 3 * seq + 2
                    wslot = widx % 2
                    wait_w(nc.tensor, widx)
                    for tt in range(NTT):
                        pb = ps_v[tt % 2]
                        acquire_banks(pb)
                        for dc in range(NDC):
                            for dh in range(2):
                                mm = nc.tensor.matmul(
                                    pb[:, dh * 512:(dh + 1) * 512],
                                    xlnT[:, dc * SL + tt * 128: dc * SL + tt * 128 + 128],
                                    wsb[wslot][:, dc * 1024 + dh * 512: dc * 1024 + dh * 512 + 512],
                                    start=(dc == 0), stop=(dc == NDC - 1))
                        pe_c = s_pe.inc(mm)
                        if tt == NTT - 1:
                            w_last_use[wslot] = pe_c
                        ewait(nc.scalar, s_pe, pe_c)
                        dst = vst4[:, tt, :, 0:64]
                        src = pb.ap().rearrange("p (h c) -> p h c", h=NH)
                        ev = s_act.inc(nc.scalar.copy(dst, src))
                        release_banks(pb, s_act, ev)

                load_w(6 + seq, 3)
                with nc.psum_tensor(f"ps_sc_{seq}", [128, 2048], F32) as ps_sc, \
                     nc.psum_tensor(f"ps_cx0_{seq}", [128, 260], F32) as pcx0, \
                     nc.psum_tensor(f"ps_cx1_{seq}", [128, 260], F32) as pcx1, \
                     nc.psum_tensor(f"ps_ctr_{seq}", [128, 256], BF16) as ptr:
                    ps_cx = [pcx0, pcx1]
                    pT = [xlnT[:, i * 1024:(i + 1) * 1024] for i in range(3)]
                    nrmc = [xlnT[:, 3072 + k * 128: 3072 + (k + 1) * 128] for k in range(4)]
                    rcol = [xlnT[:, 3584 + 2 * k: 3584 + 2 * k + 2].bitcast(F32) for k in range(8)]
                    QBS = 256
                    NQB2 = SL // QBS
                    blocks = [(hp, qb) for hp in range(NH // 2) for qb in range(NQB2)]
                    strips = []
                    for bi, (hp, qb) in enumerate(blocks):
                        for kc in range(2 * qb + 2):
                            strips.append((bi, hp, qb, kc))
                    NSTR = len(strips)
                    assert NSTR % 2 == 0
                    exp_done = {}
                    msk_done = {}
                    pT_rel = [0, 0, 0]
                    nslot_ctr = [0]
                    nrm_rel = [0] * 4

                    def issue_sT2(qi):
                        i0_, i1_ = 2 * qi, 2 * qi + 1
                        slot = qi % 3
                        sc0 = (qi % 2) * 1024
                        acquire_bank_cols(ps_sc, sc0, sc0 + 1024, 4)
                        for (i_, half) in ((i0_, 0), (i0_, 1), (i1_, 0), (i1_, 1)):
                            bi, hp, qb, kc = strips[i_]
                            oc = hp
                            q0 = oc * SL + qb * QBS
                            k0 = oc * SL + kc * 128
                            hb = half * 64
                            dst = ps_sc[:, sc0 + half * 512 + (i_ % 2) * 256: sc0 + half * 512 + (i_ % 2) * 256 + 256]
                            mm = nc.tensor.matmul(
                                dst,
                                kT[hb:hb + 64, k0:k0 + 128], qT[hb:hb + 64, q0:q0 + QBS],
                                start=True, stop=True)
                        done = s_pe.inc(mm)
                        ewait(nc.scalar, s_pe, done)
                        if pT_rel[slot]:
                            ewait(nc.scalar, s_pe, pT_rel[slot])
                        a = s_act.inc(nc.scalar.activation(pT[slot], ps_sc[:, sc0:sc0 + 1024], AF.Exp, bias=neg12_col[:, :], scale=0.125))
                        release_bank_cols(ps_sc, sc0, sc0 + 1024, 4, s_act, a)
                        exp_done[qi] = a
                        for i_ in (i0_, i1_):
                            bi, hp, qb, kc = strips[i_]
                            if kc >= 2 * qb:
                                o = qb * QBS - kc * 128 + 384
                                c0 = (i_ % 2) * 256
                                ewait(nc.vector, s_act, a)
                                nc.vector.tensor_tensor(
                                    pT[slot][:, c0:c0 + 256], pT[slot][:, c0:c0 + 256],
                                    bmask[:, o:o + 256], op=ALU.mult)
                                msk_done[i_] = s_dve.inc(nc.vector.tensor_tensor(
                                    pT[slot][:, 512 + c0:512 + c0 + 256], pT[slot][:, 512 + c0:512 + c0 + 256],
                                    bmask[:, o:o + 256], op=ALU.mult))

                    def issue_ctx(i):
                        bi, hp, qb, kc = strips[i]
                        slot = (i // 2) % 3
                        c0 = (i % 2) * 256
                        pc = ps_cx[bi % 2]
                        nkc = 2 * qb + 2
                        if kc == 0:
                            acquire_banks(pc)
                        if i in msk_done:
                            ewait(nc.tensor, s_dve, msk_done[i])
                        else:
                            ewait(nc.tensor, s_act, exp_done[i // 2])
                        v0 = kc * NH * 65
                        for half in (0, 1):
                            for hh in (0, 1):
                                mm = nc.tensor.matmul(
                                    pc[:, half * 130 + hh * 65: half * 130 + hh * 65 + 65],
                                    pT[slot][:, hh * 512 + c0 + half * 128: hh * 512 + c0 + half * 128 + 128],
                                    vst[:, v0 + (2 * hp + hh) * 65: v0 + (2 * hp + hh) * 65 + 65],
                                    start=(kc == 0 and half == 0 and hh == 0),
                                    stop=(kc == nkc - 1 and half == 1 and hh == 1))
                        r = s_pe.inc(mm)
                        if i % 2 == 1:
                            pT_rel[slot] = r
                        return r

                    def tail_dve(bi, last_pe):
                        pc = ps_cx[bi % 2]
                        ewait(nc.vector, s_pe, last_pe)
                        ks = []
                        nv = 0
                        for half in (0, 1):
                            k = nslot_ctr[0]
                            nslot_ctr[0] = (k + 1) % 4
                            ks.append(k)
                            if nrm_rel[k]:
                                ewait(nc.vector, s_pe, nrm_rel[k])
                            r0, r1 = rcol[2 * k], rcol[2 * k + 1]
                            nc.vector.reciprocal(r0, pc[:, half * 130 + 64: half * 130 + 65])
                            nc.vector.reciprocal(r1, pc[:, half * 130 + 129: half * 130 + 130])
                            nc.vector.tensor_scalar(
                                nrmc[k][:, 0:64], pc[:, half * 130: half * 130 + 64], r0, None, op0=ALU.mult)
                            nv = s_dve.inc(nc.vector.tensor_scalar(
                                nrmc[k][:, 64:128], pc[:, half * 130 + 65: half * 130 + 129], r1, None, op0=ALU.mult))
                        release_banks(pc, s_dve, nv)
                        return (bi, nv, ks)

                    def tail_pe(bi, nv, ks):
                        hp, qb = blocks[bi]
                        q0 = hp * SL + qb * QBS
                        acquire_banks(ptr)
                        ewait(nc.tensor, s_dve, nv)
                        for half in (0, 1):
                            mm = nc.tensor.transpose(ptr[:, half * 128:(half + 1) * 128], nrmc[ks[half]][:, :], ident[:, :])
                        tr = s_pe.inc(mm)
                        nrm_rel[ks[0]] = tr
                        nrm_rel[ks[1]] = tr
                        ewait(nc.scalar, s_pe, tr)
                        cc = s_act.inc(nc.scalar.copy(qT[:, q0:q0 + QBS], ptr[:, :]))
                        release_banks(ptr, s_act, cc)
                        return cc

                    NQUAD = NSTR // 2
                    ewait(nc.tensor, s_dve, ones_fix)
                    issue_sT2(0)
                    pend = []
                    ctxT_ready = 0
                    for qi in range(NQUAD + 1):
                        if qi + 1 < NQUAD:
                            issue_sT2(qi + 1)
                        if qi < 1:
                            continue
                        qc = qi - 1
                        for i in (2 * qc, 2 * qc + 1):
                            bi, hp, qb, kc = strips[i]
                            last_pe = issue_ctx(i)
                            if kc == 2 * qb + 1:
                                pend.append(tail_dve(bi, last_pe))
                                if len(pend) > 1:
                                    ctxT_ready = tail_pe(*pend.pop(0))
                    while pend:
                        ctxT_ready = tail_pe(*pend.pop(0))
                ctxT = qT

                widx = 6 + seq
                wslot = widx % 2
                with nc.psum_tensor(f"ps_o0_{seq}", [128, 1024], F32) as o0, \
                     nc.psum_tensor(f"ps_o1_{seq}", [128, 1024], F32) as o1, \
                     nc.psum_tensor(f"ps_tr2_{seq}", [128, 1024], BF16) as tr2, \
                     nc.psum_tensor(f"ps_tr3_{seq}", [128, 1024], BF16) as tr3, \
                     nc.psum_tensor(f"ps_xr0_{seq}", [16, 128], F32) as xr0, \
                     nc.psum_tensor(f"ps_xr1_{seq}", [16, 128], F32) as xr1:
                    ps_o = [o0, o1]
                    ps_tr = [tr2, tr3]
                    ps_xr = [xr0, xr1]
                    wait_w(nc.tensor, widx)
                    ewait(nc.tensor, s_act, ctxT_ready)
                    ewait(nc.sync, s_act, ctxT_ready)
                    LAGC = 3
                    r2_of = {}
                    for step in range(NTT + LAGC):
                        if step < NTT:
                            tt = step
                            slot = tt % 4
                            if tt == 0:
                                for i in range(3):
                                    load_x(seq, i, i)
                            if tt + 3 < NTT:
                                load_x(seq, tt + 3, (tt + 3) % 4)
                            pb = ps_o[tt % 2]
                            acquire_banks(pb)
                            for dc in range(NDC):
                                for dh in range(2):
                                    mm = nc.tensor.matmul(
                                        pb[:, dh * 512:(dh + 1) * 512],
                                        ctxT[:, dc * SL + tt * 128: dc * SL + tt * 128 + 128],
                                        wsb[wslot][:, dc * 1024 + dh * 512: dc * 1024 + dh * 512 + 512],
                                        start=(dc == 0), stop=(dc == NDC - 1))
                            pe_c = s_pe.inc(mm)
                            if tt == NTT - 1:
                                w_last_use[wslot] = pe_c
                            ewait(nc.vector, s_pe, pe_c)
                            xs, xv = x_done[(seq, tt)]
                            ewait(nc.vector, s_x[xs], xv)
                            if x2_rel[slot]:
                                ewait(nc.vector, s_x2w[slot], x2_rel[slot])
                            if x2_reader[slot]:
                                ewait(nc.vector, s_dve, x2_reader[slot])
                            rs = s_dve.inc(nc.vector.tensor_tensor(x2_sb[slot], pb[:, :], x_tile[slot], op=ALU.add))
                            x_free[slot] = rs
                            release_banks(pb, s_dve, rs)
                            row0 = seq * SL + tt * 128
                            ewait(nc.sync, s_dve, rs)
                            x2_rel[slot] = s_x2w[slot].inc(nc.sync.dma_start(x2s[row0:row0 + 128, :], x2_sb[slot]), 16)
                            if xln_rel[slot]:
                                ewait(nc.vector, s_pe, xln_rel[slot])
                            r2_of[tt] = ln_tile(slot, x2_sb[slot], xln_t[slot])
                            x2_reader[slot] = r2_of[tt]
                        if step >= LAGC:
                            tt = step - LAGC
                            slot = tt % 4
                            pt = ps_tr[tt % 2]
                            acquire_banks(pt)
                            ewait(nc.tensor, s_dve, r2_of[tt])
                            for i in range(8):
                                mm = nc.tensor.transpose(pt[:, i * 128:(i + 1) * 128], xln_t[slot][:, i * 128:(i + 1) * 128], ident[:, :])
                            xln_rel[slot] = s_pe.inc(mm)
                            ewait(nc.scalar, s_pe, xln_rel[slot])
                            if tr_free[slot]:
                                ewait(nc.scalar, s_xl2[slot], tr_free[slot])
                            if tr_read[slot]:
                                ewait(nc.scalar, s_pe, tr_read[slot])
                            ev = s_act.inc(nc.scalar.copy(trbuf[slot].rearrange("p (dc t) -> p dc t", dc=NDC),
                                                          pt.ap().rearrange("p (dc t) -> p dc t", dc=NDC)))
                            release_banks(pt, s_act, ev)
                            ewait(nc.sync, s_act, ev)
                            t0c = seq * SL + tt * 128
                            tr_free[slot] = s_xl2[slot].inc(nc.sync.dma_start(
                                xl2s[:, :, t0c:t0c + 128].rearrange("dc p t -> p dc t"),
                                trbuf[slot].rearrange("p (dc t) -> p dc t", dc=NDC)), 16)
                            pxr = ps_xr[tt % 2]
                            acquire_banks(pxr)
                            ewait(nc.tensor, s_act, ev)
                            ewait(nc.tensor, s_pp, 16)
                            for dc in range(NDC):
                                mm = nc.tensor.matmul(
                                    pxr[:, :],
                                    wgr_sb[:, dc * 16:(dc + 1) * 16],
                                    trbuf[slot][:, dc * 128:(dc + 1) * 128],
                                    start=(dc == 0), stop=(dc == NDC - 1))
                            xr_pe = s_pe.inc(mm)
                            tr_read[slot] = xr_pe
                            ewait(nc.scalar, s_pe, xr_pe)
                            ev2 = s_act.inc(nc.scalar.copy(xrT[:, t0c:t0c + 128], pxr[:, :]))
                            release_banks(pxr, s_act, ev2)

            NGS = 4
            gmax = [A(nc.sbuf_tensor(f"gmax{i}", [128, 1], F32)) for i in range(NGS)]
            gnegm = [A(nc.sbuf_tensor(f"gnegm{i}", [128, 1], F32)) for i in range(NGS)]
            gexp = [A(nc.sbuf_tensor(f"gexp{i}", [128, 8], F32)) for i in range(NGS)]
            gsum = [A(nc.sbuf_tensor(f"gsum{i}", [128, 1], F32)) for i in range(NGS)]
            xr_ready = s_act.n
            with nc.psum_tensor("ps_lg0", [128, 8], F32) as lg0, \
                 nc.psum_tensor("ps_lg1", [128, 8], F32) as lg1, \
                 nc.psum_tensor("ps_lg2", [128, 8], F32) as lg2, \
                 nc.psum_tensor("ps_lg3", [128, 8], F32) as lg3, \
                 nc.psum_tensor("ps_oh0", [8, 128], F32) as poh0, \
                 nc.psum_tensor("ps_oh1", [8, 128], F32) as poh1:
                ps_lg = [lg0, lg1, lg2, lg3]
                ps_oh = [poh0, poh1]
                ewait(nc.tensor, s_act, xr_ready)
                ewait(nc.tensor, s_act, wgu_ready)
                a1_slot = [0] * NGS
                sm_slot = [0] * NGS
                for tt in range(32):
                    slot = tt % NGS
                    pb = ps_lg[slot]
                    acquire_banks(pb)
                    mm = nc.tensor.matmul(pb[:, :], xrT[:, tt * 128:(tt + 1) * 128], wguT[:, :], start=True, stop=True)
                    pe_c = s_pe.inc(mm)
                    ewait(nc.vector, s_pe, pe_c)
                    if a1_slot[slot]:
                        ewait(nc.vector, s_act, a1_slot[slot])
                    m1 = s_dve.inc(nc.vector.reduce_max(gmax[slot][:, :], pb[:, :], axis=AX.X))
                    ewait(nc.scalar, s_dve, m1)
                    a1 = s_act.inc(nc.scalar.mul(gnegm[slot][:, :], gmax[slot][:, :], -1.0))
                    a1_slot[slot] = a1
                    ewait(nc.scalar, s_act, a1)
                    if sm_slot[slot]:
                        ewait(nc.scalar, s_dve, sm_slot[slot])
                    a2 = s_act.inc(nc.scalar.activation(gexp[slot][:, :], pb[:, :], AF.Exp, bias=gnegm[slot][:, :]))
                    ewait(nc.vector, s_act, a2)
                    oh1 = s_dve.inc(nc.vector.tensor_scalar(oh_sb[:, tt * 8:(tt + 1) * 8], pb[:, :], gmax[slot][:, :], None, op0=ALU.is_equal))
                    release_banks(pb, s_dve, oh1)
                    po = ps_oh[tt % 2]
                    acquire_banks(po)
                    ewait(nc.tensor, s_dve, oh1)
                    trh = s_pe.inc(nc.tensor.transpose(po[:, :], oh_sb[:, tt * 8:(tt + 1) * 8], ident32[:, :]))
                    ewait(nc.scalar, s_pe, trh)
                    ohc = s_act.inc(nc.scalar.copy(ohT_sb[0:8, tt * 128:(tt + 1) * 128], po[:, :]))
                    release_banks(po, s_act, ohc)
                    sm = s_dve.inc(nc.vector.reduce_sum(gsum[slot][:, :], gexp[slot][:, :], axis=AX.X))
                    sm_slot[slot] = sm
                    ewait(nc.vector, s_dve, sm)
                    s_dve.inc(nc.vector.reciprocal(sc_sb[:, tt:tt + 1], gsum[slot][:, :]))
            ohT_ready = s_act.n
            gating_done = s_dve.n
            act_fin = s_act.n
            ewait(nc.tensor, s_dve, gating_done)
            ewait(nc.scalar, s_dve, gating_done)
            ewait(nc.sync, s_st, s_st.n)
            for i_ in range(4):
                ewait(nc.sync, s_xl2[i_], s_xl2[i_].n)
                ewait(nc.sync, s_x2w[i_], s_x2w[i_].n)
            ewait(nc.sync, s_dve, gating_done)
            ewait(nc.sync, s_act, act_fin)
            ewait(nc.gpsimd, s_dve, gating_done)
            ewait(nc.gpsimd, s_act, act_fin)

        with ExitStack() as fst:
            F = fst.enter_context
            xbuf = [F(nc.sbuf_tensor(f"xbuf{i}", [128, NDC * TCH], BF16)) for i in range(2)]
            hT = F(nc.sbuf_tensor("hT", [128, NHC * TCH], BF16))
            w1s = [F(nc.sbuf_tensor(f"w1s{i}", [128, 4096], BF16)) for i in range(3)]
            w2s = [F(nc.sbuf_tensor(f"w2s{i}", [128, 4096], BF16)) for i in range(3)]
            acc = F(nc.sbuf_tensor("acc", [128, NSUB * 1024], F32))
            b1sb = F(nc.sbuf_tensor("b1sb", [128, E * 32], F32))

            x2b = [F(nc.sbuf_tensor(f"x2b{i}", [128, 1024], F32)) for i in range(NSUB)]
            osb = [F(nc.sbuf_tensor(f"osb{i}", [128, 1024], F32)) for i in range(NSUB)]

            s_ld.inc(nc.sync.dma_start(b1sb.ap().rearrange("p (e c) -> p e c", e=E), b1r.rearrange("e p c -> p e c")), 16)
            s_pp.inc(nc.gpsimd.dma_start(b2a_sb[:, :], b2f.rearrange("o (e d) -> (o e) d", e=E)), 16)
            const_ld = s_ld.n
            const_pp = s_pp.n

            xln_done = [0] * NTC
            def load_xln(tc):
                src = xl2s[:, :, tc * TCH:(tc + 1) * TCH].rearrange("dc p t -> p dc t")
                dst = xbuf[tc % 2].ap().rearrange("p (dc t) -> p dc t", dc=NDC)
                xln_done[tc] = s_xb[tc % 2].inc(nc.sync.dma_start(dst, src), 16)
            load_xln(0)

            xln_last_use = [0] * NTC
            w1_done = {}
            w1_last_use = {}
            w2_done = {}
            w2_last_use = {}
            gelu_done = [0] * NHC
            last_stt = 0
            acc_writer = [0] * NSUB
            x2b_free = [0] * NSUB
            osb_store_done = [0] * NSUB
            pending_stores = []
            x2_done = {}
            w1_issued = set()
            w2_issued = set()
            w1_slot_of = {}
            w2_slot_of = {}
            w1_ctr = [0]
            w2_ctr = [0]

            def issue_w1(tc_, e_, g_):
                if (tc_, e_, g_) in w1_issued or tc_ >= NTC:
                    return
                w1_issued.add((tc_, e_, g_))
                slot = w1_ctr[0] % 3
                w1_ctr[0] += 1
                w1_slot_of[(tc_, e_, g_)] = slot
                prev_use = w1_last_use.get(slot, 0)
                if prev_use:
                    ewait(nc.sync, s_pe, prev_use)
                ewait(nc.sync, s_ppe[e_], pp_ready[('e', e_)])
                w1_done[(tc_, e_, g_)] = (slot, s_w1[slot].inc(nc.sync.dma_start(w1s[slot][:, :], w1b[e_, g_]), 16))

            def issue_w2(tc_, e_, j_):
                if (tc_, e_, j_) in w2_issued or tc_ >= NTC:
                    return
                w2_issued.add((tc_, e_, j_))
                slot = w2_ctr[0] % 3
                w2_ctr[0] += 1
                w2_slot_of[(tc_, e_, j_)] = slot
                prev_use = w2_last_use.get(slot, 0)
                if prev_use:
                    ewait(nc.gpsimd, s_pe, prev_use)
                ewait(nc.gpsimd, s_ppe[e_], pp_ready[('e', e_)])
                src = w2b[e_, 4 * j_:4 * j_ + 4].rearrange("h p d -> p h d")
                dst = w2s[slot].ap().rearrange("p (h d) -> p h d", h=4)
                w2_done[(tc_, e_, j_)] = (slot, s_w2[slot].inc(nc.gpsimd.dma_start(dst, src), 16))

            def flush_stores():
                while pending_stores:
                    tc_, tsub_, w = pending_stores.pop(0)
                    row0 = tc_ * TCH + tsub_ * 128
                    ewait(nc.sync, s_dve, w)
                    osb_store_done[tsub_] = s_st.inc(nc.sync.dma_start(out[row0:row0 + 128, :], osb[tsub_][:, :]), 16)

            first_e_mm = True
            ewait(nc.tensor, s_ld, const_ld)
            ewait(nc.tensor, s_pp, const_pp)
            ewait(nc.vector, s_ld, const_ld)
            ewait(nc.scalar, s_ld, const_ld)

            for tc in range(NTC):
                if tc + 1 < NTC:
                    if tc >= 1:
                        ewait(nc.sync, s_pe, xln_last_use[tc - 1])
                    load_xln(tc + 1)
                xb = xbuf[tc % 2]
                acc_init = [0] * NSUB
                with nc.psum_tensor(f"ps_b2a_{tc}", [128, 1024], F32) as pb2a, \
                     nc.psum_tensor(f"ps_b2b_{tc}", [128, 1024], F32) as pb2b:
                    pb2l = [pb2a, pb2b]
                    ewait(nc.tensor, s_act, ohT_ready)
                    ewait(nc.tensor, s_pp, const_pp)
                    for tsub in range(NSUB):
                        pb = pb2l[tsub % 2]
                        acquire_banks(pb)
                        t0g = (tc * NSUB + tsub) * 128
                        for dh in range(2):
                            mm = nc.tensor.matmul(
                                pb[:, dh * 512:(dh + 1) * 512],
                                ohT_sb[0:8, t0g:t0g + 128],
                                b2a_sb[0:8, dh * 512:(dh + 1) * 512],
                                start=True, stop=True)
                        pe_b = s_pe.inc(mm)
                        ewait(nc.scalar, s_pe, pe_b)
                        if acc_writer[tsub]:
                            ewait(nc.scalar, s_dve, acc_writer[tsub])
                        ai = s_act.inc(nc.scalar.copy(acc[:, tsub * 1024:(tsub + 1) * 1024], pb[:, :]))
                        release_banks(pb, s_act, ai)
                        acc_init[tsub] = ai
                for e in range(E):
                    if e == 1:
                        flush_stores()
                    if e == E - 1:
                        for tsub in range(NSUB):
                            row0 = tc * TCH + tsub * 128
                            if x2b_free[tsub]:
                                ewait(nc.sync, s_dve, x2b_free[tsub])
                            x2_done[(tc, tsub)] = s_x2b.inc(nc.sync.dma_start(x2b[tsub][:, :], x2s[row0:row0 + 128, :]), 16)
                    with nc.psum_tensor(f"ps_f1a_{tc}_{e}", [128, 512], F32) as f1a, \
                         nc.psum_tensor(f"ps_f1b_{tc}_{e}", [128, 512], F32) as f1b:
                        ps_f1 = [f1a, f1b]
                        issue_w1(tc, e, 0)
                        issue_w1(tc, e, 1)
                        for g in range(NG):
                            slot = w1_slot_of[(tc, e, g)]
                            if g + 2 < NG:
                                issue_w1(tc, e, g + 2)
                            else:
                                ntc_ = tc + 1 if e == E - 1 else tc
                                issue_w1(ntc_, (e + 1) % E, g + 2 - NG)
                            w1s_, w1v_ = w1_done[(tc, e, g)]
                            ewait(nc.tensor, s_w1[w1s_], w1v_)
                            if g == 0:
                                ewait(nc.tensor, s_xb[tc % 2], xln_done[tc])
                            for hc in range(4 * g, 4 * g + 4):
                                pb = ps_f1[hc % 2]
                                acquire_banks(pb)
                                for dc in range(NDC):
                                    mm = nc.tensor.matmul(
                                        pb[:, :],
                                        w1s[slot][:, dc * 512 + (hc % 4) * 128: dc * 512 + (hc % 4) * 128 + 128],
                                        xb[:, dc * 512:(dc + 1) * 512],
                                        start=(dc == 0), stop=(dc == NDC - 1))
                                w1_last_use[slot] = s_pe.inc(mm)
                                if hc == NHC - 1:
                                    xln_last_use[tc] = w1_last_use[slot]
                                ewait(nc.scalar, s_pe, w1_last_use[slot])
                                g_i = nc.scalar.activation(
                                    hT[:, hc * TCH:(hc + 1) * TCH], pb[:, :], AF.Gelu,
                                    bias=b1sb[:, e * 32 + hc: e * 32 + hc + 1])
                                gelu_done[hc] = s_act.inc(g_i)
                                release_banks(pb, s_act, gelu_done[hc])
                        first_e_mm = False
                    with nc.psum_tensor(f"ps_f2a_{tc}_{e}", [128, 1024], F32) as f2a, \
                         nc.psum_tensor(f"ps_f2b_{tc}_{e}", [128, 1024], F32) as f2b, \
                         nc.psum_tensor(f"ps_f2c_{tc}_{e}", [128, 1024], F32) as f2c, \
                         nc.psum_tensor(f"ps_f2d_{tc}_{e}", [128, 1024], F32) as f2d:
                        ps_f2 = [f2a, f2b, f2c, f2d]
                        f2_stop = [0] * NSUB
                        issue_w2(tc, e, 0)
                        issue_w2(tc, e, 1)
                        for j in range(NHC // 4):
                            slot = w2_slot_of[(tc, e, j)]
                            if j + 2 < NHC // 4:
                                issue_w2(tc, e, j + 2)
                            else:
                                ntc_ = tc + 1 if e == E - 1 else tc
                                issue_w2(ntc_, (e + 1) % E, j + 2 - NHC // 4)
                            w2s_, w2v_ = w2_done[(tc, e, j)]
                            ewait(nc.tensor, s_w2[w2s_], w2v_)
                            for hc in range(4 * j, 4 * j + 4):
                                for tsub in range(NSUB):
                                    if hc == 0:
                                        acquire_banks(ps_f2[tsub])
                                    for dh in range(2):
                                        mm = nc.tensor.matmul(
                                            ps_f2[tsub][:, dh * 512:(dh + 1) * 512],
                                            hT[:, hc * TCH + tsub * 128: hc * TCH + tsub * 128 + 128],
                                            w2s[slot][:, (hc % 4) * 1024 + dh * 512: (hc % 4) * 1024 + (dh + 1) * 512],
                                            start=(hc == 0), stop=(hc == NHC - 1 and dh == 1))
                                    if hc == NHC - 1:
                                        f2_stop[tsub] = s_pe.inc(mm)
                            if f2_stop[NSUB - 1]:
                                w2_last_use[slot] = f2_stop[NSUB - 1] if 4 * j + 3 == NHC - 1 else s_pe.inc(mm)
                            else:
                                w2_last_use[slot] = s_pe.inc(mm)
                        for tsub in range(NSUB):
                            ewait(nc.vector, s_pe, f2_stop[tsub])
                            mcol = oh_sb[:, (tc * NSUB + tsub) * E + e: (tc * NSUB + tsub) * E + e + 1]
                            if e == 0:
                                ewait(nc.vector, s_act, acc_init[tsub])
                            halves = ((0, 512), (512, 1024))
                            for (c0, c1) in halves:
                                a_sl = acc[:, tsub * 1024 + c0: tsub * 1024 + c1]
                                i = nc.vector.scalar_tensor_tensor(a_sl, ps_f2[tsub][:, c0:c1], mcol, a_sl, op0=ALU.mult, op1=ALU.add)
                                w = s_dve.inc(i)
                                release_bank_cols(ps_f2[tsub], c0, c1, 4, s_dve, w)
                            acc_writer[tsub] = w
                        last_stt = acc_writer[NSUB - 1]
                for tsub in range(NSUB):
                    ewait(nc.vector, s_x2b, x2_done[(tc, NSUB - 1)])
                    ewait(nc.vector, s_dve, acc_writer[tsub])
                    if osb_store_done[tsub]:
                        ewait(nc.vector, s_st, osb_store_done[tsub])
                    sc = sc_sb[:, tc * NSUB + tsub: tc * NSUB + tsub + 1]
                    i = nc.vector.scalar_tensor_tensor(osb[tsub][:, :], acc[:, tsub * 1024:(tsub + 1) * 1024], sc, x2b[tsub][:, :], op0=ALU.mult, op1=ALU.add)
                    w = s_dve.inc(i)
                    pending_stores.append((tc, tsub, w))
                    acc_writer[tsub] = w
                    x2b_free[tsub] = w
            flush_stores()
            nc.sync.wait_ge(s_st.h, s_st.n)
    return nc


_NC_CACHE = {}

def _get_nc():
    if "nc" not in _NC_CACHE:
        _NC_CACHE["nc"] = build()
    return _NC_CACHE["nc"]


def kernel(L0_attn_x, L0_attn_W_Q, L0_attn_W_K, L0_attn_W_V, L0_attn_W_O,
           L0_ffn_wg_red, L0_ffn_wg, L0_ffn_w1, L0_ffn_b1, L0_ffn_w2, L0_ffn_b2):
    x = np.asarray(L0_attn_x, dtype=np.float32)
    WQ = np.asarray(L0_attn_W_Q, dtype=np.float32)
    WK = np.asarray(L0_attn_W_K, dtype=np.float32)
    WV = np.asarray(L0_attn_W_V, dtype=np.float32)
    WO = np.asarray(L0_attn_W_O, dtype=np.float32)
    wg_red = np.asarray(L0_ffn_wg_red, dtype=np.float32)
    wg = np.asarray(L0_ffn_wg, dtype=np.float32)
    w1 = np.ascontiguousarray(np.asarray(L0_ffn_w1, dtype=np.float32))
    b1 = np.asarray(L0_ffn_b1, dtype=np.float32)
    w2 = np.ascontiguousarray(np.asarray(L0_ffn_w2, dtype=np.float32))
    b2 = np.asarray(L0_ffn_b2, dtype=np.float32)

    wqt = np.ascontiguousarray(WQ.T)
    wkt = np.ascontiguousarray(WK.T)
    wvt = np.ascontiguousarray(WV.T)
    wot = np.ascontiguousarray(WO.T)
    wgr = np.ascontiguousarray(wg_red.T)
    b1r = np.ascontiguousarray(b1.reshape(E, 32, 128).transpose(0, 2, 1))
    b2f = np.ascontiguousarray(b2.reshape(1, E * D))

    nc = _get_nc()
    shared = dict(wqt=wqt, wkt=wkt, wvt=wvt, wot=wot, wgr=wgr, wg=wg,
                  w1=w1, b1r=b1r, w2=w2, b2f=b2f)
    in_maps = []
    for c in range(8):
        shard = np.ascontiguousarray(x[2 * c:2 * c + 2].reshape(T, D))
        m = dict(shared)
        m["x_in"] = shard
        in_maps.append(m)
    res = run_bass_kernel_spmd(nc, in_maps, core_ids=list(range(8)))
    outs = [np.asarray(res.results[c]["out"], dtype=np.float32).reshape(2, SL, D) for c in range(8)]
    return np.concatenate(outs, axis=0)


if __name__ == "__main__":
    import importlib.util
    spec = importlib.util.spec_from_file_location("_ref", "/root/problem/reference.py")
    ref = importlib.util.module_from_spec(spec)
    spec.loader.exec_module(ref)
    inputs = {k: np.asarray(v) for k, v in ref.setup_inputs().items()}
    got = kernel(**inputs)
    want = np.asarray(ref.reference(**inputs))
    err = got - want
    ratio = float((err ** 2).mean() / (want ** 2).mean())
    print("residual variance ratio:", ratio, "(bar 2.27e-04)")
    print("PASS" if ratio < 2.27e-4 else "FAIL")
